# version 02
# baseline (speedup 1.0000x reference)
import sys
sys.path.insert(0, "/opt/trn_rl_repo")

import contextlib
import numpy as np
import ml_dtypes

import concourse.bass as bass
import concourse.mybir as mybir
from concourse.bass import IndirectOffsetOnAxis
from concourse.bass_utils import run_bass_kernel_spmd

F32 = mybir.dt.float32
BF16 = mybir.dt.bfloat16
U32 = mybir.dt.uint32
AF = mybir.ActivationFunctionType
ALU = mybir.AluOpType
AX = mybir.AxisListType

N_CORES = 8
D = 2048
NCH = 16
HD = 128
HP = 2
TB = 512
QB = 512
KSUB = 128
KGRP = 2
INV_SQRT_HD = float(np.float32(1.0) / np.sqrt(np.float32(128.0)))
EPS = 1e-5

ENGINES = ("tensor", "scalar", "vector", "gpsimd", "sync")


class Plan:
    def __init__(self):
        self.sched = {e: [] for e in ENGINES}
        self.cnt = {}
        self.wm = {e: {} for e in ENGINES}
        self.dw = {e: set() for e in ("vector", "scalar")}
        self.dr = {e: set() for e in ("vector", "scalar")}

    def _wait(self, eng, ev):
        if ev is None:
            return
        sem, val = ev
        if self.wm[eng].get(sem, 0) >= val:
            return
        self.wm[eng][sem] = val
        self.sched[eng].append(("wait", sem, val))

    def emit(self, eng, fn, reads=(), writes=(), waits=(), inc=None, amt=1):
        for ev in waits:
            self._wait(eng, ev)
        if eng in ("vector", "scalar"):
            rs, ws = set(reads), set(writes)
            if (rs & self.dw[eng]) or (ws & (self.dw[eng] | self.dr[eng])):
                self.sched[eng].append(("drain",))
                self.dw[eng] = set()
                self.dr[eng] = set()
            self.dw[eng] |= ws
            self.dr[eng] |= rs
        ev_out = None
        if inc is not None:
            self.cnt[inc] = self.cnt.get(inc, 0) + amt
            ev_out = (inc, self.cnt[inc])
        self.sched[eng].append(("op", fn, inc, amt))
        return ev_out

    def last(self, sem):
        return (sem, self.cnt.get(sem, 0))

    def replay(self, eng, engobj, sems):
        for item in self.sched[eng]:
            if item[0] == "wait":
                engobj.wait_ge(sems[item[1]], item[2])
            elif item[0] == "drain":
                engobj.drain()
            else:
                _, fn, inc, amt = item
                inst = fn(engobj)
                if inc is not None:
                    inst.then_inc(sems[inc], amt)


def build_program(seq, taps=False, pad_pe=0, twd=512):
    NTB = seq // TB
    NQB = seq // QB
    NSB = seq // KSUB
    TOK = seq // N_CORES
    TWD = twd
    NT_D = (TOK + TWD - 1) // TWD
    NTW = [min(TWD, TOK - TWD * i) for i in range(NT_D)]
    NTSB = TOK // 128
    NG = NSB // KGRP
    MSL = 6 * D // N_CORES

    nc = bass.Bass("TRN2", target_bir_lowering=False)

    xT = nc.dram_tensor("xT", [D, seq], F32, kind="ExternalInput")
    wqkv = nc.dram_tensor("wqkv", [D, 6 * HD], F32, kind="ExternalInput")
    wproj = nc.dram_tensor("wproj", [D, D], F32, kind="ExternalInput")
    wupT = nc.dram_tensor("wupT", [D, D], F32, kind="ExternalInput")
    wdownT = nc.dram_tensor("wdownT", [D, D], F32, kind="ExternalInput")
    wexp = nc.dram_tensor("wexp", [D, D], F32, kind="ExternalInput")
    wgT = nc.dram_tensor("wgT", [D, 8], F32, kind="ExternalInput")
    wmod = nc.dram_tensor("wmod", [D, MSL], F32, kind="ExternalInput")
    c_rd = nc.dram_tensor("c_r", [128, 16], F32, kind="ExternalInput")
    cosT = nc.dram_tensor("cosT", [128, seq], BF16, kind="ExternalInput")
    sinT = nc.dram_tensor("sinT", [128, seq], BF16, kind="ExternalInput")
    pTd = nc.dram_tensor("pT", [128, 128], BF16, kind="ExternalInput")
    cidxd = nc.dram_tensor("cidx_in", [128, NCH], U32, kind="ExternalInput")

    out = nc.dram_tensor("out", [NCH, 128, TOK], F32, kind="ExternalOutput")

    mod_in = nc.dram_tensor("mod_in", [MSL], F32)
    mod_out = nc.dram_tensor("mod_out", [6 * D], F32)
    ao_in = nc.dram_tensor("ao_in", [HP * 128, seq], BF16)
    ao_out = nc.dram_tensor("ao_out", [D, seq], BF16)
    g_bnc = nc.dram_tensor("g_bnc", [TOK], F32)

    if taps:
        htap = nc.dram_tensor("htap", [128, NCH * TB], BF16, kind="ExternalOutput")
        qtap = nc.dram_tensor("qtap", [128, TB], BF16, kind="ExternalOutput")
        ktap = nc.dram_tensor("ktap", [128, TB], BF16, kind="ExternalOutput")
        vtap = nc.dram_tensor("vtap", [128, 256], BF16, kind="ExternalOutput")
        atap = nc.dram_tensor("atap", [128, KGRP * QB], BF16, kind="ExternalOutput")
        otap = nc.dram_tensor("otap", [128, QB], BF16, kind="ExternalOutput")
        x1tap = nc.dram_tensor("x1tap", [128, TOK], F32, kind="ExternalOutput")
        h2tap = nc.dram_tensor("h2tap", [128, TOK], BF16, kind="ExternalOutput")
        gtap = nc.dram_tensor("gtap", [1, TOK], F32, kind="ExternalOutput")
        mtap = nc.dram_tensor("mtap", [128, 16], F32, kind="ExternalOutput")
        utap = nc.dram_tensor("utap", [128, 1], F32, kind="ExternalOutput")

    p = Plan()

    with contextlib.ExitStack() as st:
        def sb(name, shape, dt):
            return st.enter_context(nc.sbuf_tensor(name, shape, dt))

        qk = sb("qk", [128, 4 * seq], BF16)
        x1f = qk.ap().bitcast(F32)
        va = sb("va", [128, 2 * seq], BF16)
        big = sb("big", [128, 20480], BF16)
        wA = sb("wA", [128, 12288], BF16)
        xsl = sb("xsl", [128, 2 * TOK], F32)
        ob = sb("ob", [128, 2 * 1024], F32)

        c_sb = sb("c_sb", [128, 16], F32)
        silu_c = sb("silu_c", [128, 16], F32)
        m_sa = sb("m_sa", [128, 16], F32)
        m_ca1 = sb("m_ca1", [128, 16], F32)
        m_ga = sb("m_ga", [128, 16], F32)
        m_sb = sb("m_sb", [128, 16], F32)
        m_cb1 = sb("m_cb1", [128, 16], F32)
        m_gb = sb("m_gb", [128, 16], F32)
        mscr1 = sb("mscr1", [128, 16], F32)
        mscr2 = sb("mscr2", [128, 16], F32)
        mod_row = sb("mod_row", [1, MSL], F32)

        row_m = sb("row_m", [1, TB], F32)
        row_r = sb("row_r", [1, TB], F32)
        row_mr = sb("row_mr", [1, TB], F32)
        row_s1 = sb("row_s1", [1, TB], F32)
        row_s2 = sb("row_s2", [1, TB], F32)
        rs_row = sb("rs_row", [1, QB], F32)
        rsb = sb("rsb", [128, QB], F32)
        ones_b = sb("ones_b", [128, 1], BF16)
        ones_c32 = sb("ones_c32", [128, 1], F32)
        ones_f = sb("ones_f", [1, 128], F32)
        pT_sb = sb("pT_sb", [128, 128], BF16)
        cidx = sb("cidx", [128, NCH], U32)
        mx1 = sb("mx1", [1, 1], F32)
        mx2 = sb("mx2", [1, 1], F32)
        sc1 = sb("sc1", [1, 1], F32)
        negU = sb("negU", [128, 1], F32)
        g_cols = sb("g_cols", [128, NTSB], F32)
        g_d = sb("g_d", [128, 8], F32)
        g_pen = sb("g_pen", [128, 8], F32)
        g_m1 = sb("g_m1", [128, 1], F32)
        g_m2 = sb("g_m2", [128, 1], F32)
        g_s2 = sb("g_s2", [128, 1], F32)
        g_inv = sb("g_inv", [128, 1], F32)
        g_row = sb("g_row", [1, TOK], F32)

        psA = st.enter_context(nc.psum_tensor("psA", [128, 1024], F32))
        psB = st.enter_context(nc.psum_tensor("psB", [128, 1024], F32))
        psC = st.enter_context(nc.psum_tensor("psC", [128, 512], F32))
        psD = st.enter_context(nc.psum_tensor("psD", [128, 512], F32))
        psE = st.enter_context(nc.psum_tensor("psE", [128, 512], F32))
        psF = st.enter_context(nc.psum_tensor("psF", [128, 512], F32))

        sems = {n: st.enter_context(nc.semaphore(n))
                for n in ("sINIT", "sW0", "sW1", "sW2", "sW3", "sW4", "sWG",
                          "sX0", "sX1", "sB0", "sB1", "sL0", "sL1", "sAO",
                          "sMD", "sGB", "sO0", "sO1", "sTP1", "sTP2",
                          "sCC", "sT", "sS", "sV")}
        block = st.enter_context(nc.Block())

        def mm(ps_ap, lhsT_ap, rhs_ap, start, stop, waits=(), inc=None):
            return p.emit("tensor",
                          lambda t, o=ps_ap, l=lhsT_ap, r=rhs_ap, s1=start, s2=stop:
                          t.matmul(o, l, r, start=s1, stop=s2),
                          waits=waits, inc=inc)

        last_tap = [None]

        def gdma(out_ap, in_ap, sem, waits=(), slow=False):
            waits = list(waits)
            if sem.startswith("sTP"):
                if last_tap[0] is not None:
                    waits.append(last_tap[0])
            ev = p.emit("gpsimd",
                        lambda g, o=out_ap, i=in_ap, sl=slow: g.dma_start(
                            out=o, in_=i, allow_slow_non_contiguous=sl),
                        waits=waits, inc=sem, amt=16)
            if sem.startswith("sTP"):
                last_tap[0] = ev
            return ev

        def ydma(out_ap, in_ap, sem, waits=()):
            return p.emit("sync",
                          lambda s, o=out_ap, i=in_ap: s.dma_start(out=o, in_=i),
                          waits=waits, inc=sem, amt=16)

        def vop(fn, rd=(), wr=(), waits=(), inc="sV"):
            return p.emit("vector", fn, reads=rd, writes=wr, waits=waits, inc=inc)

        def sop(fn, rd=(), wr=(), waits=(), inc="sS"):
            return p.emit("scalar", fn, reads=rd, writes=wr, waits=waits, inc=inc)

        vop(lambda v: v.memset(ones_b[:, :], 1.0), wr=("ones_b",))
        vop(lambda v: v.memset(ones_c32[:, :], 1.0), wr=("ones_c32",))
        vop(lambda v: v.memset(ones_f[:, :], 1.0), wr=("ones_f",))
        vop(lambda v: v.memset(mx1[:, :], 0.0), wr=("mx1",))
        e_const = vop(lambda v: v.memset(mx2[:, :], 0.0), wr=("mx2",))
        gdma(pT_sb[:, :], pTd[:, :], 'sINIT')
        gdma(cidx[:, :], cidxd[:, :], 'sINIT')

        def xt_sl(b, ch, lo=0, w=TB):
            o = b * 8192 + ch * TB + lo
            return big[:, o:o + w]

        def tab_sl(b, is_sin):
            o = 16384 + b * 1024 + (512 if is_sin else 0)
            return big[:, o:o + TB]

        def sq_sl(s):
            return big[:, 18432 + s * 512: 18432 + (s + 1) * 512]

        def scr_sl(s):
            return big[:, 19456 + s * 512: 19456 + (s + 1) * 512]

        e_xt = [None] * (NTB + 2)
        e_tab = [None] * (NTB + 2)
        def r3(ap, cdim):
            return ap.rearrange("p (c t) -> p c t", c=cdim)

        e_xt[0] = gdma(r3(xt_sl(0, 0, 0, NCH * TB), NCH),
                       xT[:, 0:TB].rearrange("(c p) t -> p c t", p=128), "sX0")
        ydma(tab_sl(0, False), cosT[:, 0:TB], "sB0")
        e_tab[0] = ydma(tab_sl(0, True), sinT[:, 0:TB], "sB0")

        e_init = e_c = e_pt = e_cidx = gdma(c_sb[:, :], c_rd[:, :], 'sINIT')
        e_silu = sop(lambda s: s.activation(out=silu_c[:, :], in_=c_sb[:, :],
                                            func=AF.Silu),
                     rd=("c_sb",), wr=("silu_c",), waits=[e_c])
        NMT = MSL // 512
        mod_ps = [psA[0:1, 0:512], psA[0:1, 512:1024], psB[0:1, 0:512]]
        wAf = wA.ap().bitcast(F32)
        e_modmm = None
        e_wq_ev = None
        for qtr in range(4):
            wts = [e_wq_ev] if e_wq_ev is not None else []
            e_wq_ev = gdma(wAf[:, 0:4 * MSL].rearrange("p (c m) -> p c m", c=4),
                           wmod[qtr * 512:(qtr + 1) * 512, :].rearrange(
                               "(c p) m -> p c m", p=128),
                           "sW0", waits=wts)
            e_q = None
            for ch in range(4):
                gch = qtr * 4 + ch
                for ntl in range(NMT):
                    w = [e_wq_ev, e_silu] if (ch == 0 and ntl == 0) else ()
                    e_q = mm(mod_ps[ntl], silu_c[:, gch:gch + 1],
                             wAf[:, ch * MSL + ntl * 512: ch * MSL + (ntl + 1) * 512],
                             start=(gch == 0), stop=(gch == 15), waits=w,
                             inc="sT" if (ch == 3 and ntl == NMT - 1) else None)
            e_wq_ev = e_q
        e_modmm = e_q
        e_mr = None
        for ntl in range(NMT):
            e_mr = vop(lambda v, n=ntl: v.tensor_copy(
                out=mod_row[:, n * 512:(n + 1) * 512], in_=mod_ps[n]),
                wr=("mod_row",), waits=[e_modmm])
        e_mi = gdma(mod_in.ap(), mod_row[:, :], 'sW0', waits=[e_mr])
        e_ag1 = p.emit("gpsimd",
                       lambda g: g.collective_compute(
                           "AllGather", ALU.bypass,
                           replica_groups=[list(range(N_CORES))],
                           ins=[mod_in.ap().opt()], outs=[mod_out.ap().opt()]),
                       waits=[e_mi], inc="sCC")
        mtiles = [m_sa, m_ca1, m_ga, m_sb, m_cb1, m_gb]
        mtags = ["m_sa", "m_ca1", "m_ga", "m_sb", "m_cb1", "m_gb"]
        e_md_last = None
        for vi in range(6):
            e_md_last = gdma(mtiles[vi][:, :],
                             mod_out[vi * D:(vi + 1) * D].rearrange("(c p) -> p c", p=128),
                             "sMD", waits=[e_ag1], slow=True)
        e_md = [e_md_last] * 6
        vop(lambda v: v.tensor_scalar(out=m_ca1[:, :], in0=m_ca1[:, :], scalar1=1.0,
                                      scalar2=None, op0=ALU.add),
            rd=("m_ca1",), wr=("m_ca1",), waits=[e_md[1]])
        vop(lambda v: v.tensor_scalar(out=m_cb1[:, :], in0=m_cb1[:, :], scalar1=1.0,
                                      scalar2=None, op0=ALU.add),
            rd=("m_cb1",), wr=("m_cb1",), waits=[e_md[4]])
        for gt, tg, ei in ((m_ga, "m_ga", e_md[2]), (m_gb, "m_gb", e_md[5])):
            vop(lambda v, g=gt: v.tensor_scalar(out=mscr1[:, :], in0=g[:, :],
                                                scalar1=0.0, scalar2=None, op0=ALU.is_ge),
                rd=(tg,), wr=("mscr1",), waits=[ei])
            vop(lambda v, g=gt: v.tensor_scalar(out=mscr2[:, :], in0=g[:, :],
                                                scalar1=0.25, scalar2=None, op0=ALU.max),
                rd=(tg,), wr=("mscr2",))
            vop(lambda v, g=gt: v.tensor_scalar(out=g[:, :], in0=g[:, :],
                                                scalar1=-0.25, scalar2=None, op0=ALU.min),
                rd=(tg,), wr=(tg,))
            vop(lambda v, g=gt: v.tensor_tensor(out=mscr2[:, :], in0=mscr2[:, :],
                                                in1=g[:, :], op=ALU.subtract),
                rd=("mscr2", tg), wr=("mscr2",))
            vop(lambda v: v.tensor_tensor(out=mscr2[:, :], in0=mscr1[:, :],
                                          in1=mscr2[:, :], op=ALU.mult),
                rd=("mscr1", "mscr2"), wr=("mscr2",))
            vop(lambda v, g=gt: v.tensor_tensor(out=g[:, :], in0=g[:, :],
                                                in1=mscr2[:, :], op=ALU.add),
                rd=(tg, "mscr2"), wr=(tg,))
        e_mod_rdy = p.last("sV")
        if taps:
            gdma(mtap[:, :], m_ga[:, :], "sTP2", waits=[e_mod_rdy])

        if NTB > 1:
            e_xt[1] = gdma(r3(xt_sl(1, 0, 0, NCH * TB), NCH),
                           xT[:, TB:2 * TB].rearrange("(c p) t -> p c t", p=128),
                           "sX1")
            ydma(tab_sl(1, False), cosT[:, TB:2 * TB], "sB1")
            e_tab[1] = ydma(tab_sl(1, True), sinT[:, TB:2 * TB], "sB1")
        e_wqkv = gdma(r3(wA[:, 0:NCH * 768], NCH),
                      wqkv[:, :].rearrange("(c p) m -> p c m", p=128),
                      "sW0", waits=[e_modmm])

        WQ = [wA[:, ch * 768:ch * 768 + 256] for ch in range(NCH)]
        WK = [wA[:, ch * 768 + 256:ch * 768 + 512] for ch in range(NCH)]
        WV = [wA[:, ch * 768 + 512:ch * 768 + 768] for ch in range(NCH)]

        qk_ps = [psB[:, 0:512], psB[:, 512:1024], psC[:, :], psD[:, :]]
        prev_psread = [None] * 4
        prev_hlast = None
        prev_rowend = None
        prev_umax = None
        e_vdone_last = None
        e_hlast_final = None

        for i in range(NTB):
            b = i % 2
            e_smm_last = None
            e_smm_by = {}
            for ch in range(NCH):
                ssl = ch % 2
                wsq = []
                if ch < 2:
                    wsq = [e_xt[i], e_const]
                else:
                    wsq = [e_smm_by[ch - 2]]
                e_sq = vop(lambda v, c=ch, bb=b, s=ssl: v.tensor_tensor(
                    out=sq_sl(s), in0=xt_sl(bb, c), in1=xt_sl(bb, c), op=ALU.mult),
                    rd=(f"xt{b}_{ch}",), wr=(f"sq{ssl}",), waits=wsq)
                w0 = []
                if ch == 0:
                    w0 = [e_xt[i], e_const]
                    if prev_rowend is not None:
                        w0.append(prev_rowend)
                    if prev_umax is not None:
                        w0.append(prev_umax)
                mm(psE[0:1, :], ones_b[:, :], xt_sl(b, ch),
                   start=(ch == 0), stop=(ch == NCH - 1), waits=w0)
                e_smm_last = e_smm_by[ch] = mm(psF[0:1, :], ones_b[:, :], sq_sl(ssl),
                                               start=(ch == 0), stop=(ch == NCH - 1),
                                               waits=[e_sq], inc="sT")
            vop(lambda v: v.tensor_scalar(out=row_m[:, :], in0=psE[0:1, :],
                                          scalar1=1.0 / D, scalar2=None, op0=ALU.mult),
                wr=("row_m",), waits=[e_smm_last])
            vop(lambda v: v.tensor_scalar(out=row_s1[:, :], in0=psF[0:1, :],
                                          scalar1=1.0 / D, scalar2=None, op0=ALU.mult),
                wr=("row_s1",))
            vop(lambda v: v.tensor_tensor(out=row_s2[:, :], in0=row_m[:, :],
                                          in1=row_m[:, :], op=ALU.mult),
                rd=("row_m",), wr=("row_s2",))
            vop(lambda v: v.tensor_tensor(out=row_s1[:, :], in0=row_s1[:, :],
                                          in1=row_s2[:, :], op=ALU.subtract),
                rd=("row_s1", "row_s2"), wr=("row_s1",))
            vop(lambda v: v.tensor_scalar(out=row_s1[:, :], in0=row_s1[:, :],
                                          scalar1=EPS, scalar2=None, op0=ALU.add),
                rd=("row_s1",), wr=("row_s1",))
            e_rcp = vop(lambda v: v.reciprocal(out=row_s2[:, :], in_=row_s1[:, :]),
                        rd=("row_s1",), wr=("row_s2",))
            prev_rowend = e_rcp
            e_sqrt = sop(lambda s: s.activation(out=row_r[:, :], in_=row_s2[:, :],
                                                func=AF.Sqrt),
                         rd=("row_s2",), wr=("row_r",), waits=[e_rcp])
            e_rows = vop(lambda v: v.scalar_tensor_tensor(
                out=row_mr[:, :], in0=row_m[:, :], scalar=-1.0, in1=row_r[:, :],
                op0=ALU.mult, op1=ALU.mult),
                rd=("row_m", "row_r"), wr=("row_mr",), waits=[e_sqrt])
            wbc = [e_rows]
            if prev_hlast is not None:
                wbc.append(prev_hlast)
            if i == 0:
                wbc.append(e_mr)
            mm(psA[:, 0:512], ones_f[:, :], row_r[0:1, 0:512],
               start=True, stop=True, waits=wbc)
            e_bc = mm(psA[:, 512:1024], ones_f[:, :], row_mr[0:1, 0:512],
                      start=True, stop=True, inc="sT")
            e_h = {}
            for ch in range(NCH):
                w0 = [e_bc] if ch == 0 else []
                if i == 0 and ch == 0:
                    w0.append(e_mod_rdy)
                vop(lambda v, c=ch, bb=b: v.tensor_tensor(
                    out=scr_sl(0), in0=xt_sl(bb, c), in1=psA[:, 0:512], op=ALU.mult),
                    rd=(f"xt{b}_{ch}",), wr=("scr0",), waits=w0)
                vop(lambda v: v.tensor_tensor(
                    out=scr_sl(0), in0=scr_sl(0), in1=psA[:, 512:1024], op=ALU.add),
                    rd=("scr0",), wr=("scr0",))
                e_h[ch] = vop(lambda v, c=ch, bb=b: v.tensor_scalar(
                    out=xt_sl(bb, c), in0=scr_sl(0),
                    scalar1=m_ca1[:, c:c + 1], scalar2=m_sa[:, c:c + 1],
                    op0=ALU.mult, op1=ALU.add),
                    rd=("scr0", "m_ca1", "m_sa"), wr=(f"xt{b}_{ch}",))
            prev_hlast = e_h[NCH - 1]
            e_hlast_final = e_h[NCH - 1]
            if taps and i == 0:
                gdma(htap[:, :], xt_sl(b, 0, 0, NCH * TB), 'sTP2', waits=[e_h[NCH - 1]])
            e_qk_mm = [None] * 4
            for ch in range(NCH):
                for mt in range(4):
                    lhs = (WQ[ch][:, (mt % 2) * 128:(mt % 2 + 1) * 128] if mt < 2
                           else WK[ch][:, (mt - 2) * 128:(mt - 1) * 128])
                    w0 = []
                    if ch == 0:
                        w0 = [e_h[ch], e_wqkv]
                        if prev_psread[mt] is not None:
                            w0.append(prev_psread[mt])
                    else:
                        w0 = [e_h[ch]]
                    e_qk_mm[mt] = mm(qk_ps[mt], lhs, xt_sl(b, ch),
                                     start=(ch == 0), stop=(ch == NCH - 1),
                                     waits=w0,
                                     inc="sT" if ch == NCH - 1 else None)
            e_vdone = None
            for sbi in range(TB // KSUB):
                e_vmm = None
                for ch in range(NCH):
                    w0 = []
                    if ch == 0:
                        w0 = [e_rcp, e_h[NCH - 1]]
                        if e_vdone is not None:
                            w0.append(e_vdone)
                    e_vmm = mm(psF[:, 0:256],
                               xt_sl(b, ch, sbi * KSUB, KSUB), WV[ch],
                               start=(ch == 0), stop=(ch == NCH - 1),
                               waits=w0, inc="sT" if ch == NCH - 1 else None)
                gsb = i * (TB // KSUB) + sbi
                e_vdone = sop(lambda s, g=gsb: s.activation(
                    out=va[:, g * 256:(g + 1) * 256], in_=psF[:, 0:256],
                    func=AF.Identity),
                    wr=(f"v{gsb}",), waits=[e_vmm])
                if taps and gsb == 0:
                    gdma(vtap[:, :], va[:, 0:256], 'sTP2', waits=[e_vdone])
            e_vdone_last = e_vdone
            prev_r3 = None
            prev_mx = None
            for mt in range(4):
                h = mt % 2
                isq = mt < 2
                base = (0 if isq else 2 * seq) + h * seq + i * TB
                dst = qk[:, base:base + TB]
                tag = f"qk_m{mt}_{i}"
                wev = [e_qk_mm[mt], e_vdone]
                if prev_r3 is not None:
                    wev.append(prev_r3)
                e_ev = sop(lambda s, ps=qk_ps[mt]: s.activation(
                    out=scr_sl(1), in_=ps, func=AF.Identity),
                    wr=("scr1",), waits=wev)
                e_rot = mm(qk_ps[mt], pT_sb[:, :], scr_sl(1), start=True, stop=True,
                           waits=[e_ev, e_pt], inc="sT")
                vop(lambda v, d=dst, bb=b: v.tensor_tensor(
                    out=d, in0=scr_sl(1), in1=tab_sl(bb, False), op=ALU.mult),
                    rd=("scr1", f"tab{b}"), wr=(tag,),
                    waits=[e_rot, e_tab[i]])
                e_psr = vop(lambda v, ps=qk_ps[mt], bb=b: v.tensor_tensor(
                    out=scr_sl(1), in0=ps, in1=tab_sl(bb, True), op=ALU.mult),
                    rd=(f"tab{b}",), wr=("scr1",))
                prev_psread[mt] = e_psr
                e_r3 = vop(lambda v, d=dst: v.tensor_tensor(
                    out=d, in0=d, in1=scr_sl(1), op=ALU.add),
                    rd=(tag, "scr1"), wr=(tag,))
                prev_r3 = e_r3
                e_sq2 = vop(lambda v, d=dst: v.tensor_tensor(
                    out=sq_sl(0), in0=d, in1=d, op=ALU.mult),
                    rd=(tag,), wr=("sq0",), waits=[e_smm_last])
                wn = [e_sq2]
                if prev_mx is not None:
                    wn.append(prev_mx)
                else:
                    wn.append(e_rcp)
                    if prev_umax is not None:
                        wn.append(prev_umax)
                e_nmm = mm(psE[0:1, :], ones_b[:, :], sq_sl(0), start=True, stop=True,
                           waits=wn, inc="sT")
                mxt, mxn = (mx1, "mx1") if isq else (mx2, "mx2")
                vop(lambda v: v.reduce_max(out=sc1[:, :], in_=psE[0:1, :], axis=AX.X),
                    rd=(), wr=("sc1",), waits=[e_nmm])
                e_mx = vop(lambda v, m=mxt: v.tensor_tensor(
                    out=m[:, :], in0=m[:, :], in1=sc1[:, :], op=ALU.max),
                    rd=(mxn, "sc1"), wr=(mxn,))
                prev_mx = e_mx
                if taps and i == 0 and mt == 0:
                    gdma(qtap[:, :], dst, 'sTP2', waits=[e_r3])
                if taps and i == 0 and mt == 2:
                    gdma(ktap[:, :], dst, 'sTP2', waits=[e_r3])
            prev_umax = prev_mx
            j = i + 2
            if j < NTB:
                e_qkv_done = p.last("sT")
                e_xt[j] = gdma(r3(xt_sl(j % 2, 0, 0, NCH * TB), NCH),
                               xT[:, j * TB:(j + 1) * TB].rearrange(
                                   "(c p) t -> p c t", p=128),
                               "sX0" if j % 2 == 0 else "sX1",
                               waits=[e_qkv_done])
                ydma(tab_sl(j % 2, False), cosT[:, j * TB:(j + 1) * TB],
                     "sB0" if j % 2 == 0 else "sB1", waits=[prev_r3])
                e_tab[j] = ydma(tab_sl(j % 2, True), sinT[:, j * TB:(j + 1) * TB],
                                "sB0" if j % 2 == 0 else "sB1")

        e_u1 = vop(lambda v: v.tensor_tensor(out=sc1[:, :], in0=mx1[:, :],
                                             in1=mx2[:, :], op=ALU.mult),
                   rd=("mx1", "mx2"), wr=("sc1",), waits=[prev_umax])
        e_u2 = sop(lambda s: s.activation(out=sc1[:, :], in_=sc1[:, :], func=AF.Sqrt),
                   rd=("sc1",), wr=("sc1",), waits=[e_u1])
        e_u3 = vop(lambda v: v.tensor_scalar(out=sc1[:, :], in0=sc1[:, :],
                                             scalar1=-INV_SQRT_HD, scalar2=None,
                                             op0=ALU.mult),
                   rd=("sc1",), wr=("sc1",), waits=[e_u2])
        e_u5 = mm(psF[:, 0:1], ones_f[:, :], sc1[:, :], start=True, stop=True,
                  waits=[e_u3, e_vdone_last], inc="sT")
        e_negU = vop(lambda v: v.tensor_copy(out=negU[:, :], in_=psF[:, 0:1]),
                     wr=("negU",), waits=[e_u5])
        if taps:
            gdma(utap[:, :], negU[:, :], 'sTP2', waits=[e_negU])

        e_taps_preB = p.last("sTP2") if taps else None
        def a_sl(s, lo=0, w=KGRP * QB):
            o = 16384 + s * 1024 + lo
            return big[:, o:o + w]

        e_ao = {}
        prev_exp2 = [None, None]
        prev_grp2 = [None, None]
        prev_av_evt = {0: None, 1: None}
        it = 0
        for h in range(HP):
            for qb in range(NQB):
                par = it % 2
                avbk = psC if par == 0 else psD
                smbk = psE if par == 0 else psF
                e_exp = {}
                e_grp = {}
                for g in range(NG):
                    sps = psA if g % 2 == 0 else psB
                    wts = []
                    if g >= 2:
                        wts.append(e_exp[g - 2])
                    else:
                        if prev_exp2[g % 2] is not None:
                            wts.append(prev_exp2[g % 2])
                        if it == 0:
                            wts.append(e_hlast_final)
                            for ev in prev_psread:
                                if ev is not None:
                                    wts.append(ev)
                    e_s2 = None
                    for kk in range(KGRP):
                        kb = g * KGRP + kk
                        lhs = qk[:, 2 * seq + h * seq + kb * KSUB:
                                 2 * seq + h * seq + (kb + 1) * KSUB]
                        rhs = qk[:, h * seq + qb * QB: h * seq + (qb + 1) * QB]
                        e_s2 = mm(sps[:, kk * 512:(kk + 1) * 512], lhs, rhs,
                                  start=True, stop=True,
                                  waits=wts if kk == 0 else (),
                                  inc="sT" if kk == KGRP - 1 else None)
                    wts2 = [e_s2, e_negU]
                    if g >= 2:
                        wts2.append(e_grp[g - 2])
                    elif prev_grp2[g % 2] is not None:
                        wts2.append(prev_grp2[g % 2])
                    if it == 0 and g < 2:
                        wts2.append(prev_r3)
                    if taps and it == 0 and g == 2:
                        wts2.append(e_atap)
                    e_exp[g] = sop(lambda s, sl=g % 2, ps=sps: s.activation(
                        out=a_sl(sl), in_=ps[:, 0:KGRP * 512], func=AF.Exp,
                        scale=INV_SQRT_HD, bias=negU[:, 0:1]),
                        rd=("negU",), waits=wts2)
                    for kk in range(KGRP):
                        kb = g * KGRP + kk
                        wv = []
                        if kk == 0:
                            wv = [e_exp[g]]
                            if g == 0:
                                if prev_av_evt[par] is not None:
                                    wv.append(prev_av_evt[par])
                                if it == 0:
                                    wv.append(e_negU)
                                if it == 1:
                                    wv.append(e_negU)
                        mm(avbk[:, :],
                           va[:, kb * 256 + h * 128: kb * 256 + (h + 1) * 128],
                           a_sl(g % 2, kk * 512, 512),
                           start=(kb == 0), stop=(kb == NSB - 1), waits=wv)
                        e_grp[g] = mm(smbk[0:1, :], ones_b[:, :],
                                      a_sl(g % 2, kk * 512, 512),
                                      start=(kb == 0), stop=(kb == NSB - 1),
                                      inc="sT" if kk == KGRP - 1 else None)
                    if taps and it == 0 and g == 0:
                        e_atap = gdma(atap[:, :], a_sl(0), 'sTP1', waits=[e_grp[0]])
                prev_exp2 = [e_exp[NG - 2], e_exp[NG - 1]]
                prev_grp2 = [e_grp[NG - 2], e_grp[NG - 1]]
                e_rs = vop(lambda v, sk=smbk: v.reciprocal(out=rs_row[:, :],
                                                           in_=sk[0:1, :]),
                           rd=(), wr=("rs_row",), waits=[e_grp[NG - 1]])
                e_rb = mm(smbk[:, :], ones_f[:, :], rs_row[0:1, :],
                          start=True, stop=True, waits=[e_rs], inc="sT")
                vop(lambda v, sk=smbk: v.tensor_copy(out=rsb[:, :], in_=sk[:, :]),
                    wr=("rsb",), waits=[e_rb])
                wao = []
                if taps and it == 0 and e_taps_preB is not None:
                    wao.append(e_taps_preB)
                e_ao[(h, qb)] = vop(lambda v, hh=h, q=qb, ab=avbk: v.tensor_tensor(
                    out=big[:, hh * seq + q * QB: hh * seq + (q + 1) * QB],
                    in0=ab[:, :], in1=rsb[:, :], op=ALU.mult),
                    rd=("rsb",), wr=(f"ao{h}_{qb}",), waits=wao)
                prev_av_evt[par] = e_ao[(h, qb)]
                if taps and h == 0 and qb == 0:
                    gdma(otap[:, :], big[:, 0:QB], 'sTP2', waits=[e_ao[(0, 0)]])
                it += 1

        e_aow = []
        for h in range(HP):
            e_aow.append(gdma(ao_in[h * 128:(h + 1) * 128, :],
                              big[:, h * seq:(h + 1) * seq], "sAO",
                              waits=[e_ao[(h, q)] for q in range(NQB)]))
        e_aow = [e_aow[-1]]
        e_ag2 = p.emit("gpsimd",
                       lambda g: g.collective_compute(
                           "AllGather", ALU.bypass,
                           replica_groups=[list(range(N_CORES))],
                           ins=[ao_in.ap().opt()], outs=[ao_out.ap().opt()]),
                       waits=e_aow, inc="sCC")

        xT_r = xT.ap().rearrange("d (k t) -> (d k) t", k=N_CORES)
        e_xsl = [None] * NCH
        for m in range(2):
            e_xsl[m] = p.emit(
                "gpsimd",
                lambda g, mm_=m: g.indirect_dma_start(
                    out=xsl[:, (mm_ % 2) * TOK:(mm_ % 2) * TOK + TOK],
                    out_offset=None, in_=xT_r,
                    in_offset=IndirectOffsetOnAxis(ap=cidx[:, mm_:mm_ + 1], axis=0)),
                waits=[e_init] if m == 0 else (),
                inc="sL0" if m % 2 == 0 else "sL1", amt=16)

        ao_r = ao_out.ap().rearrange("d (k t) -> (d k) t", k=N_CORES)

        def ao_loc(ch, lo=0, w=None):
            if w is None:
                w = TOK
            return big[:, ch * TOK + lo: ch * TOK + lo + w]

        e_taps_preD = p.last("sTP2") if taps else None
        e_aog_last = None
        for ch in range(NCH):
            w0 = []
            if ch == 0:
                w0 = [e_ag2, e_init]
                if e_taps_preD is not None:
                    w0.append(e_taps_preD)
            e_aog_last = p.emit(
                "gpsimd",
                lambda g, c=ch: g.indirect_dma_start(
                    out=ao_loc(c), out_offset=None, in_=ao_r,
                    in_offset=IndirectOffsetOnAxis(ap=cidx[:, c:c + 1], axis=0)),
                waits=w0, inc="sAO", amt=16)
        e_aog = [e_aog_last] * NCH

        RING = 5

        def wslot(s, ch=None, w=128):
            if ch is None:
                return wA[:, s * 2048:(s + 1) * 2048]
            return wA[:, s * 2048 + ch * w: s * 2048 + (ch + 1) * w]

        wsets = ([("proj", m) for m in range(NCH)]
                 + [("up", m) for m in range(NCH)])
        for m in range(NCH):
            wsets.append(("down", m))
            wsets.append(("exp", m))
        wsrc = {"proj": wproj, "up": wupT, "down": wdownT, "exp": wexp}
        slot_free = [None] * RING
        e_wset = [None] * len(wsets)
        set_slot = {}
        e_qkv_all_done = p.last("sT")
        e_wgate = gdma(r3(wA[:, 10240:10240 + NCH * 8], NCH),
                       wgT[:, :].rearrange("(c p) m -> p c m", p=128),
                       "sWG", waits=[e_qkv_all_done])

        def emit_wset_dma(si):
            kind, m = wsets[si]
            s = si % RING
            set_slot[si] = s
            wts = []
            if si < RING:
                wts.append(e_qkv_all_done)
            if slot_free[s] is not None:
                wts.append(slot_free[s])
            src = wsrc[kind][:, m * 128:(m + 1) * 128].rearrange(
                "(c p) m -> p c m", p=128)
            e_wset[si] = gdma(r3(wslot(s), NCH), src, f"sW{s}", wts)

        n_wdma = 0
        for _ in range(min(RING, len(wsets))):
            emit_wset_dma(n_wdma)
            n_wdma += 1

        rot_ps = [psA[:, 0:512], psA[:, 512:1024], psB[:, 0:512], psB[:, 512:1024]]
        rot_free = [prev_exp2[0], prev_exp2[0], prev_exp2[1], prev_exp2[1]]
        e_pd_start = [prev_exp2[0], prev_exp2[1]]

        si = 0
        e_x1 = {}
        ri = 0
        for m in range(NCH):
            for nt in range(NT_D):
                w = NTW[nt]
                pp = rot_ps[ri % 4]
                e_p = None
                for ch in range(NCH):
                    wts = []
                    if ch == 0:
                        wts = [e_wset[si], e_aog[ch]]
                        if rot_free[ri % 4] is not None:
                            wts.append(rot_free[ri % 4])
                    else:
                        wts = [e_aog[ch]]
                    e_p = mm(pp[:, 0:w], wslot(set_slot[si], ch),
                             ao_loc(ch, nt * TWD, w),
                             start=(ch == 0), stop=(ch == NCH - 1),
                             waits=wts, inc="sT" if ch == NCH - 1 else None)
                wts = [e_p, e_xsl[m], e_mod_rdy] + [ev for ev in e_pd_start if ev]
                if taps and e_taps_preD is not None:
                    wts.append(e_taps_preD)
                e_x1[(m, nt)] = vop(
                    lambda v, mm_=m, n=nt, ppp=pp, ww=w: v.scalar_tensor_tensor(
                        out=x1f[:, mm_ * TOK + n * TWD: mm_ * TOK + n * TWD + ww],
                        in0=ppp[:, 0:ww], scalar=m_ga[:, mm_:mm_ + 1],
                        in1=xsl[:, (mm_ % 2) * TOK + n * TWD:
                                (mm_ % 2) * TOK + n * TWD + ww],
                        op0=ALU.mult, op1=ALU.add),
                    rd=("m_ga", f"xsl{m % 2}"), wr=(f"x1_{m}",),
                    waits=wts)
                rot_free[ri % 4] = e_x1[(m, nt)]
                ri += 1
            if m + 2 < NCH:
                e_xsl[m + 2] = p.emit(
                    "gpsimd",
                    lambda g, mm_=m + 2: g.indirect_dma_start(
                        out=xsl[:, (mm_ % 2) * TOK:(mm_ % 2) * TOK + TOK],
                        out_offset=None, in_=xT_r,
                        in_offset=IndirectOffsetOnAxis(ap=cidx[:, mm_:mm_ + 1],
                                                       axis=0)),
                    waits=[e_x1[(m, NT_D - 1)]],
                    inc="sL0" if m % 2 == 0 else "sL1", amt=16)
            slot_free[set_slot[si]] = e_x1[(m, NT_D - 1)]
            si += 1
            if n_wdma < len(wsets):
                emit_wset_dma(n_wdma)
                n_wdma += 1
        e_proj_done = p.last("sT")
        if taps:
            gdma(x1tap[:, :], x1f[:, 0:TOK], 'sTP2', waits=[e_x1[(0, NT_D - 1)]])

        prev_rowend2 = None
        e_h2 = {}
        e_h2_last = None
        for nt in range(NT_D):
            w = NTW[nt]
            e_smm2 = None
            e_smm2_by = {}
            for ch in range(NCH):
                ssl = ch % 2
                wsq = [e_x1[(ch, nt)]]
                if ch >= 2:
                    wsq.append(e_smm2_by[ch - 2])
                e_sq = vop(lambda v, c=ch, n=nt, s=ssl, ww=w: v.tensor_tensor(
                    out=sq_sl(s)[:, 0:ww],
                    in0=x1f[:, c * TOK + n * TWD: c * TOK + n * TWD + ww],
                    in1=x1f[:, c * TOK + n * TWD: c * TOK + n * TWD + ww],
                    op=ALU.mult),
                    rd=(f"x1_{ch}",), wr=(f"sq{ssl}",),
                    waits=wsq)
                wts = []
                if ch == 0:
                    wts = [e_x1[(ch, nt)]]
                    if prev_rowend2 is not None:
                        wts.append(prev_rowend2)
                    else:
                        wts.append(e_ao[(HP - 1, NQB - 1)])
                        wts.append(e_ao[(HP - 1, NQB - 2)])
                        wts.append(e_negU)
                mm(psE[0:1, 0:w], ones_c32[:, :],
                   x1f[:, ch * TOK + nt * TWD: ch * TOK + nt * TWD + w],
                   start=(ch == 0), stop=(ch == NCH - 1), waits=wts)
                e_smm2 = e_smm2_by[ch] = mm(psF[0:1, 0:w], ones_b[:, :],
                                            sq_sl(ssl)[:, 0:w],
                                            start=(ch == 0), stop=(ch == NCH - 1),
                                            waits=[e_sq], inc="sT")
            vop(lambda v, ww=w: v.tensor_scalar(out=row_m[:, 0:ww], in0=psE[0:1, 0:ww],
                                                scalar1=1.0 / D, scalar2=None,
                                                op0=ALU.mult),
                wr=("row_m",), waits=[e_smm2])
            vop(lambda v, ww=w: v.tensor_scalar(out=row_s1[:, 0:ww], in0=psF[0:1, 0:ww],
                                                scalar1=1.0 / D, scalar2=None,
                                                op0=ALU.mult),
                wr=("row_s1",))
            vop(lambda v, ww=w: v.tensor_tensor(out=row_s2[:, 0:ww], in0=row_m[:, 0:ww],
                                                in1=row_m[:, 0:ww], op=ALU.mult),
                rd=("row_m",), wr=("row_s2",))
            vop(lambda v, ww=w: v.tensor_tensor(out=row_s1[:, 0:ww], in0=row_s1[:, 0:ww],
                                                in1=row_s2[:, 0:ww], op=ALU.subtract),
                rd=("row_s1", "row_s2"), wr=("row_s1",))
            vop(lambda v, ww=w: v.tensor_scalar(out=row_s1[:, 0:ww], in0=row_s1[:, 0:ww],
                                                scalar1=EPS, scalar2=None, op0=ALU.add),
                rd=("row_s1",), wr=("row_s1",))
            e_rcp2 = vop(lambda v, ww=w: v.reciprocal(out=row_s2[:, 0:ww],
                                                      in_=row_s1[:, 0:ww]),
                         rd=("row_s1",), wr=("row_s2",))
            prev_rowend2 = e_rcp2
            e_sqrt2 = sop(lambda s, ww=w: s.activation(out=row_r[:, 0:ww],
                                                       in_=row_s2[:, 0:ww],
                                                       func=AF.Sqrt),
                          rd=("row_s2",), wr=("row_r",), waits=[e_rcp2])
            e_rows2 = vop(lambda v, ww=w: v.scalar_tensor_tensor(
                out=row_mr[:, 0:ww], in0=row_m[:, 0:ww], scalar=-1.0,
                in1=row_r[:, 0:ww], op0=ALU.mult, op1=ALU.mult),
                rd=("row_m", "row_r"), wr=("row_mr",), waits=[e_sqrt2])
            wbc = [e_rows2, e_proj_done]
            if e_h2_last is not None:
                wbc.append(e_h2_last)
            mm(psA[:, 0:512][:, 0:w], ones_f[:, :], row_r[0:1, 0:w],
               start=True, stop=True, waits=wbc)
            e_bc2 = mm(psA[:, 512:1024][:, 0:w], ones_f[:, :], row_mr[0:1, 0:w],
                       start=True, stop=True, inc="sT")
            for ch in range(NCH):
                vop(lambda v, c=ch, n=nt, ww=w: v.tensor_tensor(
                    out=scr_sl(0)[:, 0:ww],
                    in0=x1f[:, c * TOK + n * TWD: c * TOK + n * TWD + ww],
                    in1=psA[:, 0:ww], op=ALU.mult),
                    rd=(f"x1_{ch}",), wr=("scr0",),
                    waits=[e_bc2] if ch == 0 else ())
                vop(lambda v, ww=w: v.tensor_tensor(
                    out=scr_sl(0)[:, 0:ww], in0=scr_sl(0)[:, 0:ww],
                    in1=psA[:, 512:512 + ww], op=ALU.add),
                    rd=("scr0",), wr=("scr0",))
                e_h2[(ch, nt)] = vop(lambda v, c=ch, n=nt, ww=w: v.tensor_scalar(
                    out=va[:, c * TOK + n * TWD: c * TOK + n * TWD + ww],
                    in0=scr_sl(0)[:, 0:ww],
                    scalar1=m_cb1[:, c:c + 1], scalar2=m_sb[:, c:c + 1],
                    op0=ALU.mult, op1=ALU.add),
                    rd=("scr0", "m_cb1", "m_sb"), wr=(f"h2_{ch}_{nt}",))
                e_h2_last = e_h2[(ch, nt)]
        e_h2tap = None
        if taps:
            e_h2tap = gdma(h2tap[:, :], va[:, 0:TOK], 'sTP2', waits=[e_h2_last])

        e_gate_vec = None
        for s8 in range(NTSB):
            e_gmm = None
            for ch in range(NCH):
                wts = []
                if ch == 0:
                    wts = [e_wgate, e_h2[(ch, ((s8 + 1) * 128 - 1) // TWD)]]
                    if e_gate_vec is not None:
                        wts.append(e_gate_vec)
                    else:
                        wts.append(prev_av_evt[0])
                else:
                    wts = [e_h2[(ch, ((s8 + 1) * 128 - 1) // TWD)]]
                e_gmm = mm(psC[:, 0:8],
                           va[:, ch * TOK + s8 * 128: ch * TOK + (s8 + 1) * 128],
                           wA[:, 10240 + ch * 8: 10240 + (ch + 1) * 8],
                           start=(ch == 0), stop=(ch == NCH - 1),
                           waits=wts, inc="sT" if ch == NCH - 1 else None)
            vop(lambda v: v.reduce_max(out=g_m1[:, :], in_=psC[:, 0:8], axis=AX.X),
                wr=("g_m1",), waits=[e_gmm])
            vop(lambda v: v.tensor_scalar(out=g_d[:, :], in0=psC[:, 0:8],
                                          scalar1=g_m1[:, 0:1], scalar2=None,
                                          op0=ALU.subtract),
                rd=("g_m1",), wr=("g_d",))
            vop(lambda v: v.tensor_scalar(out=g_pen[:, :], in0=g_d[:, :],
                                          scalar1=0.0, scalar2=-1e30,
                                          op0=ALU.is_equal, op1=ALU.mult),
                rd=("g_d",), wr=("g_pen",))
            vop(lambda v: v.tensor_tensor(out=g_pen[:, :], in0=g_pen[:, :],
                                          in1=g_d[:, :], op=ALU.add),
                rd=("g_pen", "g_d"), wr=("g_pen",))
            e_gm2 = vop(lambda v: v.reduce_max(out=g_m2[:, :], in_=g_pen[:, :],
                                               axis=AX.X),
                        rd=("g_pen",), wr=("g_m2",))
            e_gexp = sop(lambda s: s.activation(out=g_s2[:, :], in_=g_m2[:, :],
                                                func=AF.Exp),
                         rd=("g_m2",), wr=("g_s2",), waits=[e_gm2])
            vop(lambda v: v.tensor_scalar(out=g_inv[:, :], in0=g_s2[:, :],
                                          scalar1=1.0, scalar2=None, op0=ALU.add),
                rd=("g_s2",), wr=("g_inv",), waits=[e_gexp])
            vop(lambda v: v.reciprocal(out=g_inv[:, :], in_=g_inv[:, :]),
                rd=("g_inv",), wr=("g_inv",))
            vop(lambda v: v.tensor_tensor(out=g_m2[:, :], in0=g_s2[:, :],
                                          in1=g_inv[:, :], op=ALU.mult),
                rd=("g_s2", "g_inv"), wr=("g_m2",))
            e_gate_vec = vop(lambda v, s8_=s8: v.tensor_tensor(
                out=g_cols[:, s8_:s8_ + 1], in0=g_inv[:, :], in1=g_m2[:, :],
                op=ALU.add),
                rd=("g_inv", "g_m2"), wr=("g_cols",))
        e_gb1 = gdma(g_bnc.ap().rearrange("(s p) -> p s", p=128),
                     g_cols[:, 0:NTSB], "sGB", waits=[e_gate_vec], slow=True)
        e_gb2 = gdma(g_row[0:1, :], g_bnc.ap(), 'sGB', waits=[e_gb1])
        if taps:
            gdma(gtap[:, :], g_row[0:1, :], 'sTP2', waits=[e_gb2])

        up0 = NCH
        e_relu = {}
        ri = 0
        rot_free2 = [e_bc2, e_bc2, None, None]
        last_h2_read = e_h2_last
        for m in range(NCH):
            si2 = up0 + m
            for nt in range(NT_D):
                w = NTW[nt]
                pp = rot_ps[ri % 4]
                e_p = None
                for ch in range(NCH):
                    wts = [e_h2[(ch, nt)]]
                    if ch == 0:
                        wts.append(e_wset[si2])
                        if rot_free2[ri % 4] is not None:
                            wts.append(rot_free2[ri % 4])
                        if ri % 4 < 2:
                            wts.append(last_h2_read)
                    e_p = mm(pp[:, 0:w], wslot(set_slot[si2], ch),
                             va[:, ch * TOK + nt * TWD: ch * TOK + nt * TWD + w],
                             start=(ch == 0), stop=(ch == NCH - 1),
                             waits=wts, inc="sT" if ch == NCH - 1 else None)
                e_relu[(m, nt)] = sop(lambda s, mm_=m, n=nt, ppp=pp, ww=w: s.activation(
                    out=big[:, mm_ * TOK + n * TWD: mm_ * TOK + n * TWD + ww],
                    in_=ppp[:, 0:ww], func=AF.Relu),
                    wr=(f"relu{m}_{nt}",), waits=[e_p, e_proj_done])
                rot_free2[ri % 4] = e_relu[(m, nt)]
                ri += 1
            slot_free[set_slot[si2]] = e_relu[(m, NT_D - 1)]
            if n_wdma < len(wsets):
                emit_wset_dma(n_wdma)
                n_wdma += 1
        e_up_done = p.last("sT")

        e_h2g = {}
        e_h2g_last = None
        for nt in range(NT_D):
            w = NTW[nt]
            wgbd = [e_gb2, e_up_done, prev_rowend2]
            if e_h2g_last is not None:
                wgbd.append(e_h2g_last)
            if e_h2tap is not None:
                wgbd.append(e_h2tap)
            e_gbd = mm(psE[:, 0:w], ones_f[:, :], g_row[0:1, nt * TWD: nt * TWD + w],
                       start=True, stop=True, waits=wgbd, inc="sT")
            for ch in range(NCH):
                e_h2g[(ch, nt)] = vop(lambda v, c=ch, n=nt, ww=w: v.tensor_tensor(
                    out=va[:, c * TOK + n * TWD: c * TOK + n * TWD + ww],
                    in0=va[:, c * TOK + n * TWD: c * TOK + n * TWD + ww],
                    in1=psE[:, 0:ww], op=ALU.mult),
                    rd=(f"h2_{ch}_{nt}",), wr=(f"h2_{ch}_{nt}",),
                    waits=[e_gbd] if ch == 0 else ())
                e_h2g_last = e_h2g[(ch, nt)]

        dn0 = up0 + NCH
        ri = 0
        rot_free3 = [rot_free2[0], rot_free2[1], rot_free2[2], rot_free2[3]]
        e_st = {}
        for m in range(NCH):
            si_d = dn0 + 2 * m
            si_e = dn0 + 2 * m + 1
            for nt in range(NT_D):
                w = NTW[nt]
                pp = rot_ps[ri % 4]
                for r in range(NCH):
                    wts = [e_relu[(r, nt)]]
                    if r == 0:
                        wts.append(e_wset[si_d])
                        if rot_free3[ri % 4] is not None:
                            wts.append(rot_free3[ri % 4])
                    mm(pp[:, 0:w], wslot(set_slot[si_d], r),
                       big[:, r * TOK + nt * TWD: r * TOK + nt * TWD + w],
                       start=(r == 0), stop=False, waits=wts)
                e_p = None
                for ch in range(NCH):
                    wts = [e_h2g[(ch, nt)]]
                    if ch == 0:
                        wts.append(e_wset[si_e])
                    e_p = mm(pp[:, 0:w], wslot(set_slot[si_e], ch),
                             va[:, ch * TOK + nt * TWD: ch * TOK + nt * TWD + w],
                             start=False, stop=(ch == NCH - 1),
                             waits=wts, inc="sT" if ch == NCH - 1 else None)
                bsl = (m * NT_D + nt) % 2
                wts = [e_p]
                if (m * NT_D + nt) >= 2:
                    wts.append(e_st[(m * NT_D + nt) - 2])
                e_ob = vop(lambda v, mm_=m, n=nt, ppp=pp, ww=w, bs=bsl:
                           v.scalar_tensor_tensor(
                               out=ob[:, bs * 1024: bs * 1024 + ww],
                               in0=ppp[:, 0:ww], scalar=m_gb[:, mm_:mm_ + 1],
                               in1=x1f[:, mm_ * TOK + n * TWD:
                                       mm_ * TOK + n * TWD + ww],
                               op0=ALU.mult, op1=ALU.add),
                           rd=("m_gb", f"x1_{m}"), wr=(f"ob{bsl}",), waits=wts)
                e_st[m * NT_D + nt] = ydma(out[m, :, nt * TWD: nt * TWD + w],
                                           ob[:, bsl * 1024: bsl * 1024 + w],
                                           "sO0" if bsl == 0 else "sO1",
                                           waits=[e_ob])
                rot_free3[ri % 4] = e_ob
                ri += 1
            slot_free[set_slot[si_d]] = e_st[m * NT_D + NT_D - 1]
            slot_free[set_slot[si_e]] = e_st[m * NT_D + NT_D - 1]
            for _ in range(2):
                if n_wdma < len(wsets):
                    emit_wset_dma(n_wdma)
                    n_wdma += 1

        for di in range(pad_pe):
            mm(psF[0:1, 0:512], ones_b[:, :], big[:, 0:512],
               start=True, stop=True,
               waits=[p.last("sO0"), p.last("sO1")] if di == 0 else ())
        p.emit("sync", lambda s: s.nop(),
               waits=[p.last("sO0"), p.last("sO1")])
        p.emit("gpsimd", lambda g: g.nop(),
               waits=[p.last("sO0"), p.last("sO1")])

        @block.tensor
        def _(t):
            p.replay("tensor", t, sems)

        @block.scalar
        def _(s):
            p.replay("scalar", s, sems)

        @block.vector
        def _(v):
            p.replay("vector", v, sems)

        @block.gpsimd
        def _(g):
            p.replay("gpsimd", g, sems)

        @block.sync
        def _(s):
            p.replay("sync", s, sems)

    return nc


def make_inputs(seq, x, w_qkv, w_proj, W_up, W_down, W_gate, W_expert, c, wmod_full):
    TOK = seq // N_CORES
    MSL = 6 * D // N_CORES
    xT = np.ascontiguousarray(x.reshape(seq, D).T)
    wq3 = w_qkv.reshape(D, 3, 16, HD)
    theta = np.float32(10000.0)
    freqs = np.float32(1.0) / theta ** (np.arange(0, HD, 2, dtype=np.float32)
                                        / np.float32(HD))
    t = np.arange(seq, dtype=np.float32)
    f = np.outer(t, freqs)
    sin = np.repeat(np.sin(f), 2, axis=1)
    cos = np.repeat(np.cos(f), 2, axis=1)
    cosT = np.ascontiguousarray(cos.T).astype(ml_dtypes.bfloat16)
    sinT = np.ascontiguousarray(sin.T).astype(ml_dtypes.bfloat16)
    pT = np.zeros((128, 128), dtype=ml_dtypes.bfloat16)
    for i in range(64):
        pT[2 * i + 1, 2 * i] = -1.0
        pT[2 * i, 2 * i + 1] = 1.0
    c_r = np.ascontiguousarray(c.reshape(16, 128).T)
    wupT = np.ascontiguousarray(W_up.T)
    wdownT = np.ascontiguousarray(W_down.T)
    wgT = np.ascontiguousarray(W_gate.T)
    in_maps = []
    for j in range(N_CORES):
        hsel = [2 * j, 2 * j + 1]
        wqkv_j = np.ascontiguousarray(np.concatenate(
            [wq3[:, 0, hsel, :].reshape(D, 2 * HD),
             wq3[:, 1, hsel, :].reshape(D, 2 * HD),
             wq3[:, 2, hsel, :].reshape(D, 2 * HD)], axis=1))
        wmod_j = np.ascontiguousarray(wmod_full[:, j * MSL:(j + 1) * MSL])
        cidx = np.zeros((128, NCH), dtype=np.uint32)
        for ch in range(NCH):
            cidx[:, ch] = (128 * ch + np.arange(128)) * N_CORES + j
        in_maps.append({
            "xT": xT, "wqkv": wqkv_j, "wproj": w_proj, "wupT": wupT,
            "wdownT": wdownT, "wexp": W_expert, "wgT": wgT, "wmod": wmod_j,
            "c_r": c_r, "cosT": cosT, "sinT": sinT, "pT": pT, "cidx_in": cidx,
        })
    return in_maps


def kernel(L0_attn_x, L0_attn_w_qkv, L0_attn_w_proj, L0_ffn_W_up, L0_ffn_W_down,
           L0_ffn_W_gate, L0_ffn_W_expert, L0_c, L0_wmod):
    seq = 8192
    TOK = seq // N_CORES
    x = np.asarray(L0_attn_x, dtype=np.float32)
    in_maps = make_inputs(seq, x,
                          np.asarray(L0_attn_w_qkv, dtype=np.float32),
                          np.asarray(L0_attn_w_proj, dtype=np.float32),
                          np.asarray(L0_ffn_W_up, dtype=np.float32),
                          np.asarray(L0_ffn_W_down, dtype=np.float32),
                          np.asarray(L0_ffn_W_gate, dtype=np.float32),
                          np.asarray(L0_ffn_W_expert, dtype=np.float32),
                          np.asarray(L0_c, dtype=np.float32),
                          np.asarray(L0_wmod, dtype=np.float32))
    nc = build_program(seq)
    res = run_bass_kernel_spmd(nc, in_maps, core_ids=list(range(N_CORES)))
    pieces = []
    for j in range(N_CORES):
        o = np.asarray(res.results[j]["out"])
        pieces.append(o.reshape(D, TOK).T)
    full = np.concatenate(pieces, axis=0)
    return np.ascontiguousarray(full.reshape(1, seq, D)).astype(np.float32)


# version 03
# speedup vs baseline: 1.2237x; 1.2237x over previous
import sys
sys.path.insert(0, "/opt/trn_rl_repo")

import contextlib
import numpy as np
import ml_dtypes

import concourse.bass as bass
import concourse.mybir as mybir
from concourse.bass import IndirectOffsetOnAxis
from concourse.bass_utils import run_bass_kernel_spmd

F32 = mybir.dt.float32
BF16 = mybir.dt.bfloat16
U32 = mybir.dt.uint32
AF = mybir.ActivationFunctionType
ALU = mybir.AluOpType
AX = mybir.AxisListType

N_CORES = 8
D = 2048
NCH = 16
HD = 128
HP = 2
TB = 512
QB = 512
KSUB = 128
KGRP = 2
INV_SQRT_HD = float(np.float32(1.0) / np.sqrt(np.float32(128.0)))
EPS = 1e-5

ENGINES = ("tensor", "scalar", "vector", "gpsimd", "sync")


class Plan:
    def __init__(self):
        self.sched = {e: [] for e in ENGINES}
        self.cnt = {}
        self.wm = {e: {} for e in ENGINES}
        self.dw = {e: set() for e in ("vector", "scalar")}
        self.dr = {e: set() for e in ("vector", "scalar")}

    def _wait(self, eng, ev):
        if ev is None:
            return
        sem, val = ev
        if self.wm[eng].get(sem, 0) >= val:
            return
        self.wm[eng][sem] = val
        self.sched[eng].append(("wait", sem, val))

    def emit(self, eng, fn, reads=(), writes=(), waits=(), inc=None, amt=1):
        for ev in waits:
            self._wait(eng, ev)
        if eng in ("vector", "scalar"):
            rs, ws = set(reads), set(writes)
            if (rs & self.dw[eng]) or (ws & (self.dw[eng] | self.dr[eng])):
                self.sched[eng].append(("drain",))
                self.dw[eng] = set()
                self.dr[eng] = set()
            self.dw[eng] |= ws
            self.dr[eng] |= rs
        ev_out = None
        if inc is not None:
            self.cnt[inc] = self.cnt.get(inc, 0) + amt
            ev_out = (inc, self.cnt[inc])
        self.sched[eng].append(("op", fn, inc, amt))
        return ev_out

    def last(self, sem):
        return (sem, self.cnt.get(sem, 0))

    def replay(self, eng, engobj, sems):
        for item in self.sched[eng]:
            if item[0] == "wait":
                engobj.wait_ge(sems[item[1]], item[2])
            elif item[0] == "drain":
                engobj.drain()
            else:
                _, fn, inc, amt = item
                inst = fn(engobj)
                if inc is not None:
                    inst.then_inc(sems[inc], amt)


def build_program(seq, taps=False, pad_pe=0, twd=512):
    NTB = seq // TB
    NQB = seq // QB
    NSB = seq // KSUB
    TOK = seq // N_CORES
    TWD = twd
    NT_D = (TOK + TWD - 1) // TWD
    NTW = [min(TWD, TOK - TWD * i) for i in range(NT_D)]
    NTSB = TOK // 128
    NG = NSB // KGRP
    MSL = 6 * D // N_CORES

    nc = bass.Bass("TRN2", target_bir_lowering=False)

    xT = nc.dram_tensor("xT", [D, seq], F32, kind="ExternalInput")
    wqkv = nc.dram_tensor("wqkv", [D, 6 * HD], F32, kind="ExternalInput")
    wproj = nc.dram_tensor("wproj", [D, D], F32, kind="ExternalInput")
    wupT = nc.dram_tensor("wupT", [D, D], F32, kind="ExternalInput")
    wdownT = nc.dram_tensor("wdownT", [D, D], F32, kind="ExternalInput")
    wexp = nc.dram_tensor("wexp", [D, D], F32, kind="ExternalInput")
    wgT = nc.dram_tensor("wgT", [D, 8], F32, kind="ExternalInput")
    wmod = nc.dram_tensor("wmod", [D, MSL], F32, kind="ExternalInput")
    c_rd = nc.dram_tensor("c_r", [128, 16], F32, kind="ExternalInput")
    cosT = nc.dram_tensor("cosT", [128, seq], BF16, kind="ExternalInput")
    sinT = nc.dram_tensor("sinT", [128, seq], BF16, kind="ExternalInput")
    pTd = nc.dram_tensor("pT", [128, 128], BF16, kind="ExternalInput")
    cidxd = nc.dram_tensor("cidx_in", [128, NCH], U32, kind="ExternalInput")

    out = nc.dram_tensor("out", [NCH, 128, TOK], F32, kind="ExternalOutput")

    mod_in = nc.dram_tensor("mod_in", [MSL], F32)
    mod_out = nc.dram_tensor("mod_out", [6 * D], F32)
    ao_in0 = nc.dram_tensor("ao_in0", [128, seq], BF16)
    ao_in1 = nc.dram_tensor("ao_in1", [128, seq], BF16)
    ao_out0 = nc.dram_tensor("ao_out0", [N_CORES * 128, seq], BF16)
    ao_out1 = nc.dram_tensor("ao_out1", [N_CORES * 128, seq], BF16)
    g_bnc = nc.dram_tensor("g_bnc", [TOK], F32)

    if taps:
        htap = nc.dram_tensor("htap", [128, NCH * TB], BF16, kind="ExternalOutput")
        qtap = nc.dram_tensor("qtap", [128, TB], BF16, kind="ExternalOutput")
        ktap = nc.dram_tensor("ktap", [128, TB], BF16, kind="ExternalOutput")
        vtap = nc.dram_tensor("vtap", [128, 256], BF16, kind="ExternalOutput")
        atap = nc.dram_tensor("atap", [128, KGRP * QB], BF16, kind="ExternalOutput")
        otap = nc.dram_tensor("otap", [128, QB], BF16, kind="ExternalOutput")
        x1tap = nc.dram_tensor("x1tap", [128, TOK], F32, kind="ExternalOutput")
        h2tap = nc.dram_tensor("h2tap", [128, TOK], BF16, kind="ExternalOutput")
        gtap = nc.dram_tensor("gtap", [1, TOK], F32, kind="ExternalOutput")
        mtap = nc.dram_tensor("mtap", [128, 16], F32, kind="ExternalOutput")
        utap = nc.dram_tensor("utap", [128, 1], F32, kind="ExternalOutput")

    p = Plan()

    with contextlib.ExitStack() as st:
        def sb(name, shape, dt):
            return st.enter_context(nc.sbuf_tensor(name, shape, dt))

        qk = sb("qk", [128, 4 * seq], BF16)
        x1f = qk.ap().bitcast(F32)
        va = sb("va", [128, 2 * seq], BF16)
        big = sb("big", [128, 20480], BF16)
        wA = sb("wA", [128, 12288], BF16)
        xsl = sb("xsl", [128, 2 * TOK], F32)
        ob = sb("ob", [128, 2 * 1024], F32)

        c_sb = sb("c_sb", [128, 16], F32)
        silu_c = sb("silu_c", [128, 16], F32)
        m_sa = sb("m_sa", [128, 16], F32)
        m_ca1 = sb("m_ca1", [128, 16], F32)
        m_ga = sb("m_ga", [128, 16], F32)
        m_sb = sb("m_sb", [128, 16], F32)
        m_cb1 = sb("m_cb1", [128, 16], F32)
        m_gb = sb("m_gb", [128, 16], F32)
        mscr1 = sb("mscr1", [128, 16], F32)
        mscr2 = sb("mscr2", [128, 16], F32)
        mod_row = sb("mod_row", [1, MSL], F32)

        row_m = sb("row_m", [1, TB], F32)
        row_r = sb("row_r", [1, TB], F32)
        row_mr = sb("row_mr", [1, TB], F32)
        row_s1 = sb("row_s1", [1, TB], F32)
        row_s2 = sb("row_s2", [1, TB], F32)
        rsb = sb("rsb", [128, QB], F32)
        lnr = sb("lnr", [128, TB], F32)
        lnmr = sb("lnmr", [128, TB], F32)
        ones_b = sb("ones_b", [128, 1], BF16)
        ones_bb = sb("ones_bb", [128, 128], BF16)
        ones_c32 = sb("ones_c32", [128, 1], F32)
        ones_f = sb("ones_f", [1, 128], F32)
        pT_sb = sb("pT_sb", [128, 128], BF16)
        cidx = sb("cidx", [128, NCH], U32)
        mx1 = sb("mx1", [1, 1], F32)
        mx2 = sb("mx2", [1, 1], F32)
        sc1 = sb("sc1", [1, 1], F32)
        negU = sb("negU", [128, 1], F32)
        g_cols = sb("g_cols", [128, NTSB], F32)
        g_d = sb("g_d", [128, 8], F32)
        g_pen = sb("g_pen", [128, 8], F32)
        g_m1 = sb("g_m1", [128, 1], F32)
        g_m2 = sb("g_m2", [128, 1], F32)
        g_s2 = sb("g_s2", [128, 1], F32)
        g_inv = sb("g_inv", [128, 1], F32)
        g_row = sb("g_row", [1, TOK], F32)

        psA = st.enter_context(nc.psum_tensor("psA", [128, 1024], F32))
        psB = st.enter_context(nc.psum_tensor("psB", [128, 1024], F32))
        psC = st.enter_context(nc.psum_tensor("psC", [128, 512], F32))
        psD = st.enter_context(nc.psum_tensor("psD", [128, 512], F32))
        psE = st.enter_context(nc.psum_tensor("psE", [128, 512], F32))
        psF = st.enter_context(nc.psum_tensor("psF", [128, 512], F32))

        sems = {n: st.enter_context(nc.semaphore(n))
                for n in ("sINIT", "sW0", "sW1", "sW2", "sW3", "sW4", "sWG",
                          "sX0", "sX1", "sB0", "sB1", "sL0", "sL1", "sAO",
                          "sMD", "sGB", "sO0", "sO1", "sTP1", "sTP2",
                          "sCC", "sT", "sS", "sV")}
        block = st.enter_context(nc.Block())

        def mm(ps_ap, lhsT_ap, rhs_ap, start, stop, waits=(), inc=None):
            return p.emit("tensor",
                          lambda t, o=ps_ap, l=lhsT_ap, r=rhs_ap, s1=start, s2=stop:
                          t.matmul(o, l, r, start=s1, stop=s2),
                          waits=waits, inc=inc)

        last_tap = [None]

        def gdma(out_ap, in_ap, sem, waits=(), slow=False):
            waits = list(waits)
            if sem.startswith("sTP"):
                if last_tap[0] is not None:
                    waits.append(last_tap[0])
            ev = p.emit("gpsimd",
                        lambda g, o=out_ap, i=in_ap, sl=slow: g.dma_start(
                            out=o, in_=i, allow_slow_non_contiguous=sl),
                        waits=waits, inc=sem, amt=16)
            if sem.startswith("sTP"):
                last_tap[0] = ev
            return ev

        def ydma(out_ap, in_ap, sem, waits=()):
            return p.emit("sync",
                          lambda s, o=out_ap, i=in_ap: s.dma_start(out=o, in_=i),
                          waits=waits, inc=sem, amt=16)

        def vop(fn, rd=(), wr=(), waits=(), inc="sV"):
            return p.emit("vector", fn, reads=rd, writes=wr, waits=waits, inc=inc)

        def sop(fn, rd=(), wr=(), waits=(), inc="sS"):
            return p.emit("scalar", fn, reads=rd, writes=wr, waits=waits, inc=inc)

        vop(lambda v: v.memset(ones_b[:, :], 1.0), wr=("ones_b",))
        vop(lambda v: v.memset(ones_bb[:, :], 1.0), wr=("ones_bb",))
        vop(lambda v: v.memset(ones_c32[:, :], 1.0), wr=("ones_c32",))
        vop(lambda v: v.memset(ones_f[:, :], 1.0), wr=("ones_f",))
        vop(lambda v: v.memset(mx1[:, :], 0.0), wr=("mx1",))
        e_const = vop(lambda v: v.memset(mx2[:, :], 0.0), wr=("mx2",))
        gdma(pT_sb[:, :], pTd[:, :], 'sINIT')
        gdma(cidx[:, :], cidxd[:, :], 'sINIT')

        def xt_sl(b, ch, lo=0, w=TB):
            o = b * 8192 + ch * TB + lo
            return big[:, o:o + w]

        def tab_sl(b, is_sin):
            o = 16384 + b * 1024 + (512 if is_sin else 0)
            return big[:, o:o + TB]

        def sq_sl(s):
            return big[:, 18432 + s * 512: 18432 + (s + 1) * 512]

        def scr_sl(s):
            return big[:, 19456 + s * 512: 19456 + (s + 1) * 512]

        e_xt = [None] * (NTB + 2)
        e_tab = [None] * (NTB + 2)
        def r3(ap, cdim):
            return ap.rearrange("p (c t) -> p c t", c=cdim)

        e_xt[0] = gdma(r3(xt_sl(0, 0, 0, NCH * TB), NCH),
                       xT[:, 0:TB].rearrange("(c p) t -> p c t", p=128), "sX0")
        ydma(tab_sl(0, False), cosT[:, 0:TB], "sB0")
        e_tab[0] = ydma(tab_sl(0, True), sinT[:, 0:TB], "sB0")

        e_init = e_c = e_pt = e_cidx = gdma(c_sb[:, :], c_rd[:, :], 'sINIT')
        e_silu = sop(lambda s: s.activation(out=silu_c[:, :], in_=c_sb[:, :],
                                            func=AF.Silu),
                     rd=("c_sb",), wr=("silu_c",), waits=[e_c])
        NMT = MSL // 512
        mod_ps = [psA[0:1, 0:512], psA[0:1, 512:1024], psB[0:1, 0:512]]
        wAf = wA.ap().bitcast(F32)
        e_modmm = None
        e_wq_ev = None
        for qtr in range(4):
            wts = [e_wq_ev] if e_wq_ev is not None else []
            e_wq_ev = gdma(wAf[:, 0:4 * MSL].rearrange("p (c m) -> p c m", c=4),
                           wmod[qtr * 512:(qtr + 1) * 512, :].rearrange(
                               "(c p) m -> p c m", p=128),
                           "sW0", waits=wts)
            e_q = None
            for ch in range(4):
                gch = qtr * 4 + ch
                for ntl in range(NMT):
                    w = [e_wq_ev, e_silu] if (ch == 0 and ntl == 0) else ()
                    e_q = mm(mod_ps[ntl], silu_c[:, gch:gch + 1],
                             wAf[:, ch * MSL + ntl * 512: ch * MSL + (ntl + 1) * 512],
                             start=(gch == 0), stop=(gch == 15), waits=w,
                             inc="sT" if (ch == 3 and ntl == NMT - 1) else None)
            e_wq_ev = e_q
        e_modmm = e_q
        e_mr = None
        for ntl in range(NMT):
            e_mr = vop(lambda v, n=ntl: v.tensor_copy(
                out=mod_row[:, n * 512:(n + 1) * 512], in_=mod_ps[n]),
                wr=("mod_row",), waits=[e_modmm])
        e_mi = gdma(mod_in.ap(), mod_row[:, :], 'sW0', waits=[e_mr])
        e_ag1 = p.emit("gpsimd",
                       lambda g: g.collective_compute(
                           "AllGather", ALU.bypass,
                           replica_groups=[list(range(N_CORES))],
                           ins=[mod_in.ap().opt()], outs=[mod_out.ap().opt()]),
                       waits=[e_mi], inc="sCC")
        mtiles = [m_sa, m_ca1, m_ga, m_sb, m_cb1, m_gb]
        mtags = ["m_sa", "m_ca1", "m_ga", "m_sb", "m_cb1", "m_gb"]
        e_md_last = None
        for vi in range(6):
            e_md_last = gdma(mtiles[vi][:, :],
                             mod_out[vi * D:(vi + 1) * D].rearrange("(c p) -> p c", p=128),
                             "sMD", waits=[e_ag1], slow=True)
        e_md = [e_md_last] * 6
        vop(lambda v: v.tensor_scalar(out=m_ca1[:, :], in0=m_ca1[:, :], scalar1=1.0,
                                      scalar2=None, op0=ALU.add),
            rd=("m_ca1",), wr=("m_ca1",), waits=[e_md[1]])
        vop(lambda v: v.tensor_scalar(out=m_cb1[:, :], in0=m_cb1[:, :], scalar1=1.0,
                                      scalar2=None, op0=ALU.add),
            rd=("m_cb1",), wr=("m_cb1",), waits=[e_md[4]])
        for gt, tg, ei in ((m_ga, "m_ga", e_md[2]), (m_gb, "m_gb", e_md[5])):
            vop(lambda v, g=gt: v.tensor_scalar(out=mscr1[:, :], in0=g[:, :],
                                                scalar1=0.0, scalar2=None, op0=ALU.is_ge),
                rd=(tg,), wr=("mscr1",), waits=[ei])
            vop(lambda v, g=gt: v.tensor_scalar(out=mscr2[:, :], in0=g[:, :],
                                                scalar1=0.25, scalar2=None, op0=ALU.max),
                rd=(tg,), wr=("mscr2",))
            vop(lambda v, g=gt: v.tensor_scalar(out=g[:, :], in0=g[:, :],
                                                scalar1=-0.25, scalar2=None, op0=ALU.min),
                rd=(tg,), wr=(tg,))
            vop(lambda v, g=gt: v.tensor_tensor(out=mscr2[:, :], in0=mscr2[:, :],
                                                in1=g[:, :], op=ALU.subtract),
                rd=("mscr2", tg), wr=("mscr2",))
            vop(lambda v: v.tensor_tensor(out=mscr2[:, :], in0=mscr1[:, :],
                                          in1=mscr2[:, :], op=ALU.mult),
                rd=("mscr1", "mscr2"), wr=("mscr2",))
            vop(lambda v, g=gt: v.tensor_tensor(out=g[:, :], in0=g[:, :],
                                                in1=mscr2[:, :], op=ALU.add),
                rd=(tg, "mscr2"), wr=(tg,))
        e_mod_rdy = p.last("sV")
        if taps:
            gdma(mtap[:, :], m_ga[:, :], "sTP2", waits=[e_mod_rdy])

        if NTB > 1:
            e_xt[1] = gdma(r3(xt_sl(1, 0, 0, NCH * TB), NCH),
                           xT[:, TB:2 * TB].rearrange("(c p) t -> p c t", p=128),
                           "sX1")
            ydma(tab_sl(1, False), cosT[:, TB:2 * TB], "sB1")
            e_tab[1] = ydma(tab_sl(1, True), sinT[:, TB:2 * TB], "sB1")
        e_wqkv = gdma(r3(wA[:, 0:NCH * 768], NCH),
                      wqkv[:, :].rearrange("(c p) m -> p c m", p=128),
                      "sW0", waits=[e_modmm])

        WQ = [wA[:, ch * 768:ch * 768 + 256] for ch in range(NCH)]
        WK = [wA[:, ch * 768 + 256:ch * 768 + 512] for ch in range(NCH)]
        WV = [wA[:, ch * 768 + 512:ch * 768 + 768] for ch in range(NCH)]

        qk_ps = [psB[:, 0:512], psB[:, 512:1024], psC[:, :], psD[:, :]]
        prev_psread = [None] * 4
        prev_hlast = None
        prev_rowend = None
        prev_umax = None
        e_vdone_last = None
        e_hlast_final = None

        for i in range(NTB):
            b = i % 2
            e_smm_last = None
            e_smm_by = {}
            for ch in range(NCH):
                ssl = ch % 2
                wsq = []
                if ch < 2:
                    wsq = [e_xt[i], e_const]
                    if prev_umax is not None:
                        wsq.append(prev_umax)
                else:
                    wsq = [e_smm_by[ch - 2]]
                e_sq = sop(lambda s, c=ch, bb=b, sl=ssl: s.activation(
                    out=sq_sl(sl), in_=xt_sl(bb, c), func=AF.Square),
                    rd=(f"xt{b}_{ch}",), wr=(f"sq{ssl}",), waits=wsq)
                w0 = []
                if ch == 0:
                    w0 = [e_xt[i], e_const]
                    if prev_rowend is not None:
                        w0.append(prev_rowend)
                    if prev_umax is not None:
                        w0.append(prev_umax)
                mm(psE[:, :], ones_bb[:, :], xt_sl(b, ch),
                   start=(ch == 0), stop=(ch == NCH - 1), waits=w0)
                e_smm_last = e_smm_by[ch] = mm(psF[:, :], ones_bb[:, :], sq_sl(ssl),
                                               start=(ch == 0), stop=(ch == NCH - 1),
                                               waits=[e_sq], inc="sT")
            vop(lambda v: v.tensor_scalar(out=lnmr[:, :], in0=psE[:, :],
                                          scalar1=1.0 / D, scalar2=None, op0=ALU.mult),
                wr=("lnmr",), waits=[e_smm_last])
            vop(lambda v: v.tensor_tensor(out=lnr[:, :], in0=lnmr[:, :],
                                          in1=lnmr[:, :], op=ALU.mult),
                rd=("lnmr",), wr=("lnr",))
            vop(lambda v: v.scalar_tensor_tensor(
                out=lnr[:, :], in0=psF[:, :], scalar=1.0 / D, in1=lnr[:, :],
                op0=ALU.mult, op1=ALU.subtract),
                rd=("lnr",), wr=("lnr",))
            vop(lambda v: v.tensor_scalar(out=lnr[:, :], in0=lnr[:, :],
                                          scalar1=EPS, scalar2=None, op0=ALU.add),
                rd=("lnr",), wr=("lnr",))
            e_rcp = vop(lambda v: v.reciprocal(out=lnr[:, :], in_=lnr[:, :]),
                        rd=("lnr",), wr=("lnr",))
            prev_rowend = e_rcp
            e_sqrt = sop(lambda s: s.activation(out=lnr[:, :], in_=lnr[:, :],
                                                func=AF.Sqrt),
                         rd=("lnr",), wr=("lnr",), waits=[e_rcp])
            e_rows = vop(lambda v: v.scalar_tensor_tensor(
                out=lnmr[:, :], in0=lnmr[:, :], scalar=-1.0, in1=lnr[:, :],
                op0=ALU.mult, op1=ALU.mult),
                rd=("lnmr", "lnr"), wr=("lnmr",), waits=[e_sqrt])
            e_h = {}
            e_scr = {}
            for ch in range(NCH):
                ssl = ch % 2
                w0 = [e_rows]
                if ch >= 2:
                    w0.append(e_h[ch - 2])
                if i == 0 and ch == 0:
                    w0.append(e_mod_rdy)
                vop(lambda v, c=ch, bb=b, sl=ssl: v.tensor_tensor(
                    out=scr_sl(sl), in0=xt_sl(bb, c), in1=lnr[:, :], op=ALU.mult),
                    rd=(f"xt{b}_{ch}", "lnr"), wr=(f"scr{ssl}",), waits=w0)
                e_scr[ch] = vop(lambda v, sl=ssl: v.tensor_tensor(
                    out=scr_sl(sl), in0=scr_sl(sl), in1=lnmr[:, :], op=ALU.add),
                    rd=(f"scr{ssl}", "lnmr"), wr=(f"scr{ssl}",))
                wh = [e_scr[ch]]
                if i == 0 and ch == 0:
                    wh.append(e_mod_rdy)
                e_h[ch] = sop(lambda s, c=ch, bb=b, sl=ssl: s.activation(
                    out=xt_sl(bb, c), in_=scr_sl(sl), func=AF.Identity,
                    scale=m_ca1[:, c:c + 1], bias=m_sa[:, c:c + 1]),
                    rd=(f"scr{ssl}", "m_ca1", "m_sa"), wr=(f"xt{b}_{ch}",),
                    waits=wh)
            e_hlast_final = e_h[NCH - 1]
            if taps and i == 0:
                gdma(htap[:, :], xt_sl(b, 0, 0, NCH * TB), 'sTP2', waits=[e_h[NCH - 1]])
            e_qk_mm = [None] * 4
            for ch in range(NCH):
                for mt in range(4):
                    lhs = (WQ[ch][:, (mt % 2) * 128:(mt % 2 + 1) * 128] if mt < 2
                           else WK[ch][:, (mt - 2) * 128:(mt - 1) * 128])
                    w0 = []
                    if ch == 0:
                        w0 = [e_h[ch], e_wqkv]
                        if prev_psread[mt] is not None:
                            w0.append(prev_psread[mt])
                    else:
                        w0 = [e_h[ch]]
                    e_qk_mm[mt] = mm(qk_ps[mt], lhs, xt_sl(b, ch),
                                     start=(ch == 0), stop=(ch == NCH - 1),
                                     waits=w0,
                                     inc="sT" if ch == NCH - 1 else None)
            e_vdone = None
            for sbi in range(TB // KSUB):
                e_vmm = None
                for ch in range(NCH):
                    w0 = []
                    if ch == 0:
                        w0 = [e_rcp, e_h[NCH - 1]]
                        if e_vdone is not None:
                            w0.append(e_vdone)
                    e_vmm = mm(psF[:, 0:256],
                               xt_sl(b, ch, sbi * KSUB, KSUB), WV[ch],
                               start=(ch == 0), stop=(ch == NCH - 1),
                               waits=w0, inc="sT" if ch == NCH - 1 else None)
                gsb = i * (TB // KSUB) + sbi
                e_vdone = sop(lambda s, g=gsb: s.activation(
                    out=va[:, g * 256:(g + 1) * 256], in_=psF[:, 0:256],
                    func=AF.Identity),
                    wr=(f"v{gsb}",), waits=[e_vmm])
                if taps and gsb == 0:
                    gdma(vtap[:, :], va[:, 0:256], 'sTP2', waits=[e_vdone])
            e_vdone_last = e_vdone
            prev_r3 = None
            prev_mx = None
            for mt in range(4):
                h = mt % 2
                isq = mt < 2
                base = (0 if isq else 2 * seq) + h * seq + i * TB
                dst = qk[:, base:base + TB]
                tag = f"qk_m{mt}_{i}"
                wev = [e_qk_mm[mt], e_vdone]
                if prev_r3 is not None:
                    wev.append(prev_r3)
                e_ev = sop(lambda s, ps=qk_ps[mt]: s.activation(
                    out=scr_sl(1), in_=ps, func=AF.Identity),
                    wr=("scr1",), waits=wev)
                e_rot = mm(qk_ps[mt], pT_sb[:, :], scr_sl(1), start=True, stop=True,
                           waits=[e_ev, e_pt], inc="sT")
                vop(lambda v, d=dst, bb=b: v.tensor_tensor(
                    out=d, in0=scr_sl(1), in1=tab_sl(bb, False), op=ALU.mult),
                    rd=("scr1", f"tab{b}"), wr=(tag,),
                    waits=[e_rot, e_tab[i]])
                e_psr = vop(lambda v, ps=qk_ps[mt], bb=b: v.tensor_tensor(
                    out=scr_sl(1), in0=ps, in1=tab_sl(bb, True), op=ALU.mult),
                    rd=(f"tab{b}",), wr=("scr1",))
                prev_psread[mt] = e_psr
                e_r3 = vop(lambda v, d=dst: v.tensor_tensor(
                    out=d, in0=d, in1=scr_sl(1), op=ALU.add),
                    rd=(tag, "scr1"), wr=(tag,))
                prev_r3 = e_r3
                e_sq2 = vop(lambda v, d=dst: v.tensor_tensor(
                    out=sq_sl(0), in0=d, in1=d, op=ALU.mult),
                    rd=(tag,), wr=("sq0",), waits=[e_smm_last])
                wn = [e_sq2]
                if prev_mx is not None:
                    wn.append(prev_mx)
                else:
                    wn.append(e_rcp)
                    if prev_umax is not None:
                        wn.append(prev_umax)
                e_nmm = mm(psE[0:1, :], ones_b[:, :], sq_sl(0), start=True, stop=True,
                           waits=wn, inc="sT")
                mxt, mxn = (mx1, "mx1") if isq else (mx2, "mx2")
                vop(lambda v: v.reduce_max(out=sc1[:, :], in_=psE[0:1, :], axis=AX.X),
                    rd=(), wr=("sc1",), waits=[e_nmm])
                e_mx = vop(lambda v, m=mxt: v.tensor_tensor(
                    out=m[:, :], in0=m[:, :], in1=sc1[:, :], op=ALU.max),
                    rd=(mxn, "sc1"), wr=(mxn,))
                prev_mx = e_mx
                if taps and i == 0 and mt == 0:
                    gdma(qtap[:, :], dst, 'sTP2', waits=[e_r3])
                if taps and i == 0 and mt == 2:
                    gdma(ktap[:, :], dst, 'sTP2', waits=[e_r3])
            prev_umax = prev_mx
            j = i + 2
            if j < NTB:
                e_qkv_done = p.last("sT")
                e_xt[j] = gdma(r3(xt_sl(j % 2, 0, 0, NCH * TB), NCH),
                               xT[:, j * TB:(j + 1) * TB].rearrange(
                                   "(c p) t -> p c t", p=128),
                               "sX0" if j % 2 == 0 else "sX1",
                               waits=[e_qkv_done])
                ydma(tab_sl(j % 2, False), cosT[:, j * TB:(j + 1) * TB],
                     "sB0" if j % 2 == 0 else "sB1", waits=[prev_r3])
                e_tab[j] = ydma(tab_sl(j % 2, True), sinT[:, j * TB:(j + 1) * TB],
                                "sB0" if j % 2 == 0 else "sB1")

        e_u1 = vop(lambda v: v.tensor_tensor(out=sc1[:, :], in0=mx1[:, :],
                                             in1=mx2[:, :], op=ALU.mult),
                   rd=("mx1", "mx2"), wr=("sc1",), waits=[prev_umax])
        e_u2 = sop(lambda s: s.activation(out=sc1[:, :], in_=sc1[:, :], func=AF.Sqrt),
                   rd=("sc1",), wr=("sc1",), waits=[e_u1])
        e_u3 = vop(lambda v: v.tensor_scalar(out=sc1[:, :], in0=sc1[:, :],
                                             scalar1=-INV_SQRT_HD, scalar2=None,
                                             op0=ALU.mult),
                   rd=("sc1",), wr=("sc1",), waits=[e_u2])
        e_u5 = mm(psF[:, 0:1], ones_f[:, :], sc1[:, :], start=True, stop=True,
                  waits=[e_u3, e_vdone_last], inc="sT")
        e_negU = vop(lambda v: v.tensor_copy(out=negU[:, :], in_=psF[:, 0:1]),
                     wr=("negU",), waits=[e_u5])
        if taps:
            gdma(utap[:, :], negU[:, :], 'sTP2', waits=[e_negU])

        e_taps_preB = p.last("sTP2") if taps else None

        RING = 5

        def wslot(s, ch=None, w=128):
            if ch is None:
                return wA[:, s * 2048:(s + 1) * 2048]
            return wA[:, s * 2048 + ch * w: s * 2048 + (ch + 1) * w]

        wsets = ([("proj", m) for m in range(NCH)]
                 + [("up", m) for m in range(NCH)])
        for m in range(NCH):
            wsets.append(("down", m))
            wsets.append(("exp", m))
        wsrc = {"proj": wproj, "up": wupT, "down": wdownT, "exp": wexp}
        slot_free = [None] * RING
        e_wset = [None] * len(wsets)
        set_slot = {}
        e_qkv_all_done = p.last("sT")
        e_wgate = gdma(r3(wA[:, 10240:10240 + NCH * 8], NCH),
                       wgT[:, :].rearrange("(c p) m -> p c m", p=128),
                       "sWG", waits=[e_qkv_all_done])

        def emit_wset_dma(si):
            kind, m = wsets[si]
            s = si % RING
            set_slot[si] = s
            wts = []
            if si < RING:
                wts.append(e_qkv_all_done)
            if slot_free[s] is not None:
                wts.append(slot_free[s])
            src = wsrc[kind][:, m * 128:(m + 1) * 128].rearrange(
                "(c p) m -> p c m", p=128)
            e_wset[si] = gdma(r3(wslot(s), NCH), src, f"sW{s}", wts)

        n_wdma = 0
        for _ in range(min(RING, len(wsets))):
            emit_wset_dma(n_wdma)
            n_wdma += 1

        def a_sl(s, lo=0, w=KGRP * QB):
            o = 16384 + s * 1024 + lo
            return big[:, o:o + w]

        def ao_loc(ch, lo=0, w=None):
            if w is None:
                w = TOK
            return big[:, ch * TOK + lo: ch * TOK + lo + w]

        e_aog = [None] * NCH
        ao_r0 = ao_out0.ap().rearrange("d (k t) -> (d k) t", k=N_CORES)
        ao_r1 = ao_out1.ap().rearrange("d (k t) -> (d k) t", k=N_CORES)

        e_ao = {}
        prev_exp2 = [None, None]
        prev_grp2 = [None, None]
        prev_av_evt = {0: None, 1: None}
        prev_rs2 = [None, None]
        it = 0
        for h in range(HP):
            for qb in range(NQB):
                par = it % 2
                avbk = psC if par == 0 else psD
                smbk = psE if par == 0 else psF
                e_exp = {}
                e_grp = {}
                for g in range(NG):
                    sps = psA if g % 2 == 0 else psB
                    wts = []
                    if g >= 2:
                        wts.append(e_exp[g - 2])
                    else:
                        if prev_exp2[g % 2] is not None:
                            wts.append(prev_exp2[g % 2])
                        if it == 0:
                            wts.append(e_hlast_final)
                            wts.append(e_mr)
                            for ev in prev_psread:
                                if ev is not None:
                                    wts.append(ev)
                    e_s2 = None
                    for kk in range(KGRP):
                        kb = g * KGRP + kk
                        lhs = qk[:, 2 * seq + h * seq + kb * KSUB:
                                 2 * seq + h * seq + (kb + 1) * KSUB]
                        rhs = qk[:, h * seq + qb * QB: h * seq + (qb + 1) * QB]
                        e_s2 = mm(sps[:, kk * 512:(kk + 1) * 512], lhs, rhs,
                                  start=True, stop=True,
                                  waits=wts if kk == 0 else (),
                                  inc="sT" if kk == KGRP - 1 else None)
                    wts2 = [e_s2, e_negU]
                    if g >= 2:
                        wts2.append(e_grp[g - 2])
                    elif prev_grp2[g % 2] is not None:
                        wts2.append(prev_grp2[g % 2])
                    if it == 0 and g < 2:
                        wts2.append(prev_r3)
                    if taps and it == 0 and g == 2:
                        wts2.append(e_atap)
                    e_exp[g] = sop(lambda s, sl=g % 2, ps=sps: s.activation(
                        out=a_sl(sl), in_=ps[:, 0:KGRP * 512], func=AF.Exp,
                        scale=INV_SQRT_HD, bias=negU[:, 0:1]),
                        rd=("negU",), waits=wts2)
                    for kk in range(KGRP):
                        kb = g * KGRP + kk
                        wv = []
                        if kk == 0:
                            wv = [e_exp[g]]
                            if g == 0:
                                if prev_av_evt[par] is not None:
                                    wv.append(prev_av_evt[par])
                                if it == 0:
                                    wv.append(e_negU)
                                if it == 1:
                                    wv.append(e_negU)
                        mm(avbk[:, :],
                           va[:, kb * 256 + h * 128: kb * 256 + (h + 1) * 128],
                           a_sl(g % 2, kk * 512, 512),
                           start=(kb == 0), stop=(kb == NSB - 1), waits=wv)
                        wg = []
                        if kb == 0 and prev_rs2[par] is not None:
                            wg = [prev_rs2[par]]
                        e_grp[g] = mm(smbk[:, :], ones_bb[:, :],
                                      a_sl(g % 2, kk * 512, 512),
                                      start=(kb == 0), stop=(kb == NSB - 1),
                                      waits=wg,
                                      inc="sT" if kk == KGRP - 1 else None)
                    if taps and it == 0 and g == 0:
                        e_atap = gdma(atap[:, :], a_sl(0), 'sTP1', waits=[e_grp[0]])
                prev_exp2 = [e_exp[NG - 2], e_exp[NG - 1]]
                prev_grp2 = [e_grp[NG - 2], e_grp[NG - 1]]
                e_rs = vop(lambda v, sk=smbk: v.reciprocal(out=rsb[:, :],
                                                           in_=sk[:, :]),
                           rd=(), wr=("rsb",), waits=[e_grp[NG - 1]])
                prev_rs2[par] = e_rs
                wao = []
                if taps and it == 0 and e_taps_preB is not None:
                    wao.append(e_taps_preB)
                e_ao[(h, qb)] = vop(lambda v, hh=h, q=qb, ab=avbk: v.tensor_tensor(
                    out=big[:, hh * seq + q * QB: hh * seq + (q + 1) * QB],
                    in0=ab[:, :], in1=rsb[:, :], op=ALU.mult),
                    rd=("rsb",), wr=(f"ao{h}_{qb}",), waits=wao)
                prev_av_evt[par] = e_ao[(h, qb)]
                if taps and h == 0 and qb == 0:
                    gdma(otap[:, :], big[:, 0:QB], 'sTP2', waits=[e_ao[(0, 0)]])
                it += 1
            if h == 0:
                e_aow0 = gdma(ao_in0.ap(), big[:, 0:seq], "sAO",
                              waits=[e_ao[(0, q)] for q in range(NQB)])
                e_ag1 = p.emit("gpsimd",
                               lambda g: g.collective_compute(
                                   "AllGather", ALU.bypass,
                                   replica_groups=[list(range(N_CORES))],
                                   ins=[ao_in0.ap().opt()],
                                   outs=[ao_out0.ap().opt()]),
                               waits=[e_aow0], inc="sCC")
                for ch in (0, 2, 4, 6):
                    e_aog[ch] = p.emit(
                        "gpsimd",
                        lambda g, c=ch: g.indirect_dma_start(
                            out=ao_loc(c), out_offset=None, in_=ao_r0,
                            in_offset=IndirectOffsetOnAxis(
                                ap=cidx[:, c // 2:c // 2 + 1], axis=0)),
                        waits=[e_ag1, e_init] if ch == 0 else [e_ag1],
                        inc="sAO", amt=16)
            else:
                e_aow1 = gdma(ao_in1.ap(), big[:, seq:2 * seq], "sAO",
                              waits=[e_ao[(1, q)] for q in range(NQB)])
                for ch in (8, 10, 12, 14):
                    e_aog[ch] = p.emit(
                        "gpsimd",
                        lambda g, c=ch: g.indirect_dma_start(
                            out=ao_loc(c), out_offset=None, in_=ao_r0,
                            in_offset=IndirectOffsetOnAxis(
                                ap=cidx[:, c // 2:c // 2 + 1], axis=0)),
                        waits=[e_aow1, e_ag1], inc="sAO", amt=16)
                e_ag2 = p.emit("gpsimd",
                               lambda g: g.collective_compute(
                                   "AllGather", ALU.bypass,
                                   replica_groups=[list(range(N_CORES))],
                                   ins=[ao_in1.ap().opt()],
                                   outs=[ao_out1.ap().opt()]),
                               waits=[e_aow1], inc="sCC")

        xT_r = xT.ap().rearrange("d (k t) -> (d k) t", k=N_CORES)
        e_xsl = [None] * NCH
        for m in range(2):
            e_xsl[m] = p.emit(
                "gpsimd",
                lambda g, mm_=m: g.indirect_dma_start(
                    out=xsl[:, (mm_ % 2) * TOK:(mm_ % 2) * TOK + TOK],
                    out_offset=None, in_=xT_r,
                    in_offset=IndirectOffsetOnAxis(ap=cidx[:, mm_:mm_ + 1], axis=0)),
                waits=[e_init] if m == 0 else (),
                inc="sL0" if m % 2 == 0 else "sL1", amt=16)

        e_taps_preD = p.last("sTP2") if taps else None
        for ch in (1, 3, 5, 7, 9, 11, 13, 15):
            w0 = [e_ag2]
            if taps and ch == 1 and e_taps_preD is not None:
                w0.append(e_taps_preD)
            e_aog[ch] = p.emit(
                "gpsimd",
                lambda g, c=ch: g.indirect_dma_start(
                    out=ao_loc(c), out_offset=None, in_=ao_r1,
                    in_offset=IndirectOffsetOnAxis(
                        ap=cidx[:, c // 2:c // 2 + 1], axis=0)),
                waits=w0, inc="sAO", amt=16)

        rot_ps = [psA[:, 0:512], psA[:, 512:1024], psB[:, 0:512], psB[:, 512:1024]]
        rot_free = [prev_exp2[0], prev_exp2[0], prev_exp2[1], prev_exp2[1]]
        e_pd_start = [prev_exp2[0], prev_exp2[1]]

        proj_ch_order = [0, 2, 4, 6, 8, 10, 12, 14, 1, 3, 5, 7, 9, 11, 13, 15]
        si = 0
        e_x1 = {}
        ri = 0
        for m in range(NCH):
            for nt in range(NT_D):
                w = NTW[nt]
                pp = rot_ps[ri % 4]
                e_p = None
                for ci, ch in enumerate(proj_ch_order):
                    wts = []
                    if ci == 0:
                        wts = [e_wset[si], e_aog[ch]]
                        if rot_free[ri % 4] is not None:
                            wts.append(rot_free[ri % 4])
                    else:
                        wts = [e_aog[ch]]
                    e_p = mm(pp[:, 0:w], wslot(set_slot[si], ch),
                             ao_loc(ch, nt * TWD, w),
                             start=(ci == 0), stop=(ci == NCH - 1),
                             waits=wts, inc="sT" if ci == NCH - 1 else None)
                wts = [e_p, e_xsl[m], e_mod_rdy] + [ev for ev in e_pd_start if ev]
                if taps and e_taps_preD is not None:
                    wts.append(e_taps_preD)
                e_x1[(m, nt)] = vop(
                    lambda v, mm_=m, n=nt, ppp=pp, ww=w: v.scalar_tensor_tensor(
                        out=x1f[:, mm_ * TOK + n * TWD: mm_ * TOK + n * TWD + ww],
                        in0=ppp[:, 0:ww], scalar=m_ga[:, mm_:mm_ + 1],
                        in1=xsl[:, (mm_ % 2) * TOK + n * TWD:
                                (mm_ % 2) * TOK + n * TWD + ww],
                        op0=ALU.mult, op1=ALU.add),
                    rd=("m_ga", f"xsl{m % 2}"), wr=(f"x1_{m}",),
                    waits=wts)
                rot_free[ri % 4] = e_x1[(m, nt)]
                ri += 1
            if m + 2 < NCH:
                e_xsl[m + 2] = p.emit(
                    "gpsimd",
                    lambda g, mm_=m + 2: g.indirect_dma_start(
                        out=xsl[:, (mm_ % 2) * TOK:(mm_ % 2) * TOK + TOK],
                        out_offset=None, in_=xT_r,
                        in_offset=IndirectOffsetOnAxis(ap=cidx[:, mm_:mm_ + 1],
                                                       axis=0)),
                    waits=[e_x1[(m, NT_D - 1)]],
                    inc="sL0" if m % 2 == 0 else "sL1", amt=16)
            slot_free[set_slot[si]] = e_x1[(m, NT_D - 1)]
            si += 1
            if n_wdma < len(wsets):
                emit_wset_dma(n_wdma)
                n_wdma += 1
        e_proj_done = p.last("sT")
        if taps:
            gdma(x1tap[:, :], x1f[:, 0:TOK], 'sTP2', waits=[e_x1[(0, NT_D - 1)]])

        prev_rowend2 = None
        e_h2 = {}
        e_h2_last = None
        for nt in range(NT_D):
            w = NTW[nt]
            e_smm2 = None
            e_smm2_by = {}
            for ch in range(NCH):
                ssl = ch % 2
                wsq = [e_x1[(ch, nt)]]
                if ch >= 2:
                    wsq.append(e_smm2_by[ch - 2])
                e_sq = vop(lambda v, c=ch, n=nt, s=ssl, ww=w: v.tensor_tensor(
                    out=sq_sl(s)[:, 0:ww],
                    in0=x1f[:, c * TOK + n * TWD: c * TOK + n * TWD + ww],
                    in1=x1f[:, c * TOK + n * TWD: c * TOK + n * TWD + ww],
                    op=ALU.mult),
                    rd=(f"x1_{ch}",), wr=(f"sq{ssl}",),
                    waits=wsq)
                wts = []
                if ch == 0:
                    wts = [e_x1[(ch, nt)]]
                    if prev_rowend2 is not None:
                        wts.append(prev_rowend2)
                    else:
                        wts.append(e_ao[(HP - 1, NQB - 1)])
                        wts.append(e_ao[(HP - 1, NQB - 2)])
                        wts.append(e_negU)
                mm(psE[0:1, 0:w], ones_c32[:, :],
                   x1f[:, ch * TOK + nt * TWD: ch * TOK + nt * TWD + w],
                   start=(ch == 0), stop=(ch == NCH - 1), waits=wts)
                e_smm2 = e_smm2_by[ch] = mm(psF[0:1, 0:w], ones_b[:, :],
                                            sq_sl(ssl)[:, 0:w],
                                            start=(ch == 0), stop=(ch == NCH - 1),
                                            waits=[e_sq], inc="sT")
            vop(lambda v, ww=w: v.tensor_scalar(out=row_m[:, 0:ww], in0=psE[0:1, 0:ww],
                                                scalar1=1.0 / D, scalar2=None,
                                                op0=ALU.mult),
                wr=("row_m",), waits=[e_smm2])
            vop(lambda v, ww=w: v.tensor_scalar(out=row_s1[:, 0:ww], in0=psF[0:1, 0:ww],
                                                scalar1=1.0 / D, scalar2=None,
                                                op0=ALU.mult),
                wr=("row_s1",))
            vop(lambda v, ww=w: v.tensor_tensor(out=row_s2[:, 0:ww], in0=row_m[:, 0:ww],
                                                in1=row_m[:, 0:ww], op=ALU.mult),
                rd=("row_m",), wr=("row_s2",))
            vop(lambda v, ww=w: v.tensor_tensor(out=row_s1[:, 0:ww], in0=row_s1[:, 0:ww],
                                                in1=row_s2[:, 0:ww], op=ALU.subtract),
                rd=("row_s1", "row_s2"), wr=("row_s1",))
            vop(lambda v, ww=w: v.tensor_scalar(out=row_s1[:, 0:ww], in0=row_s1[:, 0:ww],
                                                scalar1=EPS, scalar2=None, op0=ALU.add),
                rd=("row_s1",), wr=("row_s1",))
            e_rcp2 = vop(lambda v, ww=w: v.reciprocal(out=row_s2[:, 0:ww],
                                                      in_=row_s1[:, 0:ww]),
                         rd=("row_s1",), wr=("row_s2",))
            prev_rowend2 = e_rcp2
            e_sqrt2 = sop(lambda s, ww=w: s.activation(out=row_r[:, 0:ww],
                                                       in_=row_s2[:, 0:ww],
                                                       func=AF.Sqrt),
                          rd=("row_s2",), wr=("row_r",), waits=[e_rcp2])
            e_rows2 = vop(lambda v, ww=w: v.scalar_tensor_tensor(
                out=row_mr[:, 0:ww], in0=row_m[:, 0:ww], scalar=-1.0,
                in1=row_r[:, 0:ww], op0=ALU.mult, op1=ALU.mult),
                rd=("row_m", "row_r"), wr=("row_mr",), waits=[e_sqrt2])
            wbc = [e_rows2, e_proj_done]
            if e_h2_last is not None:
                wbc.append(e_h2_last)
            mm(psA[:, 0:512][:, 0:w], ones_f[:, :], row_r[0:1, 0:w],
               start=True, stop=True, waits=wbc)
            e_bc2 = mm(psA[:, 512:1024][:, 0:w], ones_f[:, :], row_mr[0:1, 0:w],
                       start=True, stop=True, inc="sT")
            for ch in range(NCH):
                vop(lambda v, c=ch, n=nt, ww=w: v.tensor_tensor(
                    out=scr_sl(0)[:, 0:ww],
                    in0=x1f[:, c * TOK + n * TWD: c * TOK + n * TWD + ww],
                    in1=psA[:, 0:ww], op=ALU.mult),
                    rd=(f"x1_{ch}",), wr=("scr0",),
                    waits=[e_bc2] if ch == 0 else ())
                vop(lambda v, ww=w: v.tensor_tensor(
                    out=scr_sl(0)[:, 0:ww], in0=scr_sl(0)[:, 0:ww],
                    in1=psA[:, 512:512 + ww], op=ALU.add),
                    rd=("scr0",), wr=("scr0",))
                e_h2[(ch, nt)] = vop(lambda v, c=ch, n=nt, ww=w: v.tensor_scalar(
                    out=va[:, c * TOK + n * TWD: c * TOK + n * TWD + ww],
                    in0=scr_sl(0)[:, 0:ww],
                    scalar1=m_cb1[:, c:c + 1], scalar2=m_sb[:, c:c + 1],
                    op0=ALU.mult, op1=ALU.add),
                    rd=("scr0", "m_cb1", "m_sb"), wr=(f"h2_{ch}_{nt}",))
                e_h2_last = e_h2[(ch, nt)]
        e_h2tap = None
        if taps:
            e_h2tap = gdma(h2tap[:, :], va[:, 0:TOK], 'sTP2', waits=[e_h2_last])

        e_gate_vec = None
        for s8 in range(NTSB):
            e_gmm = None
            for ch in range(NCH):
                wts = []
                if ch == 0:
                    wts = [e_wgate, e_h2[(ch, ((s8 + 1) * 128 - 1) // TWD)]]
                    if e_gate_vec is not None:
                        wts.append(e_gate_vec)
                    else:
                        wts.append(prev_av_evt[0])
                else:
                    wts = [e_h2[(ch, ((s8 + 1) * 128 - 1) // TWD)]]
                e_gmm = mm(psC[:, 0:8],
                           va[:, ch * TOK + s8 * 128: ch * TOK + (s8 + 1) * 128],
                           wA[:, 10240 + ch * 8: 10240 + (ch + 1) * 8],
                           start=(ch == 0), stop=(ch == NCH - 1),
                           waits=wts, inc="sT" if ch == NCH - 1 else None)
            vop(lambda v: v.reduce_max(out=g_m1[:, :], in_=psC[:, 0:8], axis=AX.X),
                wr=("g_m1",), waits=[e_gmm])
            vop(lambda v: v.tensor_scalar(out=g_d[:, :], in0=psC[:, 0:8],
                                          scalar1=g_m1[:, 0:1], scalar2=None,
                                          op0=ALU.subtract),
                rd=("g_m1",), wr=("g_d",))
            vop(lambda v: v.tensor_scalar(out=g_pen[:, :], in0=g_d[:, :],
                                          scalar1=0.0, scalar2=-1e30,
                                          op0=ALU.is_equal, op1=ALU.mult),
                rd=("g_d",), wr=("g_pen",))
            vop(lambda v: v.tensor_tensor(out=g_pen[:, :], in0=g_pen[:, :],
                                          in1=g_d[:, :], op=ALU.add),
                rd=("g_pen", "g_d"), wr=("g_pen",))
            e_gm2 = vop(lambda v: v.reduce_max(out=g_m2[:, :], in_=g_pen[:, :],
                                               axis=AX.X),
                        rd=("g_pen",), wr=("g_m2",))
            e_gexp = sop(lambda s: s.activation(out=g_s2[:, :], in_=g_m2[:, :],
                                                func=AF.Exp),
                         rd=("g_m2",), wr=("g_s2",), waits=[e_gm2])
            vop(lambda v: v.tensor_scalar(out=g_inv[:, :], in0=g_s2[:, :],
                                          scalar1=1.0, scalar2=None, op0=ALU.add),
                rd=("g_s2",), wr=("g_inv",), waits=[e_gexp])
            vop(lambda v: v.reciprocal(out=g_inv[:, :], in_=g_inv[:, :]),
                rd=("g_inv",), wr=("g_inv",))
            vop(lambda v: v.tensor_tensor(out=g_m2[:, :], in0=g_s2[:, :],
                                          in1=g_inv[:, :], op=ALU.mult),
                rd=("g_s2", "g_inv"), wr=("g_m2",))
            e_gate_vec = vop(lambda v, s8_=s8: v.tensor_tensor(
                out=g_cols[:, s8_:s8_ + 1], in0=g_inv[:, :], in1=g_m2[:, :],
                op=ALU.add),
                rd=("g_inv", "g_m2"), wr=("g_cols",))
        e_gb1 = gdma(g_bnc.ap().rearrange("(s p) -> p s", p=128),
                     g_cols[:, 0:NTSB], "sGB", waits=[e_gate_vec], slow=True)
        e_gb2 = gdma(g_row[0:1, :], g_bnc.ap(), 'sGB', waits=[e_gb1])
        if taps:
            gdma(gtap[:, :], g_row[0:1, :], 'sTP2', waits=[e_gb2])

        up0 = NCH
        e_relu = {}
        ri = 0
        rot_free2 = [e_bc2, e_bc2, None, None]
        last_h2_read = e_h2_last
        for m in range(NCH):
            si2 = up0 + m
            for nt in range(NT_D):
                w = NTW[nt]
                pp = rot_ps[ri % 4]
                e_p = None
                for ch in range(NCH):
                    wts = [e_h2[(ch, nt)]]
                    if ch == 0:
                        wts.append(e_wset[si2])
                        if rot_free2[ri % 4] is not None:
                            wts.append(rot_free2[ri % 4])
                        if ri % 4 < 2:
                            wts.append(last_h2_read)
                    e_p = mm(pp[:, 0:w], wslot(set_slot[si2], ch),
                             va[:, ch * TOK + nt * TWD: ch * TOK + nt * TWD + w],
                             start=(ch == 0), stop=(ch == NCH - 1),
                             waits=wts, inc="sT" if ch == NCH - 1 else None)
                e_relu[(m, nt)] = sop(lambda s, mm_=m, n=nt, ppp=pp, ww=w: s.activation(
                    out=big[:, mm_ * TOK + n * TWD: mm_ * TOK + n * TWD + ww],
                    in_=ppp[:, 0:ww], func=AF.Relu),
                    wr=(f"relu{m}_{nt}",), waits=[e_p, e_proj_done])
                rot_free2[ri % 4] = e_relu[(m, nt)]
                ri += 1
            slot_free[set_slot[si2]] = e_relu[(m, NT_D - 1)]
            if n_wdma < len(wsets):
                emit_wset_dma(n_wdma)
                n_wdma += 1
        e_up_done = p.last("sT")

        e_h2g = {}
        e_h2g_last = None
        for nt in range(NT_D):
            w = NTW[nt]
            wgbd = [e_gb2, e_up_done, prev_rowend2]
            if e_h2g_last is not None:
                wgbd.append(e_h2g_last)
            if e_h2tap is not None:
                wgbd.append(e_h2tap)
            e_gbd = mm(psE[:, 0:w], ones_f[:, :], g_row[0:1, nt * TWD: nt * TWD + w],
                       start=True, stop=True, waits=wgbd, inc="sT")
            for ch in range(NCH):
                e_h2g[(ch, nt)] = vop(lambda v, c=ch, n=nt, ww=w: v.tensor_tensor(
                    out=va[:, c * TOK + n * TWD: c * TOK + n * TWD + ww],
                    in0=va[:, c * TOK + n * TWD: c * TOK + n * TWD + ww],
                    in1=psE[:, 0:ww], op=ALU.mult),
                    rd=(f"h2_{ch}_{nt}",), wr=(f"h2_{ch}_{nt}",),
                    waits=[e_gbd] if ch == 0 else ())
                e_h2g_last = e_h2g[(ch, nt)]

        dn0 = up0 + NCH
        ri = 0
        rot_free3 = [rot_free2[0], rot_free2[1], rot_free2[2], rot_free2[3]]
        e_st = {}
        for m in range(NCH):
            si_d = dn0 + 2 * m
            si_e = dn0 + 2 * m + 1
            for nt in range(NT_D):
                w = NTW[nt]
                pp = rot_ps[ri % 4]
                for r in range(NCH):
                    wts = [e_relu[(r, nt)]]
                    if r == 0:
                        wts.append(e_wset[si_d])
                        if rot_free3[ri % 4] is not None:
                            wts.append(rot_free3[ri % 4])
                    mm(pp[:, 0:w], wslot(set_slot[si_d], r),
                       big[:, r * TOK + nt * TWD: r * TOK + nt * TWD + w],
                       start=(r == 0), stop=False, waits=wts)
                e_p = None
                for ch in range(NCH):
                    wts = [e_h2g[(ch, nt)]]
                    if ch == 0:
                        wts.append(e_wset[si_e])
                    e_p = mm(pp[:, 0:w], wslot(set_slot[si_e], ch),
                             va[:, ch * TOK + nt * TWD: ch * TOK + nt * TWD + w],
                             start=False, stop=(ch == NCH - 1),
                             waits=wts, inc="sT" if ch == NCH - 1 else None)
                bsl = (m * NT_D + nt) % 2
                wts = [e_p]
                if (m * NT_D + nt) >= 2:
                    wts.append(e_st[(m * NT_D + nt) - 2])
                e_ob = vop(lambda v, mm_=m, n=nt, ppp=pp, ww=w, bs=bsl:
                           v.scalar_tensor_tensor(
                               out=ob[:, bs * 1024: bs * 1024 + ww],
                               in0=ppp[:, 0:ww], scalar=m_gb[:, mm_:mm_ + 1],
                               in1=x1f[:, mm_ * TOK + n * TWD:
                                       mm_ * TOK + n * TWD + ww],
                               op0=ALU.mult, op1=ALU.add),
                           rd=("m_gb", f"x1_{m}"), wr=(f"ob{bsl}",), waits=wts)
                e_st[m * NT_D + nt] = ydma(out[m, :, nt * TWD: nt * TWD + w],
                                           ob[:, bsl * 1024: bsl * 1024 + w],
                                           "sO0" if bsl == 0 else "sO1",
                                           waits=[e_ob])
                rot_free3[ri % 4] = e_ob
                ri += 1
            slot_free[set_slot[si_d]] = e_st[m * NT_D + NT_D - 1]
            slot_free[set_slot[si_e]] = e_st[m * NT_D + NT_D - 1]
            for _ in range(2):
                if n_wdma < len(wsets):
                    emit_wset_dma(n_wdma)
                    n_wdma += 1

        for di in range(pad_pe):
            mm(psF[0:1, 0:512], ones_b[:, :], big[:, 0:512],
               start=True, stop=True,
               waits=[p.last("sO0"), p.last("sO1")] if di == 0 else ())
        p.emit("sync", lambda s: s.nop(),
               waits=[p.last("sO0"), p.last("sO1")])
        p.emit("gpsimd", lambda g: g.nop(),
               waits=[p.last("sO0"), p.last("sO1")])

        @block.tensor
        def _(t):
            p.replay("tensor", t, sems)

        @block.scalar
        def _(s):
            p.replay("scalar", s, sems)

        @block.vector
        def _(v):
            p.replay("vector", v, sems)

        @block.gpsimd
        def _(g):
            p.replay("gpsimd", g, sems)

        @block.sync
        def _(s):
            p.replay("sync", s, sems)

    return nc


def make_inputs(seq, x, w_qkv, w_proj, W_up, W_down, W_gate, W_expert, c, wmod_full):
    TOK = seq // N_CORES
    MSL = 6 * D // N_CORES
    xT = np.ascontiguousarray(x.reshape(seq, D).T)
    wq3 = w_qkv.reshape(D, 3, 16, HD)
    theta = np.float32(10000.0)
    freqs = np.float32(1.0) / theta ** (np.arange(0, HD, 2, dtype=np.float32)
                                        / np.float32(HD))
    t = np.arange(seq, dtype=np.float32)
    f = np.outer(t, freqs)
    sin = np.repeat(np.sin(f), 2, axis=1)
    cos = np.repeat(np.cos(f), 2, axis=1)
    cosT = np.ascontiguousarray(cos.T).astype(ml_dtypes.bfloat16)
    sinT = np.ascontiguousarray(sin.T).astype(ml_dtypes.bfloat16)
    pT = np.zeros((128, 128), dtype=ml_dtypes.bfloat16)
    for i in range(64):
        pT[2 * i + 1, 2 * i] = -1.0
        pT[2 * i, 2 * i + 1] = 1.0
    c_r = np.ascontiguousarray(c.reshape(16, 128).T)
    wupT = np.ascontiguousarray(W_up.T)
    wdownT = np.ascontiguousarray(W_down.T)
    wgT = np.ascontiguousarray(W_gate.T)
    in_maps = []
    for j in range(N_CORES):
        hsel = [2 * j, 2 * j + 1]
        wqkv_j = np.ascontiguousarray(np.concatenate(
            [wq3[:, 0, hsel, :].reshape(D, 2 * HD),
             wq3[:, 1, hsel, :].reshape(D, 2 * HD),
             wq3[:, 2, hsel, :].reshape(D, 2 * HD)], axis=1))
        wmod_j = np.ascontiguousarray(wmod_full[:, j * MSL:(j + 1) * MSL])
        cidx = np.zeros((128, NCH), dtype=np.uint32)
        for ch in range(NCH):
            cidx[:, ch] = (128 * ch + np.arange(128)) * N_CORES + j
        in_maps.append({
            "xT": xT, "wqkv": wqkv_j, "wproj": w_proj, "wupT": wupT,
            "wdownT": wdownT, "wexp": W_expert, "wgT": wgT, "wmod": wmod_j,
            "c_r": c_r, "cosT": cosT, "sinT": sinT, "pT": pT, "cidx_in": cidx,
        })
    return in_maps


def kernel(L0_attn_x, L0_attn_w_qkv, L0_attn_w_proj, L0_ffn_W_up, L0_ffn_W_down,
           L0_ffn_W_gate, L0_ffn_W_expert, L0_c, L0_wmod):
    seq = 8192
    TOK = seq // N_CORES
    x = np.asarray(L0_attn_x, dtype=np.float32)
    in_maps = make_inputs(seq, x,
                          np.asarray(L0_attn_w_qkv, dtype=np.float32),
                          np.asarray(L0_attn_w_proj, dtype=np.float32),
                          np.asarray(L0_ffn_W_up, dtype=np.float32),
                          np.asarray(L0_ffn_W_down, dtype=np.float32),
                          np.asarray(L0_ffn_W_gate, dtype=np.float32),
                          np.asarray(L0_ffn_W_expert, dtype=np.float32),
                          np.asarray(L0_c, dtype=np.float32),
                          np.asarray(L0_wmod, dtype=np.float32))
    nc = build_program(seq)
    res = run_bass_kernel_spmd(nc, in_maps, core_ids=list(range(N_CORES)))
    pieces = []
    for j in range(N_CORES):
        o = np.asarray(res.results[j]["out"])
        pieces.append(o.reshape(D, TOK).T)
    full = np.concatenate(pieces, axis=0)
    return np.ascontiguousarray(full.reshape(1, seq, D)).astype(np.float32)


# version 04 (final)
# speedup vs baseline: 1.5931x; 1.3019x over previous
import sys
sys.path.insert(0, "/opt/trn_rl_repo")

import contextlib
import numpy as np
import ml_dtypes

import concourse.bass as bass
import concourse.mybir as mybir
from concourse.bass import IndirectOffsetOnAxis
from concourse.bass_utils import run_bass_kernel_spmd

F32 = mybir.dt.float32
BF16 = mybir.dt.bfloat16
U32 = mybir.dt.uint32
AF = mybir.ActivationFunctionType
ALU = mybir.AluOpType
AX = mybir.AxisListType

N_CORES = 8
D = 2048
NCH = 16
HD = 128
HP = 2
TB = 512
QB = 512
KSUB = 128
KGRP = 2
INV_SQRT_HD = float(np.float32(1.0) / np.sqrt(np.float32(128.0)))
EPS = 1e-5

ENGINES = ("tensor", "scalar", "vector", "gpsimd", "sync")


class Plan:
    def __init__(self):
        self.sched = {e: [] for e in ENGINES}
        self.cnt = {}
        self.wm = {e: {} for e in ENGINES}
        self.dw = {e: set() for e in ("vector", "scalar")}
        self.dr = {e: set() for e in ("vector", "scalar")}

    def _wait(self, eng, ev):
        if ev is None:
            return
        sem, val = ev
        if self.wm[eng].get(sem, 0) >= val:
            return
        self.wm[eng][sem] = val
        self.sched[eng].append(("wait", sem, val))

    def emit(self, eng, fn, reads=(), writes=(), waits=(), inc=None, amt=1):
        for ev in waits:
            self._wait(eng, ev)
        if eng in ("vector", "scalar"):
            rs, ws = set(reads), set(writes)
            if (rs & self.dw[eng]) or (ws & (self.dw[eng] | self.dr[eng])):
                self.sched[eng].append(("drain",))
                self.dw[eng] = set()
                self.dr[eng] = set()
            self.dw[eng] |= ws
            self.dr[eng] |= rs
        ev_out = None
        if inc is not None:
            self.cnt[inc] = self.cnt.get(inc, 0) + amt
            ev_out = (inc, self.cnt[inc])
        self.sched[eng].append(("op", fn, inc, amt))
        return ev_out

    def last(self, sem):
        return (sem, self.cnt.get(sem, 0))

    def replay(self, eng, engobj, sems):
        for item in self.sched[eng]:
            if item[0] == "wait":
                engobj.wait_ge(sems[item[1]], item[2])
            elif item[0] == "drain":
                engobj.drain()
            else:
                _, fn, inc, amt = item
                inst = fn(engobj)
                if inc is not None:
                    inst.then_inc(sems[inc], amt)


def build_program(seq, taps=False, pad_pe=0, twd=512):
    NTB = seq // TB
    NQB = seq // QB
    NSB = seq // KSUB
    TOK = seq // N_CORES
    TWD = twd
    NT_D = (TOK + TWD - 1) // TWD
    NTW = [min(TWD, TOK - TWD * i) for i in range(NT_D)]
    NTSB = TOK // 128
    NG = NSB // KGRP
    MSL = 6 * D // N_CORES

    nc = bass.Bass("TRN2", target_bir_lowering=False)

    xT = nc.dram_tensor("xT", [D, seq], F32, kind="ExternalInput")
    wqkv = nc.dram_tensor("wqkv", [D, 6 * HD], F32, kind="ExternalInput")
    wproj = nc.dram_tensor("wproj", [D, D], F32, kind="ExternalInput")
    wupT = nc.dram_tensor("wupT", [D, D], F32, kind="ExternalInput")
    wdownT = nc.dram_tensor("wdownT", [D, D], F32, kind="ExternalInput")
    wexp = nc.dram_tensor("wexp", [D, D], F32, kind="ExternalInput")
    wgT = nc.dram_tensor("wgT", [D, 8], F32, kind="ExternalInput")
    wmod = nc.dram_tensor("wmod", [D, MSL], F32, kind="ExternalInput")
    c_rd = nc.dram_tensor("c_r", [128, 16], F32, kind="ExternalInput")
    cosT = nc.dram_tensor("cosT", [128, seq], BF16, kind="ExternalInput")
    sinT = nc.dram_tensor("sinT", [128, seq], BF16, kind="ExternalInput")
    pTd = nc.dram_tensor("pT", [128, 128], BF16, kind="ExternalInput")
    cidxd = nc.dram_tensor("cidx_in", [128, NCH], U32, kind="ExternalInput")

    out = nc.dram_tensor("out", [NCH, 128, TOK], F32, kind="ExternalOutput")

    mod_in = nc.dram_tensor("mod_in", [MSL], F32)
    mod_out = nc.dram_tensor("mod_out", [6 * D], F32)
    ao_in0 = nc.dram_tensor("ao_in0", [128, seq], BF16)
    ao_in1 = nc.dram_tensor("ao_in1", [128, seq], BF16)
    ao_out0 = nc.dram_tensor("ao_out0", [N_CORES * 128, seq], BF16)
    ao_out1 = nc.dram_tensor("ao_out1", [N_CORES * 128, seq], BF16)
    g_bnc = nc.dram_tensor("g_bnc", [TOK], F32)

    if taps:
        htap = nc.dram_tensor("htap", [128, NCH * TB], BF16, kind="ExternalOutput")
        qtap = nc.dram_tensor("qtap", [128, TB], BF16, kind="ExternalOutput")
        ktap = nc.dram_tensor("ktap", [128, TB], BF16, kind="ExternalOutput")
        vtap = nc.dram_tensor("vtap", [128, 256], BF16, kind="ExternalOutput")
        atap = nc.dram_tensor("atap", [128, KGRP * QB], BF16, kind="ExternalOutput")
        otap = nc.dram_tensor("otap", [128, QB], BF16, kind="ExternalOutput")
        x1tap = nc.dram_tensor("x1tap", [128, TOK], F32, kind="ExternalOutput")
        h2tap = nc.dram_tensor("h2tap", [128, TOK], BF16, kind="ExternalOutput")
        gtap = nc.dram_tensor("gtap", [1, TOK], F32, kind="ExternalOutput")
        mtap = nc.dram_tensor("mtap", [128, 16], F32, kind="ExternalOutput")
        utap = nc.dram_tensor("utap", [128, 1], F32, kind="ExternalOutput")

    p = Plan()

    with contextlib.ExitStack() as st:
        def sb(name, shape, dt):
            return st.enter_context(nc.sbuf_tensor(name, shape, dt))

        qk = sb("qk", [128, 4 * seq], BF16)
        x1f = qk.ap().bitcast(F32)
        va = sb("va", [128, 2 * seq], BF16)
        big = sb("big", [128, 20480], BF16)
        wA = sb("wA", [128, 12288], BF16)
        xsl = sb("xsl", [128, 2 * TOK], F32)
        ob = sb("ob", [128, 2 * 1024], F32)

        c_sb = sb("c_sb", [128, 16], F32)
        silu_c = sb("silu_c", [128, 16], F32)
        m_sa = sb("m_sa", [128, 16], F32)
        m_ca1 = sb("m_ca1", [128, 16], F32)
        m_ga = sb("m_ga", [128, 16], F32)
        m_sb = sb("m_sb", [128, 16], F32)
        m_cb1 = sb("m_cb1", [128, 16], F32)
        m_gb = sb("m_gb", [128, 16], F32)
        mscr1 = sb("mscr1", [128, 16], F32)
        mscr2 = sb("mscr2", [128, 16], F32)
        mod_row = sb("mod_row", [1, MSL], F32)

        row_m = sb("row_m", [1, TB], F32)
        row_r = sb("row_r", [1, TB], F32)
        row_mr = sb("row_mr", [1, TB], F32)
        row_s1 = sb("row_s1", [1, TB], F32)
        row_s2 = sb("row_s2", [1, TB], F32)
        rsb = sb("rsb", [128, QB], F32)
        lnr = sb("lnr", [128, TB], F32)
        lnmr = sb("lnmr", [128, TB], F32)
        ones_b = sb("ones_b", [128, 1], BF16)
        ones_bb = sb("ones_bb", [128, 128], BF16)
        ones_c32 = sb("ones_c32", [128, 1], F32)
        ones_f = sb("ones_f", [1, 128], F32)
        pT_sb = sb("pT_sb", [128, 128], BF16)
        cidx = sb("cidx", [128, NCH], U32)
        mx1 = sb("mx1", [1, 1], F32)
        mx2 = sb("mx2", [1, 1], F32)
        sc1 = sb("sc1", [1, 1], F32)
        negU = sb("negU", [128, 1], F32)
        g_cols = sb("g_cols", [128, NTSB], F32)
        g_d = sb("g_d", [128, 8], F32)
        g_pen = sb("g_pen", [128, 8], F32)
        g_m1 = sb("g_m1", [128, 1], F32)
        g_m2 = sb("g_m2", [128, 1], F32)
        g_s2 = sb("g_s2", [128, 1], F32)
        g_inv = sb("g_inv", [128, 1], F32)
        g_row = sb("g_row", [1, TOK], F32)

        psA = st.enter_context(nc.psum_tensor("psA", [128, 1024], F32))
        psB = st.enter_context(nc.psum_tensor("psB", [128, 1024], F32))
        psC = st.enter_context(nc.psum_tensor("psC", [128, 512], F32))
        psD = st.enter_context(nc.psum_tensor("psD", [128, 512], F32))
        psE = st.enter_context(nc.psum_tensor("psE", [128, 512], F32))
        psF = st.enter_context(nc.psum_tensor("psF", [128, 512], F32))

        sems = {n: st.enter_context(nc.semaphore(n))
                for n in ("sINIT", "sW0", "sW1", "sW2", "sW3", "sW4", "sWG",
                          "sX0", "sX1", "sB0", "sB1", "sL0", "sL1", "sAO",
                          "sMD", "sGB", "sO0", "sO1", "sTP1", "sTP2",
                          "sCC", "sT", "sS", "sV")}
        block = st.enter_context(nc.Block())

        def mm(ps_ap, lhsT_ap, rhs_ap, start, stop, waits=(), inc=None):
            return p.emit("tensor",
                          lambda t, o=ps_ap, l=lhsT_ap, r=rhs_ap, s1=start, s2=stop:
                          t.matmul(o, l, r, start=s1, stop=s2),
                          waits=waits, inc=inc)

        last_tap = [None]

        def gdma(out_ap, in_ap, sem, waits=(), slow=False):
            waits = list(waits)
            if sem.startswith("sTP"):
                if last_tap[0] is not None:
                    waits.append(last_tap[0])
            ev = p.emit("gpsimd",
                        lambda g, o=out_ap, i=in_ap, sl=slow: g.dma_start(
                            out=o, in_=i, allow_slow_non_contiguous=sl),
                        waits=waits, inc=sem, amt=16)
            if sem.startswith("sTP"):
                last_tap[0] = ev
            return ev

        def ydma(out_ap, in_ap, sem, waits=()):
            return p.emit("sync",
                          lambda s, o=out_ap, i=in_ap: s.dma_start(out=o, in_=i),
                          waits=waits, inc=sem, amt=16)

        def vop(fn, rd=(), wr=(), waits=(), inc="sV"):
            return p.emit("vector", fn, reads=rd, writes=wr, waits=waits, inc=inc)

        def sop(fn, rd=(), wr=(), waits=(), inc="sS"):
            return p.emit("scalar", fn, reads=rd, writes=wr, waits=waits, inc=inc)

        vop(lambda v: v.memset(ones_b[:, :], 1.0), wr=("ones_b",))
        vop(lambda v: v.memset(ones_bb[:, :], 1.0), wr=("ones_bb",))
        vop(lambda v: v.memset(ones_c32[:, :], 1.0), wr=("ones_c32",))
        vop(lambda v: v.memset(ones_f[:, :], 1.0), wr=("ones_f",))
        vop(lambda v: v.memset(mx1[:, :], 0.0), wr=("mx1",))
        e_const = vop(lambda v: v.memset(mx2[:, :], 0.0), wr=("mx2",))
        gdma(pT_sb[:, :], pTd[:, :], 'sINIT')
        gdma(cidx[:, :], cidxd[:, :], 'sINIT')

        def xt_sl(b, ch, lo=0, w=TB):
            o = b * 8192 + ch * TB + lo
            return big[:, o:o + w]

        def tab_sl(b, is_sin):
            o = 16384 + b * 1024 + (512 if is_sin else 0)
            return big[:, o:o + TB]

        def sq_sl(s):
            return big[:, 18432 + s * 512: 18432 + (s + 1) * 512]

        def scr_sl(s):
            return big[:, 19456 + s * 512: 19456 + (s + 1) * 512]

        e_xt = [None] * (NTB + 2)
        e_tab = [None] * (NTB + 2)
        def r3(ap, cdim):
            return ap.rearrange("p (c t) -> p c t", c=cdim)

        e_xt[0] = gdma(r3(xt_sl(0, 0, 0, NCH * TB), NCH),
                       xT[:, 0:TB].rearrange("(c p) t -> p c t", p=128), "sX0")
        ydma(tab_sl(0, False), cosT[:, 0:TB], "sB0")
        e_tab[0] = ydma(tab_sl(0, True), sinT[:, 0:TB], "sB0")

        e_init = e_c = e_pt = e_cidx = gdma(c_sb[:, :], c_rd[:, :], 'sINIT')
        e_silu = sop(lambda s: s.activation(out=silu_c[:, :], in_=c_sb[:, :],
                                            func=AF.Silu),
                     rd=("c_sb",), wr=("silu_c",), waits=[e_c])
        NMT = MSL // 512
        mod_ps = [psA[0:1, 0:512], psA[0:1, 512:1024], psB[0:1, 0:512]]
        wAf = wA.ap().bitcast(F32)
        e_modmm = None
        e_wq_ev = None
        for qtr in range(4):
            wts = [e_wq_ev] if e_wq_ev is not None else []
            e_wq_ev = gdma(wAf[:, 0:4 * MSL].rearrange("p (c m) -> p c m", c=4),
                           wmod[qtr * 512:(qtr + 1) * 512, :].rearrange(
                               "(c p) m -> p c m", p=128),
                           "sW0", waits=wts)
            e_q = None
            for ch in range(4):
                gch = qtr * 4 + ch
                for ntl in range(NMT):
                    w = [e_wq_ev, e_silu] if (ch == 0 and ntl == 0) else ()
                    e_q = mm(mod_ps[ntl], silu_c[:, gch:gch + 1],
                             wAf[:, ch * MSL + ntl * 512: ch * MSL + (ntl + 1) * 512],
                             start=(gch == 0), stop=(gch == 15), waits=w,
                             inc="sT" if (ch == 3 and ntl == NMT - 1) else None)
            e_wq_ev = e_q
        e_modmm = e_q
        e_mr = None
        for ntl in range(NMT):
            e_mr = vop(lambda v, n=ntl: v.tensor_copy(
                out=mod_row[:, n * 512:(n + 1) * 512], in_=mod_ps[n]),
                wr=("mod_row",), waits=[e_modmm])
        e_mi = gdma(mod_in.ap(), mod_row[:, :], 'sW0', waits=[e_mr])
        e_ag1 = p.emit("gpsimd",
                       lambda g: g.collective_compute(
                           "AllGather", ALU.bypass,
                           replica_groups=[list(range(N_CORES))],
                           ins=[mod_in.ap().opt()], outs=[mod_out.ap().opt()]),
                       waits=[e_mi], inc="sCC")
        mtiles = [m_sa, m_ca1, m_ga, m_sb, m_cb1, m_gb]
        mtags = ["m_sa", "m_ca1", "m_ga", "m_sb", "m_cb1", "m_gb"]
        e_md_last = None
        for vi in range(6):
            e_md_last = gdma(mtiles[vi][:, :],
                             mod_out[vi * D:(vi + 1) * D].rearrange("(c p) -> p c", p=128),
                             "sMD", waits=[e_ag1], slow=True)
        e_md = [e_md_last] * 6
        vop(lambda v: v.tensor_scalar(out=m_ca1[:, :], in0=m_ca1[:, :], scalar1=1.0,
                                      scalar2=None, op0=ALU.add),
            rd=("m_ca1",), wr=("m_ca1",), waits=[e_md[1]])
        vop(lambda v: v.tensor_scalar(out=m_cb1[:, :], in0=m_cb1[:, :], scalar1=1.0,
                                      scalar2=None, op0=ALU.add),
            rd=("m_cb1",), wr=("m_cb1",), waits=[e_md[4]])
        for gt, tg, ei in ((m_ga, "m_ga", e_md[2]), (m_gb, "m_gb", e_md[5])):
            vop(lambda v, g=gt: v.tensor_scalar(out=mscr1[:, :], in0=g[:, :],
                                                scalar1=0.0, scalar2=None, op0=ALU.is_ge),
                rd=(tg,), wr=("mscr1",), waits=[ei])
            vop(lambda v, g=gt: v.tensor_scalar(out=mscr2[:, :], in0=g[:, :],
                                                scalar1=0.25, scalar2=None, op0=ALU.max),
                rd=(tg,), wr=("mscr2",))
            vop(lambda v, g=gt: v.tensor_scalar(out=g[:, :], in0=g[:, :],
                                                scalar1=-0.25, scalar2=None, op0=ALU.min),
                rd=(tg,), wr=(tg,))
            vop(lambda v, g=gt: v.tensor_tensor(out=mscr2[:, :], in0=mscr2[:, :],
                                                in1=g[:, :], op=ALU.subtract),
                rd=("mscr2", tg), wr=("mscr2",))
            vop(lambda v: v.tensor_tensor(out=mscr2[:, :], in0=mscr1[:, :],
                                          in1=mscr2[:, :], op=ALU.mult),
                rd=("mscr1", "mscr2"), wr=("mscr2",))
            vop(lambda v, g=gt: v.tensor_tensor(out=g[:, :], in0=g[:, :],
                                                in1=mscr2[:, :], op=ALU.add),
                rd=(tg, "mscr2"), wr=(tg,))
        e_mod_rdy = p.last("sV")
        if taps:
            gdma(mtap[:, :], m_ga[:, :], "sTP2", waits=[e_mod_rdy])

        if NTB > 1:
            e_xt[1] = gdma(r3(xt_sl(1, 0, 0, NCH * TB), NCH),
                           xT[:, TB:2 * TB].rearrange("(c p) t -> p c t", p=128),
                           "sX1")
            ydma(tab_sl(1, False), cosT[:, TB:2 * TB], "sB1")
            e_tab[1] = ydma(tab_sl(1, True), sinT[:, TB:2 * TB], "sB1")
        e_wqkv = gdma(r3(wA[:, 0:NCH * 768], NCH),
                      wqkv[:, :].rearrange("(c p) m -> p c m", p=128),
                      "sW0", waits=[e_modmm])

        WQ = [wA[:, ch * 768:ch * 768 + 256] for ch in range(NCH)]
        WK = [wA[:, ch * 768 + 256:ch * 768 + 512] for ch in range(NCH)]
        WV = [wA[:, ch * 768 + 512:ch * 768 + 768] for ch in range(NCH)]

        qk_ps = [psB[:, 0:512], psB[:, 512:1024], psC[:, :], psD[:, :]]
        prev_psread = [None] * 4
        prev_hlast = None
        prev_rowend = None
        prev_umax = None
        e_vdone_last = None
        e_hlast_final = None

        for i in range(NTB):
            b = i % 2
            e_smm_last = None
            e_smm_by = {}
            for ch in range(NCH):
                ssl = ch % 2
                wsq = []
                if ch < 2:
                    wsq = [e_xt[i], e_const]
                    if prev_umax is not None:
                        wsq.append(prev_umax)
                else:
                    wsq = [e_smm_by[ch - 2]]
                e_sq = sop(lambda s, c=ch, bb=b, sl=ssl: s.activation(
                    out=sq_sl(sl), in_=xt_sl(bb, c), func=AF.Square),
                    rd=(f"xt{b}_{ch}",), wr=(f"sq{ssl}",), waits=wsq)
                w0 = []
                if ch == 0:
                    w0 = [e_xt[i], e_const]
                    if prev_rowend is not None:
                        w0.append(prev_rowend)
                    if prev_umax is not None:
                        w0.append(prev_umax)
                mm(psE[:, :], ones_bb[:, :], xt_sl(b, ch),
                   start=(ch == 0), stop=(ch == NCH - 1), waits=w0)
                e_smm_last = e_smm_by[ch] = mm(psF[:, :], ones_bb[:, :], sq_sl(ssl),
                                               start=(ch == 0), stop=(ch == NCH - 1),
                                               waits=[e_sq], inc="sT")
            vop(lambda v: v.tensor_scalar(out=lnmr[:, :], in0=psE[:, :],
                                          scalar1=1.0 / D, scalar2=None, op0=ALU.mult),
                wr=("lnmr",), waits=[e_smm_last])
            vop(lambda v: v.tensor_tensor(out=lnr[:, :], in0=lnmr[:, :],
                                          in1=lnmr[:, :], op=ALU.mult),
                rd=("lnmr",), wr=("lnr",))
            vop(lambda v: v.scalar_tensor_tensor(
                out=lnr[:, :], in0=psF[:, :], scalar=1.0 / D, in1=lnr[:, :],
                op0=ALU.mult, op1=ALU.subtract),
                rd=("lnr",), wr=("lnr",))
            vop(lambda v: v.tensor_scalar(out=lnr[:, :], in0=lnr[:, :],
                                          scalar1=EPS, scalar2=None, op0=ALU.add),
                rd=("lnr",), wr=("lnr",))
            e_rcp = vop(lambda v: v.reciprocal(out=lnr[:, :], in_=lnr[:, :]),
                        rd=("lnr",), wr=("lnr",))
            prev_rowend = e_rcp
            e_sqrt = sop(lambda s: s.activation(out=lnr[:, :], in_=lnr[:, :],
                                                func=AF.Sqrt),
                         rd=("lnr",), wr=("lnr",), waits=[e_rcp])
            e_rows = vop(lambda v: v.scalar_tensor_tensor(
                out=lnmr[:, :], in0=lnmr[:, :], scalar=-1.0, in1=lnr[:, :],
                op0=ALU.mult, op1=ALU.mult),
                rd=("lnmr", "lnr"), wr=("lnmr",), waits=[e_sqrt])
            e_h = {}
            e_scr = {}
            for ch in range(NCH):
                ssl = ch % 2
                w0 = [e_rows]
                if ch >= 2:
                    w0.append(e_h[ch - 2])
                if i == 0 and ch == 0:
                    w0.append(e_mod_rdy)
                vop(lambda v, c=ch, bb=b, sl=ssl: v.tensor_tensor(
                    out=scr_sl(sl), in0=xt_sl(bb, c), in1=lnr[:, :], op=ALU.mult),
                    rd=(f"xt{b}_{ch}", "lnr"), wr=(f"scr{ssl}",), waits=w0)
                e_scr[ch] = vop(lambda v, sl=ssl: v.tensor_tensor(
                    out=scr_sl(sl), in0=scr_sl(sl), in1=lnmr[:, :], op=ALU.add),
                    rd=(f"scr{ssl}", "lnmr"), wr=(f"scr{ssl}",))
                wh = [e_scr[ch]]
                if i == 0 and ch == 0:
                    wh.append(e_mod_rdy)
                e_h[ch] = sop(lambda s, c=ch, bb=b, sl=ssl: s.activation(
                    out=xt_sl(bb, c), in_=scr_sl(sl), func=AF.Identity,
                    scale=m_ca1[:, c:c + 1], bias=m_sa[:, c:c + 1]),
                    rd=(f"scr{ssl}", "m_ca1", "m_sa"), wr=(f"xt{b}_{ch}",),
                    waits=wh)
            e_hlast_final = e_h[NCH - 1]
            if taps and i == 0:
                gdma(htap[:, :], xt_sl(b, 0, 0, NCH * TB), 'sTP2', waits=[e_h[NCH - 1]])
            e_qk_mm = [None] * 4
            for ch in range(NCH):
                for mt in range(4):
                    lhs = (WQ[ch][:, (mt % 2) * 128:(mt % 2 + 1) * 128] if mt < 2
                           else WK[ch][:, (mt - 2) * 128:(mt - 1) * 128])
                    w0 = []
                    if ch == 0:
                        w0 = [e_h[ch], e_wqkv]
                        if prev_psread[mt] is not None:
                            w0.append(prev_psread[mt])
                    else:
                        w0 = [e_h[ch]]
                    e_qk_mm[mt] = mm(qk_ps[mt], lhs, xt_sl(b, ch),
                                     start=(ch == 0), stop=(ch == NCH - 1),
                                     waits=w0,
                                     inc="sT" if ch == NCH - 1 else None)
            e_vdone = None
            for sbi in range(TB // KSUB):
                e_vmm = None
                for ch in range(NCH):
                    w0 = []
                    if ch == 0:
                        w0 = [e_rcp, e_h[NCH - 1]]
                        if e_vdone is not None:
                            w0.append(e_vdone)
                    e_vmm = mm(psF[:, 0:256],
                               xt_sl(b, ch, sbi * KSUB, KSUB), WV[ch],
                               start=(ch == 0), stop=(ch == NCH - 1),
                               waits=w0, inc="sT" if ch == NCH - 1 else None)
                gsb = i * (TB // KSUB) + sbi
                e_vdone = sop(lambda s, g=gsb: s.activation(
                    out=va[:, g * 256:(g + 1) * 256], in_=psF[:, 0:256],
                    func=AF.Identity),
                    wr=(f"v{gsb}",), waits=[e_vmm])
                if taps and gsb == 0:
                    gdma(vtap[:, :], va[:, 0:256], 'sTP2', waits=[e_vdone])
            e_vdone_last = e_vdone
            prev_r3 = None
            prev_mx = None
            for mt in range(4):
                h = mt % 2
                isq = mt < 2
                base = (0 if isq else 2 * seq) + h * seq + i * TB
                dst = qk[:, base:base + TB]
                tag = f"qk_m{mt}_{i}"
                wev = [e_qk_mm[mt], e_vdone]
                if prev_r3 is not None:
                    wev.append(prev_r3)
                e_ev = sop(lambda s, ps=qk_ps[mt]: s.activation(
                    out=scr_sl(1), in_=ps, func=AF.Identity),
                    wr=("scr1",), waits=wev)
                e_rot = mm(qk_ps[mt], pT_sb[:, :], scr_sl(1), start=True, stop=True,
                           waits=[e_ev, e_pt], inc="sT")
                vop(lambda v, d=dst, bb=b: v.tensor_tensor(
                    out=d, in0=scr_sl(1), in1=tab_sl(bb, False), op=ALU.mult),
                    rd=("scr1", f"tab{b}"), wr=(tag,),
                    waits=[e_rot, e_tab[i]])
                e_psr = vop(lambda v, ps=qk_ps[mt], bb=b: v.tensor_tensor(
                    out=scr_sl(1), in0=ps, in1=tab_sl(bb, True), op=ALU.mult),
                    rd=(f"tab{b}",), wr=("scr1",))
                prev_psread[mt] = e_psr
                e_r3 = vop(lambda v, d=dst: v.tensor_tensor(
                    out=d, in0=d, in1=scr_sl(1), op=ALU.add),
                    rd=(tag, "scr1"), wr=(tag,))
                prev_r3 = e_r3
                e_sq2 = vop(lambda v, d=dst: v.tensor_tensor(
                    out=sq_sl(0), in0=d, in1=d, op=ALU.mult),
                    rd=(tag,), wr=("sq0",), waits=[e_smm_last])
                wn = [e_sq2]
                if prev_mx is not None:
                    wn.append(prev_mx)
                else:
                    wn.append(e_rcp)
                    if prev_umax is not None:
                        wn.append(prev_umax)
                e_nmm = mm(psE[0:1, :], ones_b[:, :], sq_sl(0), start=True, stop=True,
                           waits=wn, inc="sT")
                mxt, mxn = (mx1, "mx1") if isq else (mx2, "mx2")
                vop(lambda v: v.reduce_max(out=sc1[:, :], in_=psE[0:1, :], axis=AX.X),
                    rd=(), wr=("sc1",), waits=[e_nmm])
                e_mx = vop(lambda v, m=mxt: v.tensor_tensor(
                    out=m[:, :], in0=m[:, :], in1=sc1[:, :], op=ALU.max),
                    rd=(mxn, "sc1"), wr=(mxn,))
                prev_mx = e_mx
                if taps and i == 0 and mt == 0:
                    gdma(qtap[:, :], dst, 'sTP2', waits=[e_r3])
                if taps and i == 0 and mt == 2:
                    gdma(ktap[:, :], dst, 'sTP2', waits=[e_r3])
            prev_umax = prev_mx
            j = i + 2
            if j < NTB:
                e_qkv_done = p.last("sT")
                e_xt[j] = gdma(r3(xt_sl(j % 2, 0, 0, NCH * TB), NCH),
                               xT[:, j * TB:(j + 1) * TB].rearrange(
                                   "(c p) t -> p c t", p=128),
                               "sX0" if j % 2 == 0 else "sX1",
                               waits=[e_qkv_done])
                ydma(tab_sl(j % 2, False), cosT[:, j * TB:(j + 1) * TB],
                     "sB0" if j % 2 == 0 else "sB1", waits=[prev_r3])
                e_tab[j] = ydma(tab_sl(j % 2, True), sinT[:, j * TB:(j + 1) * TB],
                                "sB0" if j % 2 == 0 else "sB1")

        e_u1 = vop(lambda v: v.tensor_tensor(out=sc1[:, :], in0=mx1[:, :],
                                             in1=mx2[:, :], op=ALU.mult),
                   rd=("mx1", "mx2"), wr=("sc1",), waits=[prev_umax])
        e_u2 = sop(lambda s: s.activation(out=sc1[:, :], in_=sc1[:, :], func=AF.Sqrt),
                   rd=("sc1",), wr=("sc1",), waits=[e_u1])
        e_u3 = vop(lambda v: v.tensor_scalar(out=sc1[:, :], in0=sc1[:, :],
                                             scalar1=-INV_SQRT_HD, scalar2=None,
                                             op0=ALU.mult),
                   rd=("sc1",), wr=("sc1",), waits=[e_u2])
        e_u5 = mm(psF[:, 0:1], ones_f[:, :], sc1[:, :], start=True, stop=True,
                  waits=[e_u3, e_vdone_last], inc="sT")
        e_negU = vop(lambda v: v.tensor_copy(out=negU[:, :], in_=psF[:, 0:1]),
                     wr=("negU",), waits=[e_u5])
        if taps:
            gdma(utap[:, :], negU[:, :], 'sTP2', waits=[e_negU])

        e_taps_preB = p.last("sTP2") if taps else None

        RING = 5

        def wslot(s, ch=None, w=128):
            if ch is None:
                return wA[:, s * 2048:(s + 1) * 2048]
            return wA[:, s * 2048 + ch * w: s * 2048 + (ch + 1) * w]

        wsets = ([("proj", m) for m in range(NCH)]
                 + [("up", m) for m in range(NCH)])
        for m in range(NCH):
            wsets.append(("down", m))
            wsets.append(("exp", m))
        wsrc = {"proj": wproj, "up": wupT, "down": wdownT, "exp": wexp}
        slot_free = [None] * RING
        e_wset = [None] * len(wsets)
        set_slot = {}
        e_qkv_all_done = p.last("sT")
        e_wgate = gdma(r3(wA[:, 10240:10240 + NCH * 8], NCH),
                       wgT[:, :].rearrange("(c p) m -> p c m", p=128),
                       "sWG", waits=[e_qkv_all_done])

        def emit_wset_dma(si):
            kind, m = wsets[si]
            s = si % RING
            set_slot[si] = s
            wts = []
            if si < RING:
                wts.append(e_qkv_all_done)
            if slot_free[s] is not None:
                wts.append(slot_free[s])
            src = wsrc[kind][:, m * 128:(m + 1) * 128].rearrange(
                "(c p) m -> p c m", p=128)
            e_wset[si] = gdma(r3(wslot(s), NCH), src, f"sW{s}", wts)

        n_wdma = 0
        for _ in range(min(RING, len(wsets))):
            emit_wset_dma(n_wdma)
            n_wdma += 1

        def a_sl(s, lo=0, w=KGRP * QB):
            o = 16384 + s * 1024 + lo
            return big[:, o:o + w]

        def ao_loc(ch, lo=0, w=None):
            if w is None:
                w = TOK
            return big[:, ch * TOK + lo: ch * TOK + lo + w]

        e_aog = [None] * NCH
        ao_r0 = ao_out0.ap().rearrange("d (k t) -> (d k) t", k=N_CORES)
        ao_r1 = ao_out1.ap().rearrange("d (k t) -> (d k) t", k=N_CORES)

        e_ao = {}
        prev_exp2 = [None, None]
        prev_grp2 = [None, None]
        prev_av_evt = {0: None, 1: None}
        prev_rs2 = [None, None]
        it = 0
        for h in range(HP):
            for qb in range(NQB):
                par = it % 2
                avbk = psC if par == 0 else psD
                smbk = psE if par == 0 else psF
                e_exp = {}
                e_grp = {}
                e_s2_by = {}

                def emit_scores(g, h=h, qb=qb, it=it, e_exp=e_exp,
                                e_s2_by=e_s2_by):
                    sps = psA if g % 2 == 0 else psB
                    wts = []
                    if g >= 2:
                        wts.append(e_exp[g - 2])
                    else:
                        if prev_exp2[g % 2] is not None:
                            wts.append(prev_exp2[g % 2])
                        if it == 0:
                            wts.append(e_hlast_final)
                            wts.append(e_mr)
                            for ev in prev_psread:
                                if ev is not None:
                                    wts.append(ev)
                    e_s2 = None
                    for kk in range(KGRP):
                        kb = g * KGRP + kk
                        lhs = qk[:, 2 * seq + h * seq + kb * KSUB:
                                 2 * seq + h * seq + (kb + 1) * KSUB]
                        rhs = qk[:, h * seq + qb * QB: h * seq + (qb + 1) * QB]
                        e_s2 = mm(sps[:, kk * 512:(kk + 1) * 512], lhs, rhs,
                                  start=True, stop=True,
                                  waits=wts if kk == 0 else (),
                                  inc="sT" if kk == KGRP - 1 else None)
                    e_s2_by[g] = e_s2

                emit_scores(0)
                emit_scores(1)
                for g in range(NG):
                    sps = psA if g % 2 == 0 else psB
                    wts2 = [e_s2_by[g], e_negU]
                    if g >= 2:
                        wts2.append(e_grp[g - 2])
                    elif prev_grp2[g % 2] is not None:
                        wts2.append(prev_grp2[g % 2])
                    if it == 0 and g < 2:
                        wts2.append(prev_r3)
                    if taps and it == 0 and g == 2:
                        wts2.append(e_atap)
                    e_exp[g] = sop(lambda s, sl=g % 2, ps=sps: s.activation(
                        out=a_sl(sl), in_=ps[:, 0:KGRP * 512], func=AF.Exp,
                        scale=INV_SQRT_HD, bias=negU[:, 0:1]),
                        rd=("negU",), waits=wts2)
                    if g + 2 < NG:
                        emit_scores(g + 2)
                    for kk in range(KGRP):
                        kb = g * KGRP + kk
                        wv = []
                        if kk == 0:
                            wv = [e_exp[g]]
                            if g == 0:
                                if prev_av_evt[par] is not None:
                                    wv.append(prev_av_evt[par])
                                if it == 0:
                                    wv.append(e_negU)
                                if it == 1:
                                    wv.append(e_negU)
                        mm(avbk[:, :],
                           va[:, kb * 256 + h * 128: kb * 256 + (h + 1) * 128],
                           a_sl(g % 2, kk * 512, 512),
                           start=(kb == 0), stop=(kb == NSB - 1), waits=wv)
                        wg = []
                        if kb == 0 and prev_rs2[par] is not None:
                            wg = [prev_rs2[par]]
                        e_grp[g] = mm(smbk[:, :], ones_bb[:, :],
                                      a_sl(g % 2, kk * 512, 512),
                                      start=(kb == 0), stop=(kb == NSB - 1),
                                      waits=wg,
                                      inc="sT" if kk == KGRP - 1 else None)
                    if taps and it == 0 and g == 0:
                        e_atap = gdma(atap[:, :], a_sl(0), 'sTP1', waits=[e_grp[0]])
                prev_exp2 = [e_exp[NG - 2], e_exp[NG - 1]]
                prev_grp2 = [e_grp[NG - 2], e_grp[NG - 1]]
                e_rs = vop(lambda v, sk=smbk: v.reciprocal(out=rsb[:, :],
                                                           in_=sk[:, :]),
                           rd=(), wr=("rsb",), waits=[e_grp[NG - 1]])
                prev_rs2[par] = e_rs
                wao = []
                if taps and it == 0 and e_taps_preB is not None:
                    wao.append(e_taps_preB)
                e_ao[(h, qb)] = vop(lambda v, hh=h, q=qb, ab=avbk: v.tensor_tensor(
                    out=big[:, hh * seq + q * QB: hh * seq + (q + 1) * QB],
                    in0=ab[:, :], in1=rsb[:, :], op=ALU.mult),
                    rd=("rsb",), wr=(f"ao{h}_{qb}",), waits=wao)
                prev_av_evt[par] = e_ao[(h, qb)]
                if taps and h == 0 and qb == 0:
                    gdma(otap[:, :], big[:, 0:QB], 'sTP2', waits=[e_ao[(0, 0)]])
                it += 1
            if h == 0:
                e_aow0 = gdma(ao_in0.ap(), big[:, 0:seq], "sAO",
                              waits=[e_ao[(0, q)] for q in range(NQB)])
                e_ag1 = p.emit("gpsimd",
                               lambda g: g.collective_compute(
                                   "AllGather", ALU.bypass,
                                   replica_groups=[list(range(N_CORES))],
                                   ins=[ao_in0.ap().opt()],
                                   outs=[ao_out0.ap().opt()]),
                               waits=[e_aow0], inc="sCC")
                for ch in (0, 2, 4, 6):
                    e_aog[ch] = p.emit(
                        "gpsimd",
                        lambda g, c=ch: g.indirect_dma_start(
                            out=ao_loc(c), out_offset=None, in_=ao_r0,
                            in_offset=IndirectOffsetOnAxis(
                                ap=cidx[:, c // 2:c // 2 + 1], axis=0)),
                        waits=[e_ag1, e_init] if ch == 0 else [e_ag1],
                        inc="sAO", amt=16)
            else:
                e_aow1 = gdma(ao_in1.ap(), big[:, seq:2 * seq], "sAO",
                              waits=[e_ao[(1, q)] for q in range(NQB)])
                for ch in (8, 10, 12, 14):
                    e_aog[ch] = p.emit(
                        "gpsimd",
                        lambda g, c=ch: g.indirect_dma_start(
                            out=ao_loc(c), out_offset=None, in_=ao_r0,
                            in_offset=IndirectOffsetOnAxis(
                                ap=cidx[:, c // 2:c // 2 + 1], axis=0)),
                        waits=[e_aow1, e_ag1], inc="sAO", amt=16)
                e_ag2 = p.emit("gpsimd",
                               lambda g: g.collective_compute(
                                   "AllGather", ALU.bypass,
                                   replica_groups=[list(range(N_CORES))],
                                   ins=[ao_in1.ap().opt()],
                                   outs=[ao_out1.ap().opt()]),
                               waits=[e_aow1], inc="sCC")

        xT_r = xT.ap().rearrange("d (k t) -> (d k) t", k=N_CORES)
        e_xsl = [None] * NCH
        for m in range(2):
            e_xsl[m] = p.emit(
                "gpsimd",
                lambda g, mm_=m: g.indirect_dma_start(
                    out=xsl[:, (mm_ % 2) * TOK:(mm_ % 2) * TOK + TOK],
                    out_offset=None, in_=xT_r,
                    in_offset=IndirectOffsetOnAxis(ap=cidx[:, mm_:mm_ + 1], axis=0)),
                waits=[e_init] if m == 0 else (),
                inc="sL0" if m % 2 == 0 else "sL1", amt=16)

        e_taps_preD = p.last("sTP2") if taps else None
        for ch in (1, 3, 5, 7, 9, 11, 13, 15):
            w0 = [e_ag2]
            if taps and ch == 1 and e_taps_preD is not None:
                w0.append(e_taps_preD)
            e_aog[ch] = p.emit(
                "gpsimd",
                lambda g, c=ch: g.indirect_dma_start(
                    out=ao_loc(c), out_offset=None, in_=ao_r1,
                    in_offset=IndirectOffsetOnAxis(
                        ap=cidx[:, c // 2:c // 2 + 1], axis=0)),
                waits=w0, inc="sAO", amt=16)

        rot_ps = [psA[:, 0:512], psA[:, 512:1024], psB[:, 0:512], psB[:, 512:1024]]
        rot_free = [prev_exp2[0], prev_exp2[0], prev_exp2[1], prev_exp2[1]]
        e_pd_start = [prev_exp2[0], prev_exp2[1]]

        proj_ch_order = [0, 2, 4, 6, 8, 10, 12, 14, 1, 3, 5, 7, 9, 11, 13, 15]
        si = 0
        e_x1 = {}
        ri = 0
        for m in range(NCH):
            for nt in range(NT_D):
                w = NTW[nt]
                pp = rot_ps[ri % 4]
                e_p = None
                for ci, ch in enumerate(proj_ch_order):
                    wts = []
                    if ci == 0:
                        wts = [e_wset[si], e_aog[ch]]
                        if rot_free[ri % 4] is not None:
                            wts.append(rot_free[ri % 4])
                    else:
                        wts = [e_aog[ch]]
                    e_p = mm(pp[:, 0:w], wslot(set_slot[si], ch),
                             ao_loc(ch, nt * TWD, w),
                             start=(ci == 0), stop=(ci == NCH - 1),
                             waits=wts, inc="sT" if ci == NCH - 1 else None)
                wts = [e_p, e_xsl[m], e_mod_rdy] + [ev for ev in e_pd_start if ev]
                if taps and e_taps_preD is not None:
                    wts.append(e_taps_preD)
                e_x1[(m, nt)] = vop(
                    lambda v, mm_=m, n=nt, ppp=pp, ww=w: v.scalar_tensor_tensor(
                        out=x1f[:, mm_ * TOK + n * TWD: mm_ * TOK + n * TWD + ww],
                        in0=ppp[:, 0:ww], scalar=m_ga[:, mm_:mm_ + 1],
                        in1=xsl[:, (mm_ % 2) * TOK + n * TWD:
                                (mm_ % 2) * TOK + n * TWD + ww],
                        op0=ALU.mult, op1=ALU.add),
                    rd=("m_ga", f"xsl{m % 2}"), wr=(f"x1_{m}",),
                    waits=wts)
                rot_free[ri % 4] = e_x1[(m, nt)]
                ri += 1
            if m + 2 < NCH:
                e_xsl[m + 2] = p.emit(
                    "gpsimd",
                    lambda g, mm_=m + 2: g.indirect_dma_start(
                        out=xsl[:, (mm_ % 2) * TOK:(mm_ % 2) * TOK + TOK],
                        out_offset=None, in_=xT_r,
                        in_offset=IndirectOffsetOnAxis(ap=cidx[:, mm_:mm_ + 1],
                                                       axis=0)),
                    waits=[e_x1[(m, NT_D - 1)]],
                    inc="sL0" if m % 2 == 0 else "sL1", amt=16)
            slot_free[set_slot[si]] = e_x1[(m, NT_D - 1)]
            si += 1
            if n_wdma < len(wsets):
                emit_wset_dma(n_wdma)
                n_wdma += 1
        e_proj_done = p.last("sT")
        if taps:
            gdma(x1tap[:, :], x1f[:, 0:TOK], 'sTP2', waits=[e_x1[(0, NT_D - 1)]])

        prev_rowend2 = None
        e_h2 = {}
        e_h2_last = None
        for nt in range(NT_D):
            w = NTW[nt]
            e_smm2 = None
            e_smm2_by = {}
            for ch in range(NCH):
                ssl = ch % 2
                wsq = [e_x1[(ch, nt)]]
                if ch >= 2:
                    wsq.append(e_smm2_by[ch - 2])
                e_sq = vop(lambda v, c=ch, n=nt, s=ssl, ww=w: v.tensor_tensor(
                    out=sq_sl(s)[:, 0:ww],
                    in0=x1f[:, c * TOK + n * TWD: c * TOK + n * TWD + ww],
                    in1=x1f[:, c * TOK + n * TWD: c * TOK + n * TWD + ww],
                    op=ALU.mult),
                    rd=(f"x1_{ch}",), wr=(f"sq{ssl}",),
                    waits=wsq)
                wts = []
                if ch == 0:
                    wts = [e_x1[(ch, nt)]]
                    if prev_rowend2 is not None:
                        wts.append(prev_rowend2)
                    else:
                        wts.append(e_ao[(HP - 1, NQB - 1)])
                        wts.append(e_ao[(HP - 1, NQB - 2)])
                        wts.append(e_negU)
                mm(psE[0:1, 0:w], ones_c32[:, :],
                   x1f[:, ch * TOK + nt * TWD: ch * TOK + nt * TWD + w],
                   start=(ch == 0), stop=(ch == NCH - 1), waits=wts)
                e_smm2 = e_smm2_by[ch] = mm(psF[0:1, 0:w], ones_b[:, :],
                                            sq_sl(ssl)[:, 0:w],
                                            start=(ch == 0), stop=(ch == NCH - 1),
                                            waits=[e_sq], inc="sT")
            vop(lambda v, ww=w: v.tensor_scalar(out=row_m[:, 0:ww], in0=psE[0:1, 0:ww],
                                                scalar1=1.0 / D, scalar2=None,
                                                op0=ALU.mult),
                wr=("row_m",), waits=[e_smm2])
            vop(lambda v, ww=w: v.tensor_scalar(out=row_s1[:, 0:ww], in0=psF[0:1, 0:ww],
                                                scalar1=1.0 / D, scalar2=None,
                                                op0=ALU.mult),
                wr=("row_s1",))
            vop(lambda v, ww=w: v.tensor_tensor(out=row_s2[:, 0:ww], in0=row_m[:, 0:ww],
                                                in1=row_m[:, 0:ww], op=ALU.mult),
                rd=("row_m",), wr=("row_s2",))
            vop(lambda v, ww=w: v.tensor_tensor(out=row_s1[:, 0:ww], in0=row_s1[:, 0:ww],
                                                in1=row_s2[:, 0:ww], op=ALU.subtract),
                rd=("row_s1", "row_s2"), wr=("row_s1",))
            vop(lambda v, ww=w: v.tensor_scalar(out=row_s1[:, 0:ww], in0=row_s1[:, 0:ww],
                                                scalar1=EPS, scalar2=None, op0=ALU.add),
                rd=("row_s1",), wr=("row_s1",))
            e_rcp2 = vop(lambda v, ww=w: v.reciprocal(out=row_s2[:, 0:ww],
                                                      in_=row_s1[:, 0:ww]),
                         rd=("row_s1",), wr=("row_s2",))
            prev_rowend2 = e_rcp2
            e_sqrt2 = sop(lambda s, ww=w: s.activation(out=row_r[:, 0:ww],
                                                       in_=row_s2[:, 0:ww],
                                                       func=AF.Sqrt),
                          rd=("row_s2",), wr=("row_r",), waits=[e_rcp2])
            e_rows2 = vop(lambda v, ww=w: v.scalar_tensor_tensor(
                out=row_mr[:, 0:ww], in0=row_m[:, 0:ww], scalar=-1.0,
                in1=row_r[:, 0:ww], op0=ALU.mult, op1=ALU.mult),
                rd=("row_m", "row_r"), wr=("row_mr",), waits=[e_sqrt2])
            wbc = [e_rows2, e_proj_done]
            if e_h2_last is not None:
                wbc.append(e_h2_last)
            mm(psA[:, 0:512][:, 0:w], ones_f[:, :], row_r[0:1, 0:w],
               start=True, stop=True, waits=wbc)
            e_bc2 = mm(psA[:, 512:1024][:, 0:w], ones_f[:, :], row_mr[0:1, 0:w],
                       start=True, stop=True, inc="sT")
            for ch in range(NCH):
                vop(lambda v, c=ch, n=nt, ww=w: v.tensor_tensor(
                    out=scr_sl(0)[:, 0:ww],
                    in0=x1f[:, c * TOK + n * TWD: c * TOK + n * TWD + ww],
                    in1=psA[:, 0:ww], op=ALU.mult),
                    rd=(f"x1_{ch}",), wr=("scr0",),
                    waits=[e_bc2] if ch == 0 else ())
                vop(lambda v, ww=w: v.tensor_tensor(
                    out=scr_sl(0)[:, 0:ww], in0=scr_sl(0)[:, 0:ww],
                    in1=psA[:, 512:512 + ww], op=ALU.add),
                    rd=("scr0",), wr=("scr0",))
                e_h2[(ch, nt)] = vop(lambda v, c=ch, n=nt, ww=w: v.tensor_scalar(
                    out=va[:, c * TOK + n * TWD: c * TOK + n * TWD + ww],
                    in0=scr_sl(0)[:, 0:ww],
                    scalar1=m_cb1[:, c:c + 1], scalar2=m_sb[:, c:c + 1],
                    op0=ALU.mult, op1=ALU.add),
                    rd=("scr0", "m_cb1", "m_sb"), wr=(f"h2_{ch}_{nt}",))
                e_h2_last = e_h2[(ch, nt)]
        e_h2tap = None
        if taps:
            e_h2tap = gdma(h2tap[:, :], va[:, 0:TOK], 'sTP2', waits=[e_h2_last])

        e_gate_vec = None
        for s8 in range(NTSB):
            e_gmm = None
            for ch in range(NCH):
                wts = []
                if ch == 0:
                    wts = [e_wgate, e_h2[(ch, ((s8 + 1) * 128 - 1) // TWD)]]
                    if e_gate_vec is not None:
                        wts.append(e_gate_vec)
                    else:
                        wts.append(prev_av_evt[0])
                else:
                    wts = [e_h2[(ch, ((s8 + 1) * 128 - 1) // TWD)]]
                e_gmm = mm(psC[:, 0:8],
                           va[:, ch * TOK + s8 * 128: ch * TOK + (s8 + 1) * 128],
                           wA[:, 10240 + ch * 8: 10240 + (ch + 1) * 8],
                           start=(ch == 0), stop=(ch == NCH - 1),
                           waits=wts, inc="sT" if ch == NCH - 1 else None)
            vop(lambda v: v.reduce_max(out=g_m1[:, :], in_=psC[:, 0:8], axis=AX.X),
                wr=("g_m1",), waits=[e_gmm])
            vop(lambda v: v.tensor_scalar(out=g_d[:, :], in0=psC[:, 0:8],
                                          scalar1=g_m1[:, 0:1], scalar2=None,
                                          op0=ALU.subtract),
                rd=("g_m1",), wr=("g_d",))
            vop(lambda v: v.tensor_scalar(out=g_pen[:, :], in0=g_d[:, :],
                                          scalar1=0.0, scalar2=-1e30,
                                          op0=ALU.is_equal, op1=ALU.mult),
                rd=("g_d",), wr=("g_pen",))
            vop(lambda v: v.tensor_tensor(out=g_pen[:, :], in0=g_pen[:, :],
                                          in1=g_d[:, :], op=ALU.add),
                rd=("g_pen", "g_d"), wr=("g_pen",))
            e_gm2 = vop(lambda v: v.reduce_max(out=g_m2[:, :], in_=g_pen[:, :],
                                               axis=AX.X),
                        rd=("g_pen",), wr=("g_m2",))
            e_gexp = sop(lambda s: s.activation(out=g_s2[:, :], in_=g_m2[:, :],
                                                func=AF.Exp),
                         rd=("g_m2",), wr=("g_s2",), waits=[e_gm2])
            vop(lambda v: v.tensor_scalar(out=g_inv[:, :], in0=g_s2[:, :],
                                          scalar1=1.0, scalar2=None, op0=ALU.add),
                rd=("g_s2",), wr=("g_inv",), waits=[e_gexp])
            vop(lambda v: v.reciprocal(out=g_inv[:, :], in_=g_inv[:, :]),
                rd=("g_inv",), wr=("g_inv",))
            vop(lambda v: v.tensor_tensor(out=g_m2[:, :], in0=g_s2[:, :],
                                          in1=g_inv[:, :], op=ALU.mult),
                rd=("g_s2", "g_inv"), wr=("g_m2",))
            e_gate_vec = vop(lambda v, s8_=s8: v.tensor_tensor(
                out=g_cols[:, s8_:s8_ + 1], in0=g_inv[:, :], in1=g_m2[:, :],
                op=ALU.add),
                rd=("g_inv", "g_m2"), wr=("g_cols",))
        e_gb1 = gdma(g_bnc.ap().rearrange("(s p) -> p s", p=128),
                     g_cols[:, 0:NTSB], "sGB", waits=[e_gate_vec], slow=True)
        e_gb2 = gdma(g_row[0:1, :], g_bnc.ap(), 'sGB', waits=[e_gb1])
        if taps:
            gdma(gtap[:, :], g_row[0:1, :], 'sTP2', waits=[e_gb2])

        up0 = NCH
        e_relu = {}
        ri = 0
        rot_free2 = [e_bc2, e_bc2, None, None]
        last_h2_read = e_h2_last
        for m in range(NCH):
            si2 = up0 + m
            for nt in range(NT_D):
                w = NTW[nt]
                pp = rot_ps[ri % 4]
                e_p = None
                for ch in range(NCH):
                    wts = [e_h2[(ch, nt)]]
                    if ch == 0:
                        wts.append(e_wset[si2])
                        if rot_free2[ri % 4] is not None:
                            wts.append(rot_free2[ri % 4])
                        if ri % 4 < 2:
                            wts.append(last_h2_read)
                    e_p = mm(pp[:, 0:w], wslot(set_slot[si2], ch),
                             va[:, ch * TOK + nt * TWD: ch * TOK + nt * TWD + w],
                             start=(ch == 0), stop=(ch == NCH - 1),
                             waits=wts, inc="sT" if ch == NCH - 1 else None)
                e_relu[(m, nt)] = sop(lambda s, mm_=m, n=nt, ppp=pp, ww=w: s.activation(
                    out=big[:, mm_ * TOK + n * TWD: mm_ * TOK + n * TWD + ww],
                    in_=ppp[:, 0:ww], func=AF.Relu),
                    wr=(f"relu{m}_{nt}",), waits=[e_p, e_proj_done])
                rot_free2[ri % 4] = e_relu[(m, nt)]
                ri += 1
            slot_free[set_slot[si2]] = e_relu[(m, NT_D - 1)]
            if n_wdma < len(wsets):
                emit_wset_dma(n_wdma)
                n_wdma += 1
        e_up_done = p.last("sT")

        e_h2g = {}
        e_h2g_last = None
        for nt in range(NT_D):
            w = NTW[nt]
            wgbd = [e_gb2, e_up_done, prev_rowend2]
            if e_h2g_last is not None:
                wgbd.append(e_h2g_last)
            if e_h2tap is not None:
                wgbd.append(e_h2tap)
            e_gbd = mm(psE[:, 0:w], ones_f[:, :], g_row[0:1, nt * TWD: nt * TWD + w],
                       start=True, stop=True, waits=wgbd, inc="sT")
            for ch in range(NCH):
                e_h2g[(ch, nt)] = vop(lambda v, c=ch, n=nt, ww=w: v.tensor_tensor(
                    out=va[:, c * TOK + n * TWD: c * TOK + n * TWD + ww],
                    in0=va[:, c * TOK + n * TWD: c * TOK + n * TWD + ww],
                    in1=psE[:, 0:ww], op=ALU.mult),
                    rd=(f"h2_{ch}_{nt}",), wr=(f"h2_{ch}_{nt}",),
                    waits=[e_gbd] if ch == 0 else ())
                e_h2g_last = e_h2g[(ch, nt)]

        dn0 = up0 + NCH
        ri = 0
        rot_free3 = [rot_free2[0], rot_free2[1], rot_free2[2], rot_free2[3]]
        e_st = {}
        for m in range(NCH):
            si_d = dn0 + 2 * m
            si_e = dn0 + 2 * m + 1
            for nt in range(NT_D):
                w = NTW[nt]
                pp = rot_ps[ri % 4]
                for r in range(NCH):
                    wts = [e_relu[(r, nt)]]
                    if r == 0:
                        wts.append(e_wset[si_d])
                        if rot_free3[ri % 4] is not None:
                            wts.append(rot_free3[ri % 4])
                    mm(pp[:, 0:w], wslot(set_slot[si_d], r),
                       big[:, r * TOK + nt * TWD: r * TOK + nt * TWD + w],
                       start=(r == 0), stop=False, waits=wts)
                e_p = None
                for ch in range(NCH):
                    wts = [e_h2g[(ch, nt)]]
                    if ch == 0:
                        wts.append(e_wset[si_e])
                    e_p = mm(pp[:, 0:w], wslot(set_slot[si_e], ch),
                             va[:, ch * TOK + nt * TWD: ch * TOK + nt * TWD + w],
                             start=False, stop=(ch == NCH - 1),
                             waits=wts, inc="sT" if ch == NCH - 1 else None)
                bsl = (m * NT_D + nt) % 2
                wts = [e_p]
                if (m * NT_D + nt) >= 2:
                    wts.append(e_st[(m * NT_D + nt) - 2])
                e_ob = vop(lambda v, mm_=m, n=nt, ppp=pp, ww=w, bs=bsl:
                           v.scalar_tensor_tensor(
                               out=ob[:, bs * 1024: bs * 1024 + ww],
                               in0=ppp[:, 0:ww], scalar=m_gb[:, mm_:mm_ + 1],
                               in1=x1f[:, mm_ * TOK + n * TWD:
                                       mm_ * TOK + n * TWD + ww],
                               op0=ALU.mult, op1=ALU.add),
                           rd=("m_gb", f"x1_{m}"), wr=(f"ob{bsl}",), waits=wts)
                e_st[m * NT_D + nt] = ydma(out[m, :, nt * TWD: nt * TWD + w],
                                           ob[:, bsl * 1024: bsl * 1024 + w],
                                           "sO0" if bsl == 0 else "sO1",
                                           waits=[e_ob])
                rot_free3[ri % 4] = e_ob
                ri += 1
            slot_free[set_slot[si_d]] = e_st[m * NT_D + NT_D - 1]
            slot_free[set_slot[si_e]] = e_st[m * NT_D + NT_D - 1]
            for _ in range(2):
                if n_wdma < len(wsets):
                    emit_wset_dma(n_wdma)
                    n_wdma += 1

        for di in range(pad_pe):
            mm(psF[0:1, 0:512], ones_b[:, :], big[:, 0:512],
               start=True, stop=True,
               waits=[p.last("sO0"), p.last("sO1")] if di == 0 else ())
        p.emit("sync", lambda s: s.nop(),
               waits=[p.last("sO0"), p.last("sO1")])
        p.emit("gpsimd", lambda g: g.nop(),
               waits=[p.last("sO0"), p.last("sO1")])

        @block.tensor
        def _(t):
            p.replay("tensor", t, sems)

        @block.scalar
        def _(s):
            p.replay("scalar", s, sems)

        @block.vector
        def _(v):
            p.replay("vector", v, sems)

        @block.gpsimd
        def _(g):
            p.replay("gpsimd", g, sems)

        @block.sync
        def _(s):
            p.replay("sync", s, sems)

    return nc


def make_inputs(seq, x, w_qkv, w_proj, W_up, W_down, W_gate, W_expert, c, wmod_full):
    TOK = seq // N_CORES
    MSL = 6 * D // N_CORES
    xT = np.ascontiguousarray(x.reshape(seq, D).T)
    wq3 = w_qkv.reshape(D, 3, 16, HD)
    theta = np.float32(10000.0)
    freqs = np.float32(1.0) / theta ** (np.arange(0, HD, 2, dtype=np.float32)
                                        / np.float32(HD))
    t = np.arange(seq, dtype=np.float32)
    f = np.outer(t, freqs)
    sin = np.repeat(np.sin(f), 2, axis=1)
    cos = np.repeat(np.cos(f), 2, axis=1)
    cosT = np.ascontiguousarray(cos.T).astype(ml_dtypes.bfloat16)
    sinT = np.ascontiguousarray(sin.T).astype(ml_dtypes.bfloat16)
    pT = np.zeros((128, 128), dtype=ml_dtypes.bfloat16)
    for i in range(64):
        pT[2 * i + 1, 2 * i] = -1.0
        pT[2 * i, 2 * i + 1] = 1.0
    c_r = np.ascontiguousarray(c.reshape(16, 128).T)
    wupT = np.ascontiguousarray(W_up.T)
    wdownT = np.ascontiguousarray(W_down.T)
    wgT = np.ascontiguousarray(W_gate.T)
    in_maps = []
    for j in range(N_CORES):
        hsel = [2 * j, 2 * j + 1]
        wqkv_j = np.ascontiguousarray(np.concatenate(
            [wq3[:, 0, hsel, :].reshape(D, 2 * HD),
             wq3[:, 1, hsel, :].reshape(D, 2 * HD),
             wq3[:, 2, hsel, :].reshape(D, 2 * HD)], axis=1))
        wmod_j = np.ascontiguousarray(wmod_full[:, j * MSL:(j + 1) * MSL])
        cidx = np.zeros((128, NCH), dtype=np.uint32)
        for ch in range(NCH):
            cidx[:, ch] = (128 * ch + np.arange(128)) * N_CORES + j
        in_maps.append({
            "xT": xT, "wqkv": wqkv_j, "wproj": w_proj, "wupT": wupT,
            "wdownT": wdownT, "wexp": W_expert, "wgT": wgT, "wmod": wmod_j,
            "c_r": c_r, "cosT": cosT, "sinT": sinT, "pT": pT, "cidx_in": cidx,
        })
    return in_maps


def kernel(L0_attn_x, L0_attn_w_qkv, L0_attn_w_proj, L0_ffn_W_up, L0_ffn_W_down,
           L0_ffn_W_gate, L0_ffn_W_expert, L0_c, L0_wmod):
    seq = 8192
    TOK = seq // N_CORES
    x = np.asarray(L0_attn_x, dtype=np.float32)
    in_maps = make_inputs(seq, x,
                          np.asarray(L0_attn_w_qkv, dtype=np.float32),
                          np.asarray(L0_attn_w_proj, dtype=np.float32),
                          np.asarray(L0_ffn_W_up, dtype=np.float32),
                          np.asarray(L0_ffn_W_down, dtype=np.float32),
                          np.asarray(L0_ffn_W_gate, dtype=np.float32),
                          np.asarray(L0_ffn_W_expert, dtype=np.float32),
                          np.asarray(L0_c, dtype=np.float32),
                          np.asarray(L0_wmod, dtype=np.float32))
    nc = build_program(seq)
    res = run_bass_kernel_spmd(nc, in_maps, core_ids=list(range(N_CORES)))
    pieces = []
    for j in range(N_CORES):
        o = np.asarray(res.results[j]["out"])
        pieces.append(o.reshape(D, TOK).T)
    full = np.concatenate(pieces, axis=0)
    return np.ascontiguousarray(full.reshape(1, seq, D)).astype(np.float32)


# version 05
# speedup vs baseline: 1.5997x; 1.0042x over previous
import sys
sys.path.insert(0, "/opt/trn_rl_repo")

import contextlib
import numpy as np
import ml_dtypes

import concourse.bass as bass
import concourse.mybir as mybir
from concourse.bass import IndirectOffsetOnAxis
from concourse.bass_utils import run_bass_kernel_spmd

F32 = mybir.dt.float32
BF16 = mybir.dt.bfloat16
U32 = mybir.dt.uint32
AF = mybir.ActivationFunctionType
ALU = mybir.AluOpType
AX = mybir.AxisListType

N_CORES = 8
D = 2048
NCH = 16
HD = 128
HP = 2
TB = 512
QB = 512
KSUB = 128
KGRP = 2
INV_SQRT_HD = float(np.float32(1.0) / np.sqrt(np.float32(128.0)))
EPS = 1e-5

ENGINES = ("tensor", "scalar", "vector", "gpsimd", "sync")


class Plan:
    def __init__(self):
        self.sched = {e: [] for e in ENGINES}
        self.cnt = {}
        self.wm = {e: {} for e in ENGINES}
        self.dw = {e: set() for e in ("vector", "scalar")}
        self.dr = {e: set() for e in ("vector", "scalar")}

    def _wait(self, eng, ev):
        if ev is None:
            return
        sem, val = ev
        if self.wm[eng].get(sem, 0) >= val:
            return
        self.wm[eng][sem] = val
        self.sched[eng].append(("wait", sem, val))

    def emit(self, eng, fn, reads=(), writes=(), waits=(), inc=None, amt=1):
        for ev in waits:
            self._wait(eng, ev)
        if eng in ("vector", "scalar"):
            rs, ws = set(reads), set(writes)
            if (rs & self.dw[eng]) or (ws & (self.dw[eng] | self.dr[eng])):
                self.sched[eng].append(("drain",))
                self.dw[eng] = set()
                self.dr[eng] = set()
            self.dw[eng] |= ws
            self.dr[eng] |= rs
        ev_out = None
        if inc is not None:
            self.cnt[inc] = self.cnt.get(inc, 0) + amt
            ev_out = (inc, self.cnt[inc])
        self.sched[eng].append(("op", fn, inc, amt))
        return ev_out

    def last(self, sem):
        return (sem, self.cnt.get(sem, 0))

    def replay(self, eng, engobj, sems):
        for item in self.sched[eng]:
            if item[0] == "wait":
                engobj.wait_ge(sems[item[1]], item[2])
            elif item[0] == "drain":
                engobj.drain()
            else:
                _, fn, inc, amt = item
                inst = fn(engobj)
                if inc is not None:
                    inst.then_inc(sems[inc], amt)


def build_program(seq, taps=False, pad_pe=0, twd=512):
    NTB = seq // TB
    NQB = seq // QB
    NSB = seq // KSUB
    TOK = seq // N_CORES
    TWD = twd
    NT_D = (TOK + TWD - 1) // TWD
    NTW = [min(TWD, TOK - TWD * i) for i in range(NT_D)]
    NTSB = TOK // 128
    NG = NSB // KGRP
    MSL = 6 * D // N_CORES

    nc = bass.Bass("TRN2", target_bir_lowering=False)

    xT = nc.dram_tensor("xT", [D, seq], F32, kind="ExternalInput")
    wqkv = nc.dram_tensor("wqkv", [D, 6 * HD], F32, kind="ExternalInput")
    wproj = nc.dram_tensor("wproj", [D, D], F32, kind="ExternalInput")
    wupT = nc.dram_tensor("wupT", [D, D], F32, kind="ExternalInput")
    wdownT = nc.dram_tensor("wdownT", [D, D], F32, kind="ExternalInput")
    wexp = nc.dram_tensor("wexp", [D, D], F32, kind="ExternalInput")
    wgT = nc.dram_tensor("wgT", [D, 8], F32, kind="ExternalInput")
    wmod = nc.dram_tensor("wmod", [D, MSL], F32, kind="ExternalInput")
    c_rd = nc.dram_tensor("c_r", [128, 16], F32, kind="ExternalInput")
    cosT = nc.dram_tensor("cosT", [128, seq], BF16, kind="ExternalInput")
    sinT = nc.dram_tensor("sinT", [128, seq], BF16, kind="ExternalInput")
    pTd = nc.dram_tensor("pT", [128, 128], BF16, kind="ExternalInput")
    cidxd = nc.dram_tensor("cidx_in", [128, NCH], U32, kind="ExternalInput")

    out = nc.dram_tensor("out", [NCH, 128, TOK], F32, kind="ExternalOutput")

    mod_in = nc.dram_tensor("mod_in", [MSL], F32)
    mod_out = nc.dram_tensor("mod_out", [6 * D], F32)
    ao_in0 = nc.dram_tensor("ao_in0", [128, seq], BF16)
    ao_in1 = nc.dram_tensor("ao_in1", [128, seq], BF16)
    ao_out0 = nc.dram_tensor("ao_out0", [N_CORES * 128, seq], BF16)
    ao_out1 = nc.dram_tensor("ao_out1", [N_CORES * 128, seq], BF16)
    g_bnc = nc.dram_tensor("g_bnc", [TOK], F32)

    if taps:
        htap = nc.dram_tensor("htap", [128, NCH * TB], BF16, kind="ExternalOutput")
        qtap = nc.dram_tensor("qtap", [128, TB], BF16, kind="ExternalOutput")
        ktap = nc.dram_tensor("ktap", [128, TB], BF16, kind="ExternalOutput")
        vtap = nc.dram_tensor("vtap", [128, 256], BF16, kind="ExternalOutput")
        atap = nc.dram_tensor("atap", [128, KGRP * QB], BF16, kind="ExternalOutput")
        otap = nc.dram_tensor("otap", [128, QB], BF16, kind="ExternalOutput")
        x1tap = nc.dram_tensor("x1tap", [128, TOK], F32, kind="ExternalOutput")
        h2tap = nc.dram_tensor("h2tap", [128, TOK], BF16, kind="ExternalOutput")
        gtap = nc.dram_tensor("gtap", [1, TOK], F32, kind="ExternalOutput")
        mtap = nc.dram_tensor("mtap", [128, 16], F32, kind="ExternalOutput")
        utap = nc.dram_tensor("utap", [128, 1], F32, kind="ExternalOutput")

    p = Plan()

    with contextlib.ExitStack() as st:
        def sb(name, shape, dt):
            return st.enter_context(nc.sbuf_tensor(name, shape, dt))

        qk = sb("qk", [128, 4 * seq], BF16)
        x1f = qk.ap().bitcast(F32)
        va = sb("va", [128, 2 * seq], BF16)
        big = sb("big", [128, 20480], BF16)
        wA = sb("wA", [128, 12288], BF16)
        xsl = sb("xsl", [128, 2 * TOK], F32)
        ob = sb("ob", [128, 2 * 1024], F32)

        c_sb = sb("c_sb", [128, 16], F32)
        silu_c = sb("silu_c", [128, 16], F32)
        m_sa = sb("m_sa", [128, 16], F32)
        m_ca1 = sb("m_ca1", [128, 16], F32)
        m_ga = sb("m_ga", [128, 16], F32)
        m_sb = sb("m_sb", [128, 16], F32)
        m_cb1 = sb("m_cb1", [128, 16], F32)
        m_gb = sb("m_gb", [128, 16], F32)
        mscr1 = sb("mscr1", [128, 16], F32)
        mscr2 = sb("mscr2", [128, 16], F32)
        mod_row = sb("mod_row", [1, MSL], F32)

        row_m = sb("row_m", [1, TB], F32)
        row_r = sb("row_r", [1, TB], F32)
        row_mr = sb("row_mr", [1, TB], F32)
        row_s1 = sb("row_s1", [1, TB], F32)
        row_s2 = sb("row_s2", [1, TB], F32)
        rsb = sb("rsb", [128, QB], F32)
        lnr = sb("lnr", [128, TB], F32)
        lnmr = sb("lnmr", [128, TB], F32)
        ones_b = sb("ones_b", [128, 1], BF16)
        ones_bb = sb("ones_bb", [128, 128], BF16)
        ones_c32 = sb("ones_c32", [128, 1], F32)
        ones_f = sb("ones_f", [1, 128], F32)
        pT_sb = sb("pT_sb", [128, 128], BF16)
        cidx = sb("cidx", [128, NCH], U32)
        mx1 = sb("mx1", [1, 1], F32)
        mx2 = sb("mx2", [1, 1], F32)
        sc1 = sb("sc1", [1, 1], F32)
        negU = sb("negU", [128, 1], F32)
        g_cols = sb("g_cols", [128, NTSB], F32)
        g_d = sb("g_d", [128, 8], F32)
        g_pen = sb("g_pen", [128, 8], F32)
        g_m1 = sb("g_m1", [128, 1], F32)
        g_m2 = sb("g_m2", [128, 1], F32)
        g_s2 = sb("g_s2", [128, 1], F32)
        g_inv = sb("g_inv", [128, 1], F32)
        g_row = sb("g_row", [1, TOK], F32)

        psA = st.enter_context(nc.psum_tensor("psA", [128, 1024], F32))
        psB = st.enter_context(nc.psum_tensor("psB", [128, 1024], F32))
        psC = st.enter_context(nc.psum_tensor("psC", [128, 512], F32))
        psD = st.enter_context(nc.psum_tensor("psD", [128, 512], F32))
        psE = st.enter_context(nc.psum_tensor("psE", [128, 512], F32))
        psF = st.enter_context(nc.psum_tensor("psF", [128, 512], F32))

        sems = {n: st.enter_context(nc.semaphore(n))
                for n in ("sINIT", "sW0", "sW1", "sW2", "sW3", "sW4", "sWG",
                          "sX0", "sX1", "sB0", "sB1", "sL0", "sL1", "sAO",
                          "sMD", "sGB", "sO0", "sO1", "sTP1", "sTP2",
                          "sCC", "sT", "sS", "sV")}
        block = st.enter_context(nc.Block())

        def mm(ps_ap, lhsT_ap, rhs_ap, start, stop, waits=(), inc=None):
            return p.emit("tensor",
                          lambda t, o=ps_ap, l=lhsT_ap, r=rhs_ap, s1=start, s2=stop:
                          t.matmul(o, l, r, start=s1, stop=s2),
                          waits=waits, inc=inc)

        last_tap = [None]

        def gdma(out_ap, in_ap, sem, waits=(), slow=False):
            waits = list(waits)
            if sem.startswith("sTP"):
                if last_tap[0] is not None:
                    waits.append(last_tap[0])
            ev = p.emit("gpsimd",
                        lambda g, o=out_ap, i=in_ap, sl=slow: g.dma_start(
                            out=o, in_=i, allow_slow_non_contiguous=sl),
                        waits=waits, inc=sem, amt=16)
            if sem.startswith("sTP"):
                last_tap[0] = ev
            return ev

        def ydma(out_ap, in_ap, sem, waits=()):
            return p.emit("sync",
                          lambda s, o=out_ap, i=in_ap: s.dma_start(out=o, in_=i),
                          waits=waits, inc=sem, amt=16)

        def vop(fn, rd=(), wr=(), waits=(), inc="sV"):
            return p.emit("vector", fn, reads=rd, writes=wr, waits=waits, inc=inc)

        def sop(fn, rd=(), wr=(), waits=(), inc="sS"):
            return p.emit("scalar", fn, reads=rd, writes=wr, waits=waits, inc=inc)

        vop(lambda v: v.memset(ones_b[:, :], 1.0), wr=("ones_b",))
        vop(lambda v: v.memset(ones_bb[:, :], 1.0), wr=("ones_bb",))
        vop(lambda v: v.memset(ones_c32[:, :], 1.0), wr=("ones_c32",))
        vop(lambda v: v.memset(ones_f[:, :], 1.0), wr=("ones_f",))
        vop(lambda v: v.memset(mx1[:, :], 0.0), wr=("mx1",))
        e_const = vop(lambda v: v.memset(mx2[:, :], 0.0), wr=("mx2",))
        gdma(pT_sb[:, :], pTd[:, :], 'sINIT')
        gdma(cidx[:, :], cidxd[:, :], 'sINIT')

        def xt_sl(b, ch, lo=0, w=TB):
            o = b * 8192 + ch * TB + lo
            return big[:, o:o + w]

        def tab_sl(b, is_sin):
            o = 16384 + b * 1024 + (512 if is_sin else 0)
            return big[:, o:o + TB]

        def sq_sl(s):
            return big[:, 18432 + s * 512: 18432 + (s + 1) * 512]

        def scr_sl(s):
            return big[:, 19456 + s * 512: 19456 + (s + 1) * 512]

        e_xt = [None] * (NTB + 2)
        e_tab = [None] * (NTB + 2)
        def r3(ap, cdim):
            return ap.rearrange("p (c t) -> p c t", c=cdim)

        e_xt[0] = gdma(r3(xt_sl(0, 0, 0, NCH * TB), NCH),
                       xT[:, 0:TB].rearrange("(c p) t -> p c t", p=128), "sX0")
        ydma(tab_sl(0, False), cosT[:, 0:TB], "sB0")
        e_tab[0] = ydma(tab_sl(0, True), sinT[:, 0:TB], "sB0")

        e_init = e_c = e_pt = e_cidx = gdma(c_sb[:, :], c_rd[:, :], 'sINIT')
        e_silu = sop(lambda s: s.activation(out=silu_c[:, :], in_=c_sb[:, :],
                                            func=AF.Silu),
                     rd=("c_sb",), wr=("silu_c",), waits=[e_c])
        NMT = MSL // 512
        mod_ps = [psA[0:1, 0:512], psA[0:1, 512:1024], psB[0:1, 0:512]]
        wAf = wA.ap().bitcast(F32)
        e_modmm = None
        e_wq_ev = None
        for qtr in range(4):
            wts = [e_wq_ev] if e_wq_ev is not None else []
            e_wq_ev = gdma(wAf[:, 0:4 * MSL].rearrange("p (c m) -> p c m", c=4),
                           wmod[qtr * 512:(qtr + 1) * 512, :].rearrange(
                               "(c p) m -> p c m", p=128),
                           "sW0", waits=wts)
            e_q = None
            for ch in range(4):
                gch = qtr * 4 + ch
                for ntl in range(NMT):
                    w = [e_wq_ev, e_silu] if (ch == 0 and ntl == 0) else ()
                    e_q = mm(mod_ps[ntl], silu_c[:, gch:gch + 1],
                             wAf[:, ch * MSL + ntl * 512: ch * MSL + (ntl + 1) * 512],
                             start=(gch == 0), stop=(gch == 15), waits=w,
                             inc="sT" if (ch == 3 and ntl == NMT - 1) else None)
            e_wq_ev = e_q
        e_modmm = e_q
        e_mr = None
        for ntl in range(NMT):
            e_mr = vop(lambda v, n=ntl: v.tensor_copy(
                out=mod_row[:, n * 512:(n + 1) * 512], in_=mod_ps[n]),
                wr=("mod_row",), waits=[e_modmm])
        e_mi = gdma(mod_in.ap(), mod_row[:, :], 'sW0', waits=[e_mr])
        e_ag1 = p.emit("gpsimd",
                       lambda g: g.collective_compute(
                           "AllGather", ALU.bypass,
                           replica_groups=[list(range(N_CORES))],
                           ins=[mod_in.ap().opt()], outs=[mod_out.ap().opt()]),
                       waits=[e_mi], inc="sCC")
        mtiles = [m_sa, m_ca1, m_ga, m_sb, m_cb1, m_gb]
        mtags = ["m_sa", "m_ca1", "m_ga", "m_sb", "m_cb1", "m_gb"]
        e_md_last = None
        for vi in range(6):
            e_md_last = gdma(mtiles[vi][:, :],
                             mod_out[vi * D:(vi + 1) * D].rearrange("(c p) -> p c", p=128),
                             "sMD", waits=[e_ag1], slow=True)
        e_md = [e_md_last] * 6
        vop(lambda v: v.tensor_scalar(out=m_ca1[:, :], in0=m_ca1[:, :], scalar1=1.0,
                                      scalar2=None, op0=ALU.add),
            rd=("m_ca1",), wr=("m_ca1",), waits=[e_md[1]])
        vop(lambda v: v.tensor_scalar(out=m_cb1[:, :], in0=m_cb1[:, :], scalar1=1.0,
                                      scalar2=None, op0=ALU.add),
            rd=("m_cb1",), wr=("m_cb1",), waits=[e_md[4]])
        for gt, tg, ei in ((m_ga, "m_ga", e_md[2]), (m_gb, "m_gb", e_md[5])):
            vop(lambda v, g=gt: v.tensor_scalar(out=mscr1[:, :], in0=g[:, :],
                                                scalar1=0.0, scalar2=None, op0=ALU.is_ge),
                rd=(tg,), wr=("mscr1",), waits=[ei])
            vop(lambda v, g=gt: v.tensor_scalar(out=mscr2[:, :], in0=g[:, :],
                                                scalar1=0.25, scalar2=None, op0=ALU.max),
                rd=(tg,), wr=("mscr2",))
            vop(lambda v, g=gt: v.tensor_scalar(out=g[:, :], in0=g[:, :],
                                                scalar1=-0.25, scalar2=None, op0=ALU.min),
                rd=(tg,), wr=(tg,))
            vop(lambda v, g=gt: v.tensor_tensor(out=mscr2[:, :], in0=mscr2[:, :],
                                                in1=g[:, :], op=ALU.subtract),
                rd=("mscr2", tg), wr=("mscr2",))
            vop(lambda v: v.tensor_tensor(out=mscr2[:, :], in0=mscr1[:, :],
                                          in1=mscr2[:, :], op=ALU.mult),
                rd=("mscr1", "mscr2"), wr=("mscr2",))
            vop(lambda v, g=gt: v.tensor_tensor(out=g[:, :], in0=g[:, :],
                                                in1=mscr2[:, :], op=ALU.add),
                rd=(tg, "mscr2"), wr=(tg,))
        e_mod_rdy = p.last("sV")
        if taps:
            gdma(mtap[:, :], m_ga[:, :], "sTP2", waits=[e_mod_rdy])

        if NTB > 1:
            e_xt[1] = gdma(r3(xt_sl(1, 0, 0, NCH * TB), NCH),
                           xT[:, TB:2 * TB].rearrange("(c p) t -> p c t", p=128),
                           "sX1")
            ydma(tab_sl(1, False), cosT[:, TB:2 * TB], "sB1")
            e_tab[1] = ydma(tab_sl(1, True), sinT[:, TB:2 * TB], "sB1")
        e_wqkv = gdma(r3(wA[:, 0:NCH * 768], NCH),
                      wqkv[:, :].rearrange("(c p) m -> p c m", p=128),
                      "sW0", waits=[e_modmm])

        WQ = [wA[:, ch * 768:ch * 768 + 256] for ch in range(NCH)]
        WK = [wA[:, ch * 768 + 256:ch * 768 + 512] for ch in range(NCH)]
        WV = [wA[:, ch * 768 + 512:ch * 768 + 768] for ch in range(NCH)]

        qk_ps = [psB[:, 0:512], psB[:, 512:1024], psC[:, :], psD[:, :]]
        prev_psread = [None] * 4
        prev_hlast = None
        prev_rowend = None
        prev_umax = None
        e_vdone_last = None
        e_hlast_final = None

        for i in range(NTB):
            b = i % 2
            e_smm_last = None
            e_smm_by = {}
            for ch in range(NCH):
                ssl = ch % 2
                wsq = []
                if ch < 2:
                    wsq = [e_xt[i], e_const]
                    if prev_umax is not None:
                        wsq.append(prev_umax)
                else:
                    wsq = [e_smm_by[ch - 2]]
                e_sq = sop(lambda s, c=ch, bb=b, sl=ssl: s.activation(
                    out=sq_sl(sl), in_=xt_sl(bb, c), func=AF.Square),
                    rd=(f"xt{b}_{ch}",), wr=(f"sq{ssl}",), waits=wsq)
                w0 = []
                if ch == 0:
                    w0 = [e_xt[i], e_const]
                    if prev_rowend is not None:
                        w0.append(prev_rowend)
                    if prev_umax is not None:
                        w0.append(prev_umax)
                mm(psE[:, :], ones_bb[:, :], xt_sl(b, ch),
                   start=(ch == 0), stop=(ch == NCH - 1), waits=w0)
                e_smm_last = e_smm_by[ch] = mm(psF[:, :], ones_bb[:, :], sq_sl(ssl),
                                               start=(ch == 0), stop=(ch == NCH - 1),
                                               waits=[e_sq], inc="sT")
            vop(lambda v: v.tensor_scalar(out=lnmr[:, :], in0=psE[:, :],
                                          scalar1=1.0 / D, scalar2=None, op0=ALU.mult),
                wr=("lnmr",), waits=[e_smm_last])
            vop(lambda v: v.tensor_tensor(out=lnr[:, :], in0=lnmr[:, :],
                                          in1=lnmr[:, :], op=ALU.mult),
                rd=("lnmr",), wr=("lnr",))
            vop(lambda v: v.scalar_tensor_tensor(
                out=lnr[:, :], in0=psF[:, :], scalar=1.0 / D, in1=lnr[:, :],
                op0=ALU.mult, op1=ALU.subtract),
                rd=("lnr",), wr=("lnr",))
            vop(lambda v: v.tensor_scalar(out=lnr[:, :], in0=lnr[:, :],
                                          scalar1=EPS, scalar2=None, op0=ALU.add),
                rd=("lnr",), wr=("lnr",))
            e_rcp = vop(lambda v: v.reciprocal(out=lnr[:, :], in_=lnr[:, :]),
                        rd=("lnr",), wr=("lnr",))
            prev_rowend = e_rcp
            e_sqrt = sop(lambda s: s.activation(out=lnr[:, :], in_=lnr[:, :],
                                                func=AF.Sqrt),
                         rd=("lnr",), wr=("lnr",), waits=[e_rcp])
            e_rows = vop(lambda v: v.scalar_tensor_tensor(
                out=lnmr[:, :], in0=lnmr[:, :], scalar=-1.0, in1=lnr[:, :],
                op0=ALU.mult, op1=ALU.mult),
                rd=("lnmr", "lnr"), wr=("lnmr",), waits=[e_sqrt])
            e_h = {}
            e_scr = {}
            for ch in range(NCH):
                ssl = ch % 2
                w0 = [e_rows]
                if i == 0 and ch == 0:
                    w0.append(e_mod_rdy)
                vop(lambda v, c=ch, bb=b, sl=ssl: v.tensor_tensor(
                    out=scr_sl(sl), in0=xt_sl(bb, c), in1=lnr[:, :], op=ALU.mult),
                    rd=(f"xt{b}_{ch}", "lnr"), wr=(f"scr{ssl}",), waits=w0)
                e_scr[ch] = vop(lambda v, c=ch, bb=b, sl=ssl: v.tensor_tensor(
                    out=xt_sl(bb, c), in0=scr_sl(sl), in1=lnmr[:, :], op=ALU.add),
                    rd=(f"scr{ssl}", "lnmr"), wr=(f"xt{b}_{ch}",))
                wh = [e_scr[ch]]
                if i == 0 and ch == 0:
                    wh.append(e_mod_rdy)
                e_h[ch] = sop(lambda s, c=ch, bb=b: s.activation(
                    out=xt_sl(bb, c), in_=xt_sl(bb, c), func=AF.Identity,
                    scale=m_ca1[:, c:c + 1], bias=m_sa[:, c:c + 1]),
                    rd=(f"xt{b}_{ch}", "m_ca1", "m_sa"), wr=(f"xt{b}_{ch}",),
                    waits=wh)
            e_hlast_final = e_h[NCH - 1]
            if taps and i == 0:
                gdma(htap[:, :], xt_sl(b, 0, 0, NCH * TB), 'sTP2', waits=[e_h[NCH - 1]])
            e_qk_mm = [None] * 4
            for ch in range(NCH):
                for mt in range(4):
                    lhs = (WQ[ch][:, (mt % 2) * 128:(mt % 2 + 1) * 128] if mt < 2
                           else WK[ch][:, (mt - 2) * 128:(mt - 1) * 128])
                    w0 = []
                    if ch == 0:
                        w0 = [e_h[ch], e_wqkv]
                        if prev_psread[mt] is not None:
                            w0.append(prev_psread[mt])
                    else:
                        w0 = [e_h[ch]]
                    e_qk_mm[mt] = mm(qk_ps[mt], lhs, xt_sl(b, ch),
                                     start=(ch == 0), stop=(ch == NCH - 1),
                                     waits=w0,
                                     inc="sT" if ch == NCH - 1 else None)
            e_vdone = None
            for sbi in range(TB // KSUB):
                e_vmm = None
                for ch in range(NCH):
                    w0 = []
                    if ch == 0:
                        w0 = [e_rcp, e_h[NCH - 1]]
                        if e_vdone is not None:
                            w0.append(e_vdone)
                    e_vmm = mm(psF[:, 0:256],
                               xt_sl(b, ch, sbi * KSUB, KSUB), WV[ch],
                               start=(ch == 0), stop=(ch == NCH - 1),
                               waits=w0, inc="sT" if ch == NCH - 1 else None)
                gsb = i * (TB // KSUB) + sbi
                e_vdone = sop(lambda s, g=gsb: s.activation(
                    out=va[:, g * 256:(g + 1) * 256], in_=psF[:, 0:256],
                    func=AF.Identity),
                    wr=(f"v{gsb}",), waits=[e_vmm])
                if taps and gsb == 0:
                    gdma(vtap[:, :], va[:, 0:256], 'sTP2', waits=[e_vdone])
            e_vdone_last = e_vdone
            prev_r3 = None
            prev_mx = None
            for mt in range(4):
                h = mt % 2
                isq = mt < 2
                base = (0 if isq else 2 * seq) + h * seq + i * TB
                dst = qk[:, base:base + TB]
                tag = f"qk_m{mt}_{i}"
                wev = [e_qk_mm[mt], e_vdone]
                if prev_r3 is not None:
                    wev.append(prev_r3)
                e_ev = sop(lambda s, ps=qk_ps[mt]: s.activation(
                    out=scr_sl(1), in_=ps, func=AF.Identity),
                    wr=("scr1",), waits=wev)
                e_rot = mm(qk_ps[mt], pT_sb[:, :], scr_sl(1), start=True, stop=True,
                           waits=[e_ev, e_pt], inc="sT")
                vop(lambda v, d=dst, bb=b: v.tensor_tensor(
                    out=d, in0=scr_sl(1), in1=tab_sl(bb, False), op=ALU.mult),
                    rd=("scr1", f"tab{b}"), wr=(tag,),
                    waits=[e_rot, e_tab[i]])
                e_psr = vop(lambda v, ps=qk_ps[mt], bb=b: v.tensor_tensor(
                    out=scr_sl(1), in0=ps, in1=tab_sl(bb, True), op=ALU.mult),
                    rd=(f"tab{b}",), wr=("scr1",))
                prev_psread[mt] = e_psr
                e_r3 = vop(lambda v, d=dst: v.tensor_tensor(
                    out=d, in0=d, in1=scr_sl(1), op=ALU.add),
                    rd=(tag, "scr1"), wr=(tag,))
                prev_r3 = e_r3
                e_sq2 = vop(lambda v, d=dst: v.tensor_tensor(
                    out=sq_sl(0), in0=d, in1=d, op=ALU.mult),
                    rd=(tag,), wr=("sq0",), waits=[e_smm_last])
                wn = [e_sq2]
                if prev_mx is not None:
                    wn.append(prev_mx)
                else:
                    wn.append(e_rcp)
                    if prev_umax is not None:
                        wn.append(prev_umax)
                e_nmm = mm(psE[0:1, :], ones_b[:, :], sq_sl(0), start=True, stop=True,
                           waits=wn, inc="sT")
                mxt, mxn = (mx1, "mx1") if isq else (mx2, "mx2")
                vop(lambda v: v.reduce_max(out=sc1[:, :], in_=psE[0:1, :], axis=AX.X),
                    rd=(), wr=("sc1",), waits=[e_nmm])
                e_mx = vop(lambda v, m=mxt: v.tensor_tensor(
                    out=m[:, :], in0=m[:, :], in1=sc1[:, :], op=ALU.max),
                    rd=(mxn, "sc1"), wr=(mxn,))
                prev_mx = e_mx
                if taps and i == 0 and mt == 0:
                    gdma(qtap[:, :], dst, 'sTP2', waits=[e_r3])
                if taps and i == 0 and mt == 2:
                    gdma(ktap[:, :], dst, 'sTP2', waits=[e_r3])
            prev_umax = prev_mx
            j = i + 2
            if j < NTB:
                e_qkv_done = p.last("sT")
                e_xt[j] = gdma(r3(xt_sl(j % 2, 0, 0, NCH * TB), NCH),
                               xT[:, j * TB:(j + 1) * TB].rearrange(
                                   "(c p) t -> p c t", p=128),
                               "sX0" if j % 2 == 0 else "sX1",
                               waits=[e_qkv_done])
                ydma(tab_sl(j % 2, False), cosT[:, j * TB:(j + 1) * TB],
                     "sB0" if j % 2 == 0 else "sB1", waits=[prev_r3])
                e_tab[j] = ydma(tab_sl(j % 2, True), sinT[:, j * TB:(j + 1) * TB],
                                "sB0" if j % 2 == 0 else "sB1")

        e_u1 = vop(lambda v: v.tensor_tensor(out=sc1[:, :], in0=mx1[:, :],
                                             in1=mx2[:, :], op=ALU.mult),
                   rd=("mx1", "mx2"), wr=("sc1",), waits=[prev_umax])
        e_u2 = sop(lambda s: s.activation(out=sc1[:, :], in_=sc1[:, :], func=AF.Sqrt),
                   rd=("sc1",), wr=("sc1",), waits=[e_u1])
        e_u3 = vop(lambda v: v.tensor_scalar(out=sc1[:, :], in0=sc1[:, :],
                                             scalar1=-INV_SQRT_HD, scalar2=None,
                                             op0=ALU.mult),
                   rd=("sc1",), wr=("sc1",), waits=[e_u2])
        e_u5 = mm(psF[:, 0:1], ones_f[:, :], sc1[:, :], start=True, stop=True,
                  waits=[e_u3, e_vdone_last], inc="sT")
        e_negU = vop(lambda v: v.tensor_copy(out=negU[:, :], in_=psF[:, 0:1]),
                     wr=("negU",), waits=[e_u5])
        if taps:
            gdma(utap[:, :], negU[:, :], 'sTP2', waits=[e_negU])

        e_taps_preB = p.last("sTP2") if taps else None

        RING = 5

        def wslot(s, ch=None, w=128):
            if ch is None:
                return wA[:, s * 2048:(s + 1) * 2048]
            return wA[:, s * 2048 + ch * w: s * 2048 + (ch + 1) * w]

        wsets = ([("proj", m) for m in range(NCH)]
                 + [("up", m) for m in range(NCH)])
        for m in range(NCH):
            wsets.append(("down", m))
            wsets.append(("exp", m))
        wsrc = {"proj": wproj, "up": wupT, "down": wdownT, "exp": wexp}
        slot_free = [None] * RING
        e_wset = [None] * len(wsets)
        set_slot = {}
        e_qkv_all_done = p.last("sT")
        e_wgate = gdma(r3(wA[:, 10240:10240 + NCH * 8], NCH),
                       wgT[:, :].rearrange("(c p) m -> p c m", p=128),
                       "sWG", waits=[e_qkv_all_done])

        def emit_wset_dma(si):
            kind, m = wsets[si]
            s = si % RING
            set_slot[si] = s
            wts = []
            if si < RING:
                wts.append(e_qkv_all_done)
            if slot_free[s] is not None:
                wts.append(slot_free[s])
            src = wsrc[kind][:, m * 128:(m + 1) * 128].rearrange(
                "(c p) m -> p c m", p=128)
            e_wset[si] = gdma(r3(wslot(s), NCH), src, f"sW{s}", wts)

        n_wdma = 0
        for _ in range(min(RING, len(wsets))):
            emit_wset_dma(n_wdma)
            n_wdma += 1

        def a_sl(s, lo=0, w=KGRP * QB):
            o = 16384 + s * 1024 + lo
            return big[:, o:o + w]

        def ao_loc(ch, lo=0, w=None):
            if w is None:
                w = TOK
            return big[:, ch * TOK + lo: ch * TOK + lo + w]

        e_aog = [None] * NCH
        ao_r0 = ao_out0.ap().rearrange("d (k t) -> (d k) t", k=N_CORES)
        ao_r1 = ao_out1.ap().rearrange("d (k t) -> (d k) t", k=N_CORES)

        e_ao = {}
        prev_exp2 = [None, None]
        prev_grp2 = [None, None]
        prev_av_evt = {0: None, 1: None}
        prev_rs2 = [None, None]
        it = 0
        for h in range(HP):
            for qb in range(NQB):
                par = it % 2
                avbk = psC if par == 0 else psD
                smbk = psE if par == 0 else psF
                e_exp = {}
                e_grp = {}
                e_s2_by = {}

                def emit_scores(g, h=h, qb=qb, it=it, e_exp=e_exp,
                                e_s2_by=e_s2_by):
                    sps = psA if g % 2 == 0 else psB
                    wts = []
                    if g >= 2:
                        wts.append(e_exp[g - 2])
                    else:
                        if prev_exp2[g % 2] is not None:
                            wts.append(prev_exp2[g % 2])
                        if it == 0:
                            wts.append(e_hlast_final)
                            wts.append(e_mr)
                            for ev in prev_psread:
                                if ev is not None:
                                    wts.append(ev)
                    e_s2 = None
                    for kk in range(KGRP):
                        kb = g * KGRP + kk
                        lhs = qk[:, 2 * seq + h * seq + kb * KSUB:
                                 2 * seq + h * seq + (kb + 1) * KSUB]
                        rhs = qk[:, h * seq + qb * QB: h * seq + (qb + 1) * QB]
                        e_s2 = mm(sps[:, kk * 512:(kk + 1) * 512], lhs, rhs,
                                  start=True, stop=True,
                                  waits=wts if kk == 0 else (),
                                  inc="sT" if kk == KGRP - 1 else None)
                    e_s2_by[g] = e_s2

                emit_scores(0)
                emit_scores(1)
                for g in range(NG):
                    sps = psA if g % 2 == 0 else psB
                    wts2 = [e_s2_by[g], e_negU]
                    if g >= 2:
                        wts2.append(e_grp[g - 2])
                    elif prev_grp2[g % 2] is not None:
                        wts2.append(prev_grp2[g % 2])
                    if it == 0 and g < 2:
                        wts2.append(prev_r3)
                    if taps and it == 0 and g == 2:
                        wts2.append(e_atap)
                    e_exp[g] = sop(lambda s, sl=g % 2, ps=sps: s.activation(
                        out=a_sl(sl), in_=ps[:, 0:KGRP * 512], func=AF.Exp,
                        scale=INV_SQRT_HD, bias=negU[:, 0:1]),
                        rd=("negU",), waits=wts2)
                    if g + 2 < NG:
                        emit_scores(g + 2)
                    for kk in range(KGRP):
                        kb = g * KGRP + kk
                        wv = []
                        if kk == 0:
                            wv = [e_exp[g]]
                            if g == 0:
                                if prev_av_evt[par] is not None:
                                    wv.append(prev_av_evt[par])
                                if it == 0:
                                    wv.append(e_negU)
                                if it == 1:
                                    wv.append(e_negU)
                        mm(avbk[:, :],
                           va[:, kb * 256 + h * 128: kb * 256 + (h + 1) * 128],
                           a_sl(g % 2, kk * 512, 512),
                           start=(kb == 0), stop=(kb == NSB - 1), waits=wv)
                        wg = []
                        if kb == 0 and prev_rs2[par] is not None:
                            wg = [prev_rs2[par]]
                        e_grp[g] = mm(smbk[:, :], ones_bb[:, :],
                                      a_sl(g % 2, kk * 512, 512),
                                      start=(kb == 0), stop=(kb == NSB - 1),
                                      waits=wg,
                                      inc="sT" if kk == KGRP - 1 else None)
                    if taps and it == 0 and g == 0:
                        e_atap = gdma(atap[:, :], a_sl(0), 'sTP1', waits=[e_grp[0]])
                prev_exp2 = [e_exp[NG - 2], e_exp[NG - 1]]
                prev_grp2 = [e_grp[NG - 2], e_grp[NG - 1]]
                e_rs = vop(lambda v, sk=smbk: v.reciprocal(out=rsb[:, :],
                                                           in_=sk[:, :]),
                           rd=(), wr=("rsb",), waits=[e_grp[NG - 1]])
                prev_rs2[par] = e_rs
                wao = []
                if taps and it == 0 and e_taps_preB is not None:
                    wao.append(e_taps_preB)
                e_ao[(h, qb)] = vop(lambda v, hh=h, q=qb, ab=avbk: v.tensor_tensor(
                    out=big[:, hh * seq + q * QB: hh * seq + (q + 1) * QB],
                    in0=ab[:, :], in1=rsb[:, :], op=ALU.mult),
                    rd=("rsb",), wr=(f"ao{h}_{qb}",), waits=wao)
                prev_av_evt[par] = e_ao[(h, qb)]
                if taps and h == 0 and qb == 0:
                    gdma(otap[:, :], big[:, 0:QB], 'sTP2', waits=[e_ao[(0, 0)]])
                it += 1
            if h == 0:
                e_aow0 = gdma(ao_in0.ap(), big[:, 0:seq], "sAO",
                              waits=[e_ao[(0, q)] for q in range(NQB)])
                e_ag1 = p.emit("gpsimd",
                               lambda g: g.collective_compute(
                                   "AllGather", ALU.bypass,
                                   replica_groups=[list(range(N_CORES))],
                                   ins=[ao_in0.ap().opt()],
                                   outs=[ao_out0.ap().opt()]),
                               waits=[e_aow0], inc="sCC")
                for ch in (0, 2, 4, 6):
                    e_aog[ch] = p.emit(
                        "gpsimd",
                        lambda g, c=ch: g.indirect_dma_start(
                            out=ao_loc(c), out_offset=None, in_=ao_r0,
                            in_offset=IndirectOffsetOnAxis(
                                ap=cidx[:, c // 2:c // 2 + 1], axis=0)),
                        waits=[e_ag1, e_init] if ch == 0 else [e_ag1],
                        inc="sAO", amt=16)
            else:
                e_aow1 = gdma(ao_in1.ap(), big[:, seq:2 * seq], "sAO",
                              waits=[e_ao[(1, q)] for q in range(NQB)])
                for ch in (8, 10, 12, 14):
                    e_aog[ch] = p.emit(
                        "gpsimd",
                        lambda g, c=ch: g.indirect_dma_start(
                            out=ao_loc(c), out_offset=None, in_=ao_r0,
                            in_offset=IndirectOffsetOnAxis(
                                ap=cidx[:, c // 2:c // 2 + 1], axis=0)),
                        waits=[e_aow1, e_ag1], inc="sAO", amt=16)
                e_ag2 = p.emit("gpsimd",
                               lambda g: g.collective_compute(
                                   "AllGather", ALU.bypass,
                                   replica_groups=[list(range(N_CORES))],
                                   ins=[ao_in1.ap().opt()],
                                   outs=[ao_out1.ap().opt()]),
                               waits=[e_aow1], inc="sCC")

        xT_r = xT.ap().rearrange("d (k t) -> (d k) t", k=N_CORES)
        e_xsl = [None] * NCH
        for m in range(2):
            e_xsl[m] = p.emit(
                "gpsimd",
                lambda g, mm_=m: g.indirect_dma_start(
                    out=xsl[:, (mm_ % 2) * TOK:(mm_ % 2) * TOK + TOK],
                    out_offset=None, in_=xT_r,
                    in_offset=IndirectOffsetOnAxis(ap=cidx[:, mm_:mm_ + 1], axis=0)),
                waits=[e_init] if m == 0 else (),
                inc="sL0" if m % 2 == 0 else "sL1", amt=16)

        e_taps_preD = p.last("sTP2") if taps else None
        for ch in (1, 3, 5, 7, 9, 11, 13, 15):
            w0 = [e_ag2]
            if taps and ch == 1 and e_taps_preD is not None:
                w0.append(e_taps_preD)
            e_aog[ch] = p.emit(
                "gpsimd",
                lambda g, c=ch: g.indirect_dma_start(
                    out=ao_loc(c), out_offset=None, in_=ao_r1,
                    in_offset=IndirectOffsetOnAxis(
                        ap=cidx[:, c // 2:c // 2 + 1], axis=0)),
                waits=w0, inc="sAO", amt=16)

        rot_ps = [psA[:, 0:512], psA[:, 512:1024], psB[:, 0:512], psB[:, 512:1024]]
        rot_free = [prev_exp2[0], prev_exp2[0], prev_exp2[1], prev_exp2[1]]
        e_pd_start = [prev_exp2[0], prev_exp2[1]]

        proj_ch_order = [0, 2, 4, 6, 8, 10, 12, 14, 1, 3, 5, 7, 9, 11, 13, 15]
        si = 0
        e_x1 = {}
        ri = 0
        for m in range(NCH):
            for nt in range(NT_D):
                w = NTW[nt]
                pp = rot_ps[ri % 4]
                e_p = None
                for ci, ch in enumerate(proj_ch_order):
                    wts = []
                    if ci == 0:
                        wts = [e_wset[si], e_aog[ch]]
                        if rot_free[ri % 4] is not None:
                            wts.append(rot_free[ri % 4])
                    else:
                        wts = [e_aog[ch]]
                    e_p = mm(pp[:, 0:w], wslot(set_slot[si], ch),
                             ao_loc(ch, nt * TWD, w),
                             start=(ci == 0), stop=(ci == NCH - 1),
                             waits=wts, inc="sT" if ci == NCH - 1 else None)
                wts = [e_p, e_xsl[m], e_mod_rdy] + [ev for ev in e_pd_start if ev]
                if taps and e_taps_preD is not None:
                    wts.append(e_taps_preD)
                e_x1[(m, nt)] = vop(
                    lambda v, mm_=m, n=nt, ppp=pp, ww=w: v.scalar_tensor_tensor(
                        out=x1f[:, mm_ * TOK + n * TWD: mm_ * TOK + n * TWD + ww],
                        in0=ppp[:, 0:ww], scalar=m_ga[:, mm_:mm_ + 1],
                        in1=xsl[:, (mm_ % 2) * TOK + n * TWD:
                                (mm_ % 2) * TOK + n * TWD + ww],
                        op0=ALU.mult, op1=ALU.add),
                    rd=("m_ga", f"xsl{m % 2}"), wr=(f"x1_{m}",),
                    waits=wts)
                rot_free[ri % 4] = e_x1[(m, nt)]
                ri += 1
            if m + 2 < NCH:
                e_xsl[m + 2] = p.emit(
                    "gpsimd",
                    lambda g, mm_=m + 2: g.indirect_dma_start(
                        out=xsl[:, (mm_ % 2) * TOK:(mm_ % 2) * TOK + TOK],
                        out_offset=None, in_=xT_r,
                        in_offset=IndirectOffsetOnAxis(ap=cidx[:, mm_:mm_ + 1],
                                                       axis=0)),
                    waits=[e_x1[(m, NT_D - 1)]],
                    inc="sL0" if m % 2 == 0 else "sL1", amt=16)
            slot_free[set_slot[si]] = e_x1[(m, NT_D - 1)]
            si += 1
            if n_wdma < len(wsets):
                emit_wset_dma(n_wdma)
                n_wdma += 1
        e_proj_done = p.last("sT")
        if taps:
            gdma(x1tap[:, :], x1f[:, 0:TOK], 'sTP2', waits=[e_x1[(0, NT_D - 1)]])

        prev_rowend2 = None
        e_h2 = {}
        e_h2_last = None
        for nt in range(NT_D):
            w = NTW[nt]
            e_smm2 = None
            e_smm2_by = {}
            for ch in range(NCH):
                ssl = ch % 2
                wsq = [e_x1[(ch, nt)]]
                if ch >= 2:
                    wsq.append(e_smm2_by[ch - 2])
                e_sq = vop(lambda v, c=ch, n=nt, s=ssl, ww=w: v.tensor_tensor(
                    out=sq_sl(s)[:, 0:ww],
                    in0=x1f[:, c * TOK + n * TWD: c * TOK + n * TWD + ww],
                    in1=x1f[:, c * TOK + n * TWD: c * TOK + n * TWD + ww],
                    op=ALU.mult),
                    rd=(f"x1_{ch}",), wr=(f"sq{ssl}",),
                    waits=wsq)
                wts = []
                if ch == 0:
                    wts = [e_x1[(ch, nt)]]
                    if prev_rowend2 is not None:
                        wts.append(prev_rowend2)
                    else:
                        wts.append(e_ao[(HP - 1, NQB - 1)])
                        wts.append(e_ao[(HP - 1, NQB - 2)])
                        wts.append(e_negU)
                mm(psE[0:1, 0:w], ones_c32[:, :],
                   x1f[:, ch * TOK + nt * TWD: ch * TOK + nt * TWD + w],
                   start=(ch == 0), stop=(ch == NCH - 1), waits=wts)
                e_smm2 = e_smm2_by[ch] = mm(psF[0:1, 0:w], ones_b[:, :],
                                            sq_sl(ssl)[:, 0:w],
                                            start=(ch == 0), stop=(ch == NCH - 1),
                                            waits=[e_sq], inc="sT")
            vop(lambda v, ww=w: v.tensor_scalar(out=row_m[:, 0:ww], in0=psE[0:1, 0:ww],
                                                scalar1=1.0 / D, scalar2=None,
                                                op0=ALU.mult),
                wr=("row_m",), waits=[e_smm2])
            vop(lambda v, ww=w: v.tensor_scalar(out=row_s1[:, 0:ww], in0=psF[0:1, 0:ww],
                                                scalar1=1.0 / D, scalar2=None,
                                                op0=ALU.mult),
                wr=("row_s1",))
            vop(lambda v, ww=w: v.tensor_tensor(out=row_s2[:, 0:ww], in0=row_m[:, 0:ww],
                                                in1=row_m[:, 0:ww], op=ALU.mult),
                rd=("row_m",), wr=("row_s2",))
            vop(lambda v, ww=w: v.tensor_tensor(out=row_s1[:, 0:ww], in0=row_s1[:, 0:ww],
                                                in1=row_s2[:, 0:ww], op=ALU.subtract),
                rd=("row_s1", "row_s2"), wr=("row_s1",))
            vop(lambda v, ww=w: v.tensor_scalar(out=row_s1[:, 0:ww], in0=row_s1[:, 0:ww],
                                                scalar1=EPS, scalar2=None, op0=ALU.add),
                rd=("row_s1",), wr=("row_s1",))
            e_rcp2 = vop(lambda v, ww=w: v.reciprocal(out=row_s2[:, 0:ww],
                                                      in_=row_s1[:, 0:ww]),
                         rd=("row_s1",), wr=("row_s2",))
            prev_rowend2 = e_rcp2
            e_sqrt2 = sop(lambda s, ww=w: s.activation(out=row_r[:, 0:ww],
                                                       in_=row_s2[:, 0:ww],
                                                       func=AF.Sqrt),
                          rd=("row_s2",), wr=("row_r",), waits=[e_rcp2])
            e_rows2 = vop(lambda v, ww=w: v.scalar_tensor_tensor(
                out=row_mr[:, 0:ww], in0=row_m[:, 0:ww], scalar=-1.0,
                in1=row_r[:, 0:ww], op0=ALU.mult, op1=ALU.mult),
                rd=("row_m", "row_r"), wr=("row_mr",), waits=[e_sqrt2])
            wbc = [e_rows2, e_proj_done]
            if e_h2_last is not None:
                wbc.append(e_h2_last)
            mm(psA[:, 0:512][:, 0:w], ones_f[:, :], row_r[0:1, 0:w],
               start=True, stop=True, waits=wbc)
            e_bc2 = mm(psA[:, 512:1024][:, 0:w], ones_f[:, :], row_mr[0:1, 0:w],
                       start=True, stop=True, inc="sT")
            for ch in range(NCH):
                vop(lambda v, c=ch, n=nt, ww=w: v.tensor_tensor(
                    out=scr_sl(0)[:, 0:ww],
                    in0=x1f[:, c * TOK + n * TWD: c * TOK + n * TWD + ww],
                    in1=psA[:, 0:ww], op=ALU.mult),
                    rd=(f"x1_{ch}",), wr=("scr0",),
                    waits=[e_bc2] if ch == 0 else ())
                vop(lambda v, ww=w: v.tensor_tensor(
                    out=scr_sl(0)[:, 0:ww], in0=scr_sl(0)[:, 0:ww],
                    in1=psA[:, 512:512 + ww], op=ALU.add),
                    rd=("scr0",), wr=("scr0",))
                e_h2[(ch, nt)] = vop(lambda v, c=ch, n=nt, ww=w: v.tensor_scalar(
                    out=va[:, c * TOK + n * TWD: c * TOK + n * TWD + ww],
                    in0=scr_sl(0)[:, 0:ww],
                    scalar1=m_cb1[:, c:c + 1], scalar2=m_sb[:, c:c + 1],
                    op0=ALU.mult, op1=ALU.add),
                    rd=("scr0", "m_cb1", "m_sb"), wr=(f"h2_{ch}_{nt}",))
                e_h2_last = e_h2[(ch, nt)]
        e_h2tap = None
        if taps:
            e_h2tap = gdma(h2tap[:, :], va[:, 0:TOK], 'sTP2', waits=[e_h2_last])

        e_gate_vec = None
        for s8 in range(NTSB):
            e_gmm = None
            for ch in range(NCH):
                wts = []
                if ch == 0:
                    wts = [e_wgate, e_h2[(ch, ((s8 + 1) * 128 - 1) // TWD)]]
                    if e_gate_vec is not None:
                        wts.append(e_gate_vec)
                    else:
                        wts.append(prev_av_evt[0])
                else:
                    wts = [e_h2[(ch, ((s8 + 1) * 128 - 1) // TWD)]]
                e_gmm = mm(psC[:, 0:8],
                           va[:, ch * TOK + s8 * 128: ch * TOK + (s8 + 1) * 128],
                           wA[:, 10240 + ch * 8: 10240 + (ch + 1) * 8],
                           start=(ch == 0), stop=(ch == NCH - 1),
                           waits=wts, inc="sT" if ch == NCH - 1 else None)
            vop(lambda v: v.reduce_max(out=g_m1[:, :], in_=psC[:, 0:8], axis=AX.X),
                wr=("g_m1",), waits=[e_gmm])
            vop(lambda v: v.tensor_scalar(out=g_d[:, :], in0=psC[:, 0:8],
                                          scalar1=g_m1[:, 0:1], scalar2=None,
                                          op0=ALU.subtract),
                rd=("g_m1",), wr=("g_d",))
            vop(lambda v: v.tensor_scalar(out=g_pen[:, :], in0=g_d[:, :],
                                          scalar1=0.0, scalar2=-1e30,
                                          op0=ALU.is_equal, op1=ALU.mult),
                rd=("g_d",), wr=("g_pen",))
            vop(lambda v: v.tensor_tensor(out=g_pen[:, :], in0=g_pen[:, :],
                                          in1=g_d[:, :], op=ALU.add),
                rd=("g_pen", "g_d"), wr=("g_pen",))
            e_gm2 = vop(lambda v: v.reduce_max(out=g_m2[:, :], in_=g_pen[:, :],
                                               axis=AX.X),
                        rd=("g_pen",), wr=("g_m2",))
            e_gexp = sop(lambda s: s.activation(out=g_s2[:, :], in_=g_m2[:, :],
                                                func=AF.Exp),
                         rd=("g_m2",), wr=("g_s2",), waits=[e_gm2])
            vop(lambda v: v.tensor_scalar(out=g_inv[:, :], in0=g_s2[:, :],
                                          scalar1=1.0, scalar2=None, op0=ALU.add),
                rd=("g_s2",), wr=("g_inv",), waits=[e_gexp])
            vop(lambda v: v.reciprocal(out=g_inv[:, :], in_=g_inv[:, :]),
                rd=("g_inv",), wr=("g_inv",))
            vop(lambda v: v.tensor_tensor(out=g_m2[:, :], in0=g_s2[:, :],
                                          in1=g_inv[:, :], op=ALU.mult),
                rd=("g_s2", "g_inv"), wr=("g_m2",))
            e_gate_vec = vop(lambda v, s8_=s8: v.tensor_tensor(
                out=g_cols[:, s8_:s8_ + 1], in0=g_inv[:, :], in1=g_m2[:, :],
                op=ALU.add),
                rd=("g_inv", "g_m2"), wr=("g_cols",))
        e_gb1 = gdma(g_bnc.ap().rearrange("(s p) -> p s", p=128),
                     g_cols[:, 0:NTSB], "sGB", waits=[e_gate_vec], slow=True)
        e_gb2 = gdma(g_row[0:1, :], g_bnc.ap(), 'sGB', waits=[e_gb1])
        if taps:
            gdma(gtap[:, :], g_row[0:1, :], 'sTP2', waits=[e_gb2])

        up0 = NCH
        e_relu = {}
        ri = 0
        rot_free2 = [e_bc2, e_bc2, None, None]
        last_h2_read = e_h2_last
        for m in range(NCH):
            si2 = up0 + m
            for nt in range(NT_D):
                w = NTW[nt]
                pp = rot_ps[ri % 4]
                e_p = None
                for ch in range(NCH):
                    wts = [e_h2[(ch, nt)]]
                    if ch == 0:
                        wts.append(e_wset[si2])
                        if rot_free2[ri % 4] is not None:
                            wts.append(rot_free2[ri % 4])
                        if ri % 4 < 2:
                            wts.append(last_h2_read)
                    e_p = mm(pp[:, 0:w], wslot(set_slot[si2], ch),
                             va[:, ch * TOK + nt * TWD: ch * TOK + nt * TWD + w],
                             start=(ch == 0), stop=(ch == NCH - 1),
                             waits=wts, inc="sT" if ch == NCH - 1 else None)
                e_relu[(m, nt)] = sop(lambda s, mm_=m, n=nt, ppp=pp, ww=w: s.activation(
                    out=big[:, mm_ * TOK + n * TWD: mm_ * TOK + n * TWD + ww],
                    in_=ppp[:, 0:ww], func=AF.Relu),
                    wr=(f"relu{m}_{nt}",), waits=[e_p, e_proj_done])
                rot_free2[ri % 4] = e_relu[(m, nt)]
                ri += 1
            slot_free[set_slot[si2]] = e_relu[(m, NT_D - 1)]
            if n_wdma < len(wsets):
                emit_wset_dma(n_wdma)
                n_wdma += 1
        e_up_done = p.last("sT")

        e_h2g = {}
        e_h2g_last = None
        for nt in range(NT_D):
            w = NTW[nt]
            wgbd = [e_gb2, e_up_done, prev_rowend2]
            if e_h2g_last is not None:
                wgbd.append(e_h2g_last)
            if e_h2tap is not None:
                wgbd.append(e_h2tap)
            e_gbd = mm(psE[:, 0:w], ones_f[:, :], g_row[0:1, nt * TWD: nt * TWD + w],
                       start=True, stop=True, waits=wgbd, inc="sT")
            for ch in range(NCH):
                e_h2g[(ch, nt)] = vop(lambda v, c=ch, n=nt, ww=w: v.tensor_tensor(
                    out=va[:, c * TOK + n * TWD: c * TOK + n * TWD + ww],
                    in0=va[:, c * TOK + n * TWD: c * TOK + n * TWD + ww],
                    in1=psE[:, 0:ww], op=ALU.mult),
                    rd=(f"h2_{ch}_{nt}",), wr=(f"h2_{ch}_{nt}",),
                    waits=[e_gbd] if ch == 0 else ())
                e_h2g_last = e_h2g[(ch, nt)]

        dn0 = up0 + NCH
        ri = 0
        rot_free3 = [rot_free2[0], rot_free2[1], rot_free2[2], rot_free2[3]]
        e_st = {}
        for m in range(NCH):
            si_d = dn0 + 2 * m
            si_e = dn0 + 2 * m + 1
            for nt in range(NT_D):
                w = NTW[nt]
                pp = rot_ps[ri % 4]
                for r in range(NCH):
                    wts = [e_relu[(r, nt)]]
                    if r == 0:
                        wts.append(e_wset[si_d])
                        if rot_free3[ri % 4] is not None:
                            wts.append(rot_free3[ri % 4])
                    mm(pp[:, 0:w], wslot(set_slot[si_d], r),
                       big[:, r * TOK + nt * TWD: r * TOK + nt * TWD + w],
                       start=(r == 0), stop=False, waits=wts)
                e_p = None
                for ch in range(NCH):
                    wts = [e_h2g[(ch, nt)]]
                    if ch == 0:
                        wts.append(e_wset[si_e])
                    e_p = mm(pp[:, 0:w], wslot(set_slot[si_e], ch),
                             va[:, ch * TOK + nt * TWD: ch * TOK + nt * TWD + w],
                             start=False, stop=(ch == NCH - 1),
                             waits=wts, inc="sT" if ch == NCH - 1 else None)
                bsl = (m * NT_D + nt) % 2
                wts = [e_p]
                if (m * NT_D + nt) >= 2:
                    wts.append(e_st[(m * NT_D + nt) - 2])
                e_ob = vop(lambda v, mm_=m, n=nt, ppp=pp, ww=w, bs=bsl:
                           v.scalar_tensor_tensor(
                               out=ob[:, bs * 1024: bs * 1024 + ww],
                               in0=ppp[:, 0:ww], scalar=m_gb[:, mm_:mm_ + 1],
                               in1=x1f[:, mm_ * TOK + n * TWD:
                                       mm_ * TOK + n * TWD + ww],
                               op0=ALU.mult, op1=ALU.add),
                           rd=("m_gb", f"x1_{m}"), wr=(f"ob{bsl}",), waits=wts)
                e_st[m * NT_D + nt] = ydma(out[m, :, nt * TWD: nt * TWD + w],
                                           ob[:, bsl * 1024: bsl * 1024 + w],
                                           "sO0" if bsl == 0 else "sO1",
                                           waits=[e_ob])
                rot_free3[ri % 4] = e_ob
                ri += 1
            slot_free[set_slot[si_d]] = e_st[m * NT_D + NT_D - 1]
            slot_free[set_slot[si_e]] = e_st[m * NT_D + NT_D - 1]
            for _ in range(2):
                if n_wdma < len(wsets):
                    emit_wset_dma(n_wdma)
                    n_wdma += 1

        for di in range(pad_pe):
            mm(psF[0:1, 0:512], ones_b[:, :], big[:, 0:512],
               start=True, stop=True,
               waits=[p.last("sO0"), p.last("sO1")] if di == 0 else ())
        p.emit("sync", lambda s: s.nop(),
               waits=[p.last("sO0"), p.last("sO1")])
        p.emit("gpsimd", lambda g: g.nop(),
               waits=[p.last("sO0"), p.last("sO1")])

        @block.tensor
        def _(t):
            p.replay("tensor", t, sems)

        @block.scalar
        def _(s):
            p.replay("scalar", s, sems)

        @block.vector
        def _(v):
            p.replay("vector", v, sems)

        @block.gpsimd
        def _(g):
            p.replay("gpsimd", g, sems)

        @block.sync
        def _(s):
            p.replay("sync", s, sems)

    return nc


def make_inputs(seq, x, w_qkv, w_proj, W_up, W_down, W_gate, W_expert, c, wmod_full):
    TOK = seq // N_CORES
    MSL = 6 * D // N_CORES
    xT = np.ascontiguousarray(x.reshape(seq, D).T)
    wq3 = w_qkv.reshape(D, 3, 16, HD)
    theta = np.float32(10000.0)
    freqs = np.float32(1.0) / theta ** (np.arange(0, HD, 2, dtype=np.float32)
                                        / np.float32(HD))
    t = np.arange(seq, dtype=np.float32)
    f = np.outer(t, freqs)
    sin = np.repeat(np.sin(f), 2, axis=1)
    cos = np.repeat(np.cos(f), 2, axis=1)
    cosT = np.ascontiguousarray(cos.T).astype(ml_dtypes.bfloat16)
    sinT = np.ascontiguousarray(sin.T).astype(ml_dtypes.bfloat16)
    pT = np.zeros((128, 128), dtype=ml_dtypes.bfloat16)
    for i in range(64):
        pT[2 * i + 1, 2 * i] = -1.0
        pT[2 * i, 2 * i + 1] = 1.0
    c_r = np.ascontiguousarray(c.reshape(16, 128).T)
    wupT = np.ascontiguousarray(W_up.T)
    wdownT = np.ascontiguousarray(W_down.T)
    wgT = np.ascontiguousarray(W_gate.T)
    in_maps = []
    for j in range(N_CORES):
        hsel = [2 * j, 2 * j + 1]
        wqkv_j = np.ascontiguousarray(np.concatenate(
            [wq3[:, 0, hsel, :].reshape(D, 2 * HD),
             wq3[:, 1, hsel, :].reshape(D, 2 * HD),
             wq3[:, 2, hsel, :].reshape(D, 2 * HD)], axis=1))
        wmod_j = np.ascontiguousarray(wmod_full[:, j * MSL:(j + 1) * MSL])
        cidx = np.zeros((128, NCH), dtype=np.uint32)
        for ch in range(NCH):
            cidx[:, ch] = (128 * ch + np.arange(128)) * N_CORES + j
        in_maps.append({
            "xT": xT, "wqkv": wqkv_j, "wproj": w_proj, "wupT": wupT,
            "wdownT": wdownT, "wexp": W_expert, "wgT": wgT, "wmod": wmod_j,
            "c_r": c_r, "cosT": cosT, "sinT": sinT, "pT": pT, "cidx_in": cidx,
        })
    return in_maps


def kernel(L0_attn_x, L0_attn_w_qkv, L0_attn_w_proj, L0_ffn_W_up, L0_ffn_W_down,
           L0_ffn_W_gate, L0_ffn_W_expert, L0_c, L0_wmod):
    seq = 8192
    TOK = seq // N_CORES
    x = np.asarray(L0_attn_x, dtype=np.float32)
    in_maps = make_inputs(seq, x,
                          np.asarray(L0_attn_w_qkv, dtype=np.float32),
                          np.asarray(L0_attn_w_proj, dtype=np.float32),
                          np.asarray(L0_ffn_W_up, dtype=np.float32),
                          np.asarray(L0_ffn_W_down, dtype=np.float32),
                          np.asarray(L0_ffn_W_gate, dtype=np.float32),
                          np.asarray(L0_ffn_W_expert, dtype=np.float32),
                          np.asarray(L0_c, dtype=np.float32),
                          np.asarray(L0_wmod, dtype=np.float32))
    nc = build_program(seq)
    res = run_bass_kernel_spmd(nc, in_maps, core_ids=list(range(N_CORES)))
    pieces = []
    for j in range(N_CORES):
        o = np.asarray(res.results[j]["out"])
        pieces.append(o.reshape(D, TOK).T)
    full = np.concatenate(pieces, axis=0)
    return np.ascontiguousarray(full.reshape(1, seq, D)).astype(np.float32)
